# version 05
# baseline (speedup 1.0000x reference)
import math

import numpy as np

import concourse.bass as bass
import concourse.mybir as mybir
from concourse.bass_utils import run_bass_kernel_spmd

F32 = mybir.dt.float32
BF16 = mybir.dt.bfloat16
FP8 = mybir.dt.float8e4
AF = mybir.ActivationFunctionType
ALU = mybir.AluOpType
AX = mybir.AxisListType

D = 2048
KC = 16
T = 1024
NCORES = 8
L0_HEADS = 16
L0_SCALE = float(np.sqrt(128.0))
L1_NH, L1_HD = 32, 64
DS = 2816
DE = 1408
DI = 6144
MC_DS = DS // 128
MC_DE = DE // 128
MC_DI = DI // 128

STOP_AFTER = None



def _mstripes(w_io):
    K, M = w_io.shape
    kc, mc = K // 128, M // 128
    t = w_io.reshape(kc, 128, mc, 128).transpose(2, 1, 0, 3).reshape(mc, 128, kc * 128)
    return np.ascontiguousarray(t)


def _rowstripes(w_io):
    K, M = w_io.shape
    return np.ascontiguousarray(w_io.reshape(K // 128, 128, M))


def _rope_tables(positions):
    import jax
    import jax.numpy as jnp
    with jax.default_device(jax.devices("cpu")[0]):
        inv_freq = 1.0 / 10000 ** (jnp.arange(0, L1_HD, 2, dtype=jnp.float32) / L1_HD)
        pos = jnp.asarray(positions, dtype=jnp.float32)
        freqs = pos[:, None] * inv_freq[None, :]
        cos = np.asarray(jnp.cos(freqs)).T
        sin = np.asarray(jnp.sin(freqs)).T
    cos64 = np.concatenate([cos, cos], axis=0)
    sin64 = np.concatenate([-sin, sin], axis=0)
    cos128 = np.concatenate([cos64, cos64], axis=0)
    sin128 = np.concatenate([sin64, sin64], axis=0)
    return np.ascontiguousarray(cos128), np.ascontiguousarray(sin128)


def _consts():
    ident = np.eye(128, dtype=np.float32)
    perm = np.zeros((128, 128), dtype=np.float32)
    for m in range(128):
        p = (m // 64) * 64 + ((m % 64) + 32) % 64
        perm[p, m] = 1.0
    tri = np.triu(np.ones((128, 128), dtype=np.float32)).astype(np.float32)
    ones = np.ones((128, 128), dtype=np.float32)
    ind8 = np.zeros((8, 8 * 128), dtype=np.float32)
    for e in range(8):
        ind8[e, e * 128:(e + 1) * 128] = 1.0
    return ident, perm, tri, ones, ind8


def prepare_inputs(inputs, ncores=NCORES):
    f = {k: np.asarray(v) for k, v in inputs.items()}
    shared = {}
    shared["wq_s"] = _mstripes(np.ascontiguousarray(f["L0_attn_wq"].T))
    shared["wu_s"] = _mstripes(np.ascontiguousarray(f["L0_attn_wu"].T))
    shared["wo_s"] = _mstripes(np.ascontiguousarray(f["L0_attn_wo"].T))
    shared["wk_r"] = _rowstripes(np.ascontiguousarray(f["L0_attn_wk"].T))
    shared["wv_r"] = _rowstripes(np.ascontiguousarray(f["L0_attn_wv"].T))
    shared["wrtr"] = np.ascontiguousarray(
        f["L0_ffn_W_router"].reshape(KC, 128, 8).transpose(1, 0, 2).reshape(128, KC * 8))
    shared["wsg_s"] = _mstripes(f["L0_ffn_Ws_gate"])
    shared["wsu_s"] = _mstripes(f["L0_ffn_Ws_up"])
    shared["wsd_s"] = _mstripes(f["L0_ffn_Ws_down"])
    shared["weg_s"] = np.ascontiguousarray(
        np.stack([_mstripes(f["L0_ffn_We_gate"][e]) for e in range(8)]))
    shared["weu_s"] = np.ascontiguousarray(
        np.stack([_mstripes(f["L0_ffn_We_up"][e]) for e in range(8)]))
    shared["wed_s"] = np.ascontiguousarray(
        np.stack([_mstripes(f["L0_ffn_We_down"][e]) for e in range(8)]))
    wqkv = f["L1_attn_w_qkv"]
    shared["l1q_s"] = _mstripes(np.ascontiguousarray(wqkv[:, 0:2048]))
    shared["l1k_s"] = np.ascontiguousarray(
        wqkv[:, 2048:2112].reshape(KC, 128, 64).transpose(1, 0, 2).reshape(128, KC * 64))
    shared["l1v_r"] = _rowstripes(np.ascontiguousarray(wqkv[:, 2112:2176]))
    shared["l1d_s"] = _mstripes(f["L1_attn_w_dense"])
    shared["l1g_s"] = _mstripes(f["L1_ffn_gate_w"])
    shared["l1u_s"] = _mstripes(f["L1_ffn_up_w"])
    shared["l1dn_s"] = _mstripes(f["L1_ffn_down_w"])
    shared["l1gsc"] = np.ascontiguousarray(f["L1_ffn_gate_scale"].reshape(MC_DI, 128).T)
    shared["l1usc"] = np.ascontiguousarray(f["L1_ffn_up_scale"].reshape(MC_DI, 128).T)
    shared["l1dsc"] = np.ascontiguousarray(f["L1_ffn_down_scale"].reshape(KC, 128).T)
    ident, perm, tri, ones, ind8 = _consts()
    cb = np.zeros((128, 4), dtype=np.float32)
    cb[:, 0] = 0.5 * math.log(128.0)
    cb[:, 1] = 1e-5
    shared["c_bias"] = cb
    shared["c_ident"] = ident
    shared["c_perm"] = perm
    shared["c_tri"] = tri
    shared["c_ones"] = ones
    shared["c_ind8"] = ind8

    x = f["L0_attn_x"]
    in_maps = []
    for c in range(ncores):
        b, par = c // 2, c % 2
        blocks = [par + 2 * j for j in range(8)]
        tok = np.concatenate([np.arange(g * 128, (g + 1) * 128) for g in blocks])
        m = dict(shared)
        m["x"] = np.ascontiguousarray(x[b, tok, :].T)
        cosq, sinq = _rope_tables(tok)
        m["rope_cos"] = cosq
        m["rope_sin"] = sinq
        dm = np.zeros((128, KC * 128), dtype=np.float32)
        tri = np.triu(np.ones((128, 128), dtype=np.float32))
        for g in range(KC):
            gb = par + 2 * (g // 2)
            if gb == g:
                dm[:, g * 128:(g + 1) * 128] = tri
            elif gb > g:
                dm[:, g * 128:(g + 1) * 128] = 1.0
        m["c_dmask"] = dm
        in_maps.append(m)
    return in_maps


def gather_output(results, ncores=NCORES):
    out = np.empty((4, 2048, 2048), dtype=np.float32)
    for c in range(ncores):
        b, par = c // 2, c % 2
        blocks = [par + 2 * j for j in range(8)]
        tok = np.concatenate([np.arange(g * 128, (g + 1) * 128) for g in blocks])
        out[b, tok, :] = results[c]["y"].T
    return out



ENGS = ("gp", "pe", "dve", "act", "sp")


class Plan:
    def __init__(self, nc):
        self.nc = nc
        self.q = {e: [] for e in ENGS}
        self.cnt = {}
        self.sems = {}

    def sem(self, name):
        if name not in self.sems:
            self.sems[name] = self.nc.alloc_semaphore(name=name)
            self.cnt[name] = 0
        return name

    def wait(self, eng, s, v):
        if v > 0:
            self.q[eng].append(("w", s, int(v)))

    def op(self, eng, fn, *args, inc=(), **kw):
        for s, k in inc:
            self.cnt[s] += k
        self.q[eng].append(("o", fn, args, kw, tuple(inc)))

    def n(self, s):
        return self.cnt.get(s, 0)

    def emit(self, eng, engine):
        for item in self.q[eng]:
            if item[0] == "w":
                engine.wait_ge(self.sems[item[1]], item[2])
            else:
                _, fn, args, kw, inc = item
                inst = getattr(engine, fn)(*args, **kw)
                for s, k in inc:
                    inst = inst.then_inc(self.sems[s], k)


class PsumPool:

    def __init__(self, plan, tensors):
        self.p = plan
        self.t = tensors
        self.free_at = [None] * len(tensors)

    def acquire(self, eng, u):
        fa = self.free_at[u]
        if fa is not None:
            self.p.wait(eng, fa[0], fa[1])
        return self.t[u]

    def release(self, u, sem, val):
        self.free_at[u] = (sem, val)


class WStream:

    NB = 3
    LA = 2

    def __init__(self, plan, bufs):
        self.p = plan
        self.bufs = bufs
        self.sems = [plan.sem(f"sW{j}") for j in range(self.NB)]
        self.stripes = []
        self.loaded = 0
        self.consumed_at = {}
        self.f8_done = {}
        self.nexti = 0

    def register(self, src_ap, ncols, fp8=None):
        self.stripes.append((src_ap, ncols, fp8))
        return len(self.stripes) - 1

    def _plan_load(self):
        k = self.loaded
        if k >= len(self.stripes):
            return
        src, ncols, fp8 = self.stripes[k]
        b = k % self.NB
        if fp8 is not None:
            f8bufs, f8sems, f8idx = fp8
            j = f8idx % 2
            prev = self.f8_done.get(f8idx - 2)
            if prev is not None:
                self.p.wait("gp", prev[0], prev[1])
            self.p.op("gp", "dma_start", out=f8bufs[j][:, 0:ncols], in_=src,
                      inc=[(f8sems[j], 16)])
            self.p.wait("gp", f8sems[j], self.p.n(f8sems[j]))
            if k >= self.NB:
                dep = self.consumed_at.get(k - self.NB)
                assert dep is not None
                self.p.wait("gp", dep[0], dep[1])
            self.p.op("gp", "dma_start", out=self.bufs[b][:, 0:ncols],
                      in_=f8bufs[j][:, 0:ncols], inc=[(self.sems[b], 16)])
            self.f8_done[f8idx] = (self.sems[b], 16 * (k // self.NB + 1))
        else:
            if k >= self.NB:
                dep = self.consumed_at.get(k - self.NB)
                assert dep is not None, f"stripe {k} loads before {k - self.NB} consumed"
                self.p.wait("gp", dep[0], dep[1])
            self.p.op("gp", "dma_start", out=self.bufs[b][:, 0:ncols], in_=src,
                      inc=[(self.sems[b], 16)])
        self.loaded += 1

    def consume(self, k):
        while self.loaded <= min(k + self.LA, len(self.stripes) - 1):
            self._plan_load()
        b = k % self.NB
        return self.bufs[b], self.sems[b], 16 * (k // self.NB + 1)

    def done(self, k, sem, val):
        self.consumed_at[k] = (sem, val)



def build_nc(ncores=NCORES, stop_after=STOP_AFTER):
    nc = bass.Bass("TRN2", target_bir_lowering=False)

    dx = nc.dram_tensor("x", [D, T], F32, kind="ExternalInput")
    dy = nc.dram_tensor("y", [D, T], F32, kind="ExternalOutput")
    din = {}
    def di(name, shape):
        din[name] = nc.dram_tensor(name, shape, F32, kind="ExternalInput")
        return din[name]
    di("wq_s", [KC, 128, KC * 128]); di("wu_s", [KC, 128, KC * 128])
    di("wo_s", [KC, 128, KC * 128])
    di("wk_r", [KC, 128, D]); di("wv_r", [KC, 128, D])
    di("wrtr", [128, KC * 8])
    di("wsg_s", [MC_DS, 128, KC * 128]); di("wsu_s", [MC_DS, 128, KC * 128])
    di("wsd_s", [KC, 128, MC_DS * 128])
    di("weg_s", [8, MC_DE, 128, KC * 128]); di("weu_s", [8, MC_DE, 128, KC * 128])
    di("wed_s", [8, KC, 128, MC_DE * 128])
    di("l1q_s", [KC, 128, KC * 128]); di("l1k_s", [128, KC * 64])
    di("l1v_r", [KC, 128, 64]); di("l1d_s", [KC, 128, KC * 128])
    di("l1g_s", [MC_DI, 128, KC * 128]); di("l1u_s", [MC_DI, 128, KC * 128])
    di("l1dn_s", [KC, 128, MC_DI * 128])
    di("l1gsc", [128, MC_DI]); di("l1usc", [128, MC_DI]); di("l1dsc", [128, KC])
    di("c_bias", [128, 4])
    di("c_ident", [128, 128]); di("c_perm", [128, 128]); di("c_tri", [128, 128])
    di("c_ones", [128, 128]); di("c_ind8", [8, 8 * 128])
    di("rope_cos", [128, T]); di("rope_sin", [128, T])
    di("c_dmask", [128, KC * 128])

    kv_part = nc.dram_tensor("kv_part", [4, 128, 512], F32)
    kv_sum = nc.dram_tensor("kv_sum", [4, 128, 512], F32)
    l1kv_loc = nc.dram_tensor("l1kv_loc", [2, 64, T], BF16)
    l1kv_all = nc.dram_tensor("l1kv_all", [2, 2, 64, T], BF16)

    a = nc.alloc_sbuf_tensor
    RALL = a("RALL", [128, KC * T], F32)
    XBALL = a("XBALL", [128, KC * T], BF16)
    AALL = a("AALL", [128, KC * T], BF16)
    BALL = a("BALL", [128, KC * T], BF16)
    R = [RALL[:, i * T:(i + 1) * T] for i in range(KC)]
    XB = [XBALL[:, i * T:(i + 1) * T] for i in range(KC)]
    A = [AALL[:, i * T:(i + 1) * T] for i in range(KC)]
    B = [BALL[:, i * T:(i + 1) * T] for i in range(KC)]
    WB = [a(f"WB{j}", [128, 2048], BF16) for j in range(WStream.NB)]
    F8 = [a(f"F8{j}", [128, 2048], FP8) for j in range(2)]
    S32 = [a(f"S32{j}", [128, T], F32) for j in range(2)]
    ROWS = a("ROWS", [128, T], F32)
    ROWSB = a("ROWSB", [128, T], BF16)
    cos_t = a("cos_t", [128, T], F32)
    sin_t = a("sin_t", [128, T], F32)
    ident = a("ident", [128, 128], F32)
    perm = a("perm", [128, 128], F32)
    tri = a("tri", [128, 128], BF16)
    onesf = a("onesf", [128, 128], F32)
    onesb = a("onesb", [128, 128], BF16)
    CB = a("CB", [128, 4], F32)
    ind8 = a("ind8", [8, 8 * 128], BF16)
    rtr = a("rtr", [128, KC * 8], BF16)
    dmask = a("dmask", [128, KC * 128], BF16)
    SMX = a("SMX", [128, 48], F32)
    l1sc = a("l1sc", [128, 2 * MC_DI + KC], F32)
    psum = [nc.alloc_psum_tensor(f"PS{u}", [128, 1024], F32) for u in range(4)]

    P = Plan(nc)
    PP = PsumPool(P, psum)
    LN_XB = {"v": None}
    for s in ("sX", "sXG", "sPE", "sDVE", "sACT", "sGP", "sGPC", "sCC",
              "sOUT", "sSP", "sH", "sF80", "sF81"):
        P.sem(s)
    WS = WStream(P, WB)

    xsrc = bass.AP(dx, 0, [[T, 128], [128 * T, KC], [1, T]])
    P.op("sp", "dma_start", out=RALL[:, :], in_=xsrc, inc=[("sX", 16)])
    P.op("gp", "dma_start", out=XBALL[:, :], in_=xsrc, inc=[("sXG", 16)])
    for dst, src in ((cos_t, din["rope_cos"]), (sin_t, din["rope_sin"]),
                     (ident, din["c_ident"]), (perm, din["c_perm"]),
                     (onesf, din["c_ones"]), (CB, din["c_bias"])):
        P.op("sp", "dma_start", out=dst[:, :], in_=src[:, :], inc=[("sX", 16)])
    for dst, src in ((tri, din["c_tri"]), (onesb, din["c_ones"][:, :]),
                     (ind8, din["c_ind8"]), (rtr, din["wrtr"]),
                     (dmask, din["c_dmask"])):
        P.op("gp", "dma_start", out=dst[:, :], in_=src[:, :], inc=[("sXG", 16)])
    X_READY = P.n("sX")
    XG_READY = P.n("sXG")

    for e in ("pe", "dve", "act"):
        P.wait(e, "sX", X_READY)
        P.wait(e, "sXG", XG_READY)

    state = {"u": 0, "avoid": set()}

    def next_unit():
        while True:
            u = state["u"]
            state["u"] = (u + 1) % 4
            if u not in state["avoid"]:
                return u

    def proj(src_chunks, stripe_ids, kcw, evac, kr=0):
        for m, sid in enumerate(stripe_ids):
            buf, wsem, wval = WS.consume(sid)
            u = next_unit()
            pt = PP.acquire("pe", u)
            P.wait("pe", wsem, wval)
            for k in range(kcw):
                lhs = buf[:, k * 128:(k + 1) * 128]
                for h in range(2):
                    n0 = h * 512
                    P.op("pe", "matmul", pt[:, n0:n0 + 512], lhs,
                         src_chunks[kr + k][:, n0:n0 + 512],
                         start=(k == 0), stop=(k == kcw - 1),
                         inc=[("sPE", 1)] if (k == kcw - 1 and h == 1) else ())
            pv = P.n("sPE")
            WS.done(sid, "sPE", pv)
            sem, val = evac(m, pt, pv)
            PP.release(u, sem, val)

    def l0_attn():
        kv_sids = [[[WS.register(
                        bass.AP(w, 4 * k4 * 128 * 2048 + hg * 512,
                                [[2048, 128], [128 * 2048, 4], [1, 512]]), 2048)
                     for k4 in range(KC // 4)]
                    for w in (din["wk_r"], din["wv_r"])] for hg in range(4)]
        q_sids = [WS.register(din["wq_s"][m, :, :], 2048) for m in range(KC)]
        u_sids = [WS.register(din["wu_s"][m, :, :], 2048) for m in range(KC)]
        o_sids = [WS.register(din["wo_s"][m, :, :], 2048) for m in range(KC)]

        gp_kv_dma = 0
        for hg in range(4):
            CK = [B[t // 2][:, (t % 2) * 512:(t % 2 + 1) * 512] for t in range(8)]
            CV = [B[4 + t // 2][:, (t % 2) * 512:(t % 2 + 1) * 512] for t in range(8)]
            for which, CC, actf in ((0, CK, AF.Relu), (1, CV, AF.Copy)):
                units = [next_unit() for _ in range(4)]
                for uu in units:
                    PP.acquire("pe", uu)
                for k4 in range(KC // 4):
                    sid = kv_sids[hg][which][k4]
                    buf, wsem, wval = WS.consume(sid)
                    P.wait("pe", wsem, wval)
                    for kk in range(4):
                        k = 4 * k4 + kk
                        for t in range(8):
                            pt = psum[units[t % 4]]
                            half = (t // 4) * 512
                            P.op("pe", "matmul", pt[:, half:half + 512],
                                 XB[k][:, t * 128:(t + 1) * 128],
                                 buf[:, kk * 512:kk * 512 + 512],
                                 start=(k == 0), stop=(k == KC - 1),
                                 inc=[("sPE", 1)] if t == 7 else ())
                    WS.done(sid, "sPE", P.n("sPE"))
                pv = P.n("sPE")
                P.wait("act", "sPE", pv)
                for t in range(8):
                    pt = psum[units[t % 4]]
                    half = (t // 4) * 512
                    P.op("act", "activation", CC[t], pt[:, half:half + 512], actf,
                         scale=(1.0 / L0_SCALE) if which == 0 else 1.0,
                         inc=[("sACT", 1)])
                av = P.n("sACT")
                for uu in units:
                    PP.release(uu, "sACT", av)
            u = next_unit()
            pt = PP.acquire("pe", u)
            P.wait("pe", "sACT", P.n("sACT"))
            for hh in range(4):
                hc = slice(hh * 128, (hh + 1) * 128)
                for t in range(8):
                    P.op("pe", "matmul", pt[:, hc], CK[t][:, hc], CV[t][:, hc],
                         start=(t == 0), stop=(t == 7),
                         inc=[("sPE", 1)] if (hh == 3 and t == 7) else ())
            pv = P.n("sPE")
            slot = S32[hg % 2]
            P.wait("act", "sPE", pv)
            if hg >= 2:
                P.wait("act", "sH", 16 * hg)
            P.op("act", "activation", slot[:, 0:512], pt[:, 0:512], AF.Copy,
                 inc=[("sACT", 1)])
            av = P.n("sACT")
            PP.release(u, "sACT", av)
            P.wait("sp", "sACT", av)
            P.op("sp", "dma_start", out=kv_part[hg, :, :], in_=slot[:, 0:512],
                 inc=[("sH", 16)])
            gp_kv_dma = P.n("sH")
        groups = [[2 * i, 2 * i + 1] for i in range(max(1, ncores // 2))]
        P.wait("gp", "sH", gp_kv_dma)
        P.op("gp", "collective_compute", "AllReduce", ALU.add,
             replica_groups=groups, ins=[kv_part.ap().opt()],
             outs=[kv_sum.ap().opt()], inc=[("sCC", 1)])
        CC_DONE = P.n("sCC")

        def evq(m, pt, pv):
            P.wait("act", "sPE", pv)
            P.op("act", "activation", A[m][:, 0:512], pt[:, 0:512], AF.Relu,
                 scale=1.0 / L0_SCALE)
            P.op("act", "activation", A[m][:, 512:1024], pt[:, 512:1024], AF.Relu,
                 scale=1.0 / L0_SCALE, inc=[("sACT", 1)])
            return ("sACT", P.n("sACT"))

        def evu(m, pt, pv):
            P.wait("act", "sPE", pv)
            P.op("act", "activation", B[m][:, 0:512], pt[:, 0:512], AF.Silu)
            P.op("act", "activation", B[m][:, 512:1024], pt[:, 512:1024], AF.Silu,
                 inc=[("sACT", 1)])
            return ("sACT", P.n("sACT"))

        proj(XB, q_sids, KC, evq)
        proj(XB, u_sids, KC, evu)
        ACT_QU = P.n("sACT")
        PE_PROJ = P.n("sPE")

        P.wait("gp", "sCC", CC_DONE)
        P.wait("gp", "sPE", PE_PROJ)
        for hg in range(4):
            P.op("gp", "dma_start",
                 out=XB[hg // 2][:, (hg % 2) * 512:(hg % 2 + 1) * 512],
                 in_=kv_sum[hg, :, :], inc=[("sGP", 16)])
        KVRDY = P.n("sGP")

        P.wait("pe", "sGP", KVRDY)
        P.wait("pe", "sACT", ACT_QU)
        P.wait("dve", "sACT", ACT_QU)
        P.wait("act", "sDVE", P.n("sDVE"))
        UO = [0, 1]
        UCS, UBC = 2, 3
        st_ = {}

        def srms_o(h):
            kvt = XB[h // 8][:, (h % 8) * 128:(h % 8 + 1) * 128]
            u = UO[h % 2]
            pt = PP.acquire("pe", u)
            P.op("pe", "matmul", pt[:, 0:512], kvt, A[h][:, 0:512],
                 start=True, stop=True)
            P.op("pe", "matmul", pt[:, 512:1024], kvt, A[h][:, 512:1024],
                 start=True, stop=True, inc=[("sPE", 1)])
            st_[("opv", h)] = P.n("sPE")

        def srms_stats(h):
            u = UO[h % 2]
            pt = psum[u]
            P.wait("act", "sPE", st_[("opv", h)])
            P.wait("act", "sDVE", st_.get(("tt1", h - 2), 0))
            P.wait("act", "sPE", st_.get(("cs", h - 2), 0))
            P.op("act", "activation", S32[h % 2][:, :], pt[:, :], AF.Copy,
                 inc=[("sACT", 1)])
            P.op("act", "activation", XB[3 + h % 2][:, :], pt[:, :], AF.Square,
                 inc=[("sACT", 1)])
            av = P.n("sACT")
            st_[("ev", h)] = av
            PP.release(u, "sACT", av)
            ptc = PP.acquire("pe", UCS)
            P.wait("pe", "sACT", av)
            lane = 32 + (h % 2) * 32
            P.op("pe", "matmul", ptc[lane:lane + 1, 0:512], onesb[:, 0:1],
                 XB[3 + h % 2][:, 0:512], start=True, stop=True)
            P.op("pe", "matmul", ptc[lane:lane + 1, 512:1024], onesb[:, 0:1],
                 XB[3 + h % 2][:, 512:1024], start=True, stop=True,
                 inc=[("sPE", 1)])
            st_[("cs", h)] = P.n("sPE")
            P.wait("dve", "sPE", st_[("cs", h)])
            P.wait("dve", "sACT", st_.get(("rstd", h - 1), 0))
            P.op("dve", "tensor_scalar_max", ROWS[lane:lane + 1, :],
                 ptc[lane:lane + 1, :], 1.2829695e-22, inc=[("sDVE", 1)])
            dv = P.n("sDVE")
            PP.release(UCS, "sDVE", dv)
            P.wait("act", "sDVE", dv)
            P.op("act", "activation", ROWS[lane:lane + 1, :],
                 ROWS[lane:lane + 1, :], AF.Ln, inc=[("sACT", 1)])
            P.wait("act", "sACT", P.n("sACT"))
            P.op("act", "activation", ROWSB[lane:lane + 1, :],
                 ROWS[lane:lane + 1, :], AF.Exp, scale=-0.5,
                 bias=CB[lane:lane + 1, 0:1], inc=[("sACT", 1)])
            st_[("rstd", h)] = P.n("sACT")

        def srms_mul(h):
            lane = 32 + (h % 2) * 32
            ptb = PP.acquire("pe", UBC)
            P.wait("pe", "sACT", st_[("rstd", h)])
            P.op("pe", "matmul", ptb[:, 0:512], onesb[lane:lane + 1, 0:128],
                 ROWSB[lane:lane + 1, 0:512], start=True, stop=True)
            P.op("pe", "matmul", ptb[:, 512:1024], onesb[lane:lane + 1, 0:128],
                 ROWSB[lane:lane + 1, 512:1024], start=True, stop=True,
                 inc=[("sPE", 1)])
            pv3 = P.n("sPE")
            P.wait("dve", "sPE", pv3)
            P.wait("dve", "sDVE", st_.get(("tt2", h - 1), 0))
            P.op("dve", "tensor_mul", XB[2][:, :], S32[h % 2][:, :], ptb[:, :],
                 inc=[("sDVE", 1)])
            st_[("tt1", h)] = P.n("sDVE")
            PP.release(UBC, "sDVE", st_[("tt1", h)])
            P.wait("dve", "sDVE", st_[("tt1", h)])
            P.op("dve", "tensor_mul", B[h][:, :], XB[2][:, :], B[h][:, :],
                 inc=[("sDVE", 1)])
            st_[("tt2", h)] = P.n("sDVE")

        for h in range(L0_HEADS + 2):
            if h < L0_HEADS:
                srms_o(h)
            if 1 <= h <= L0_HEADS:
                srms_stats(h - 1)
            if h >= 2:
                srms_mul(h - 2)

        DVE_O = P.n("sDVE")
        P.wait("pe", "sDVE", DVE_O)
        P.wait("dve", "sX", X_READY)

        def evo(m, pt, pv):
            P.wait("dve", "sPE", pv)
            P.op("dve", "tensor_add", R[m][:, 0:512], R[m][:, 0:512], pt[:, 0:512])
            P.op("dve", "tensor_add", R[m][:, 512:1024], R[m][:, 512:1024],
                 pt[:, 512:1024], inc=[("sDVE", 1)])
            val = P.n("sDVE")
            ln_feed(m, "sDVE", val, pv)
            return ("sDVE", val)

        ln_begin(A[0], A[1])
        proj(B, o_sids, KC, evo)
        ln_finish()

    LNC = {"pend": []}

    def ln_begin(sq0, sq1):
        u = next_unit()
        pt = PP.acquire("pe", u)
        u2 = next_unit()
        pt2 = PP.acquire("pe", u2)
        state["avoid"] = {u, u2}
        LNC.update(u=u, u2=u2, pt=pt, pt2=pt2, sq=(sq0, sq1), cs={}, pend=[])

    def ln_chunk(m, rsem, rval, xbfree):
        pt, pt2 = LNC["pt"], LNC["pt2"]
        sq = LNC["sq"][m % 2]
        P.wait("act", rsem, rval)
        P.wait("act", "sPE", xbfree)
        P.op("act", "activation", XB[m][:, :], R[m][:, :], AF.Copy,
             inc=[("sACT", 1)])
        av = P.n("sACT")
        P.wait("dve", "sACT", av)
        P.wait("dve", "sPE", LNC["cs"].get(m - 2, 0))
        P.op("dve", "tensor_mul", sq[:, :], XB[m][:, :], XB[m][:, :],
             inc=[("sDVE", 1)])
        dv = P.n("sDVE")
        P.wait("pe", "sACT", av)
        for hh in range(2):
            n0 = hh * 512
            P.op("pe", "matmul", pt[32:33, n0:n0 + 512], onesb[:, 0:1],
                 XB[m][:, n0:n0 + 512], start=(m == 0), stop=(m == KC - 1))
        P.wait("pe", "sDVE", dv)
        for hh in range(2):
            n0 = hh * 512
            P.op("pe", "matmul", pt2[32:33, n0:n0 + 512], onesb[:, 0:1],
                 sq[:, n0:n0 + 512], start=(m == 0), stop=(m == KC - 1),
                 inc=[("sPE", 1)] if (m == KC - 1 and hh == 1) else ())
        LNC["cs"][m] = P.n("sPE")

    def ln_feed(m, rsem, rval, xbfree):
        LNC["pend"].append((m, rsem, rval, xbfree))
        if len(LNC["pend"]) > 1:
            ln_chunk(*LNC["pend"].pop(0))

    def ln_finish():
        for e_ in LNC["pend"]:
            ln_chunk(*e_)
        LNC["pend"] = []
        state["avoid"] = set()
        u, u2, pt, pt2 = LNC["u"], LNC["u2"], LNC["pt"], LNC["pt2"]
        pv = P.n("sPE")
        P.wait("act", "sPE", pv)
        P.op("act", "activation", ROWS[32:33, :], pt[32:33, :], AF.Copy,
             scale=1.0 / 2048.0, inc=[("sACT", 1)])
        P.op("act", "activation", S32[0][32:33, :], pt2[32:33, :], AF.Copy,
             scale=1.0 / 2048.0, inc=[("sACT", 1)])
        av = P.n("sACT")
        PP.release(u, "sACT", av)
        PP.release(u2, "sACT", av)
        P.wait("dve", "sACT", av)
        P.wait("dve", "sDVE", P.n("sDVE"))
        P.op("dve", "tensor_mul", S32[1][32:33, :], ROWS[32:33, :],
             ROWS[32:33, :], inc=[("sDVE", 1)])
        P.wait("dve", "sDVE", P.n("sDVE"))
        P.op("dve", "tensor_sub", S32[1][32:33, :], S32[0][32:33, :],
             S32[1][32:33, :], inc=[("sDVE", 1)])
        dv = P.n("sDVE")
        P.wait("act", "sDVE", dv)
        P.op("act", "activation", S32[1][32:33, :], S32[1][32:33, :], AF.Ln,
             bias=CB[32:33, 1:2], inc=[("sACT", 1)])
        P.wait("act", "sACT", P.n("sACT"))
        P.op("act", "activation", S32[1][32:33, :], S32[1][32:33, :], AF.Exp,
             scale=-0.5, inc=[("sACT", 1)])
        av2 = P.n("sACT")
        u3 = next_unit()
        ptm = PP.acquire("pe", u3)
        u4 = next_unit()
        ptr = PP.acquire("pe", u4)
        P.wait("pe", "sACT", av2)
        for hh in range(2):
            n0 = hh * 512
            P.op("pe", "matmul", ptm[:, n0:n0 + 512], onesf[32:33, :],
                 ROWS[32:33, n0:n0 + 512], start=True, stop=True)
            P.op("pe", "matmul", ptr[:, n0:n0 + 512], onesf[32:33, :],
                 S32[1][32:33, n0:n0 + 512], start=True, stop=True,
                 inc=[("sPE", 1)] if hh == 1 else ())
        pv2 = P.n("sPE")
        xbv = []
        P.wait("dve", "sPE", pv2)
        for m in range(KC):
            if m > 0:
                P.wait("dve", "sDVE", P.n("sDVE"))
            P.op("dve", "tensor_sub", S32[0][:, :], R[m][:, :], ptm[:, :],
                 inc=[("sDVE", 1)])
            P.wait("dve", "sDVE", P.n("sDVE"))
            P.op("dve", "tensor_mul", R[m][:, :], S32[0][:, :], ptr[:, :],
                 inc=[("sDVE", 1)])
            dv3 = P.n("sDVE")
            P.wait("act", "sDVE", dv3)
            P.op("act", "activation", XB[m][:, :], R[m][:, :], AF.Copy,
                 inc=[("sACT", 1)])
            xbv.append(P.n("sACT"))
        LN_XB["v"] = xbv
        PP.release(u3, "sDVE", P.n("sDVE"))
        PP.release(u4, "sDVE", P.n("sDVE"))

    def l0_ffn():
        P.wait("dve", "sDVE", P.n("sDVE"))
        P.wait("act", "sACT", P.n("sACT"))
        xbr = LN_XB["v"]
        sh_gu = []
        for m in range(MC_DS):
            sh_gu.append(WS.register(din["wsg_s"][m, :, :], 2048))
            sh_gu.append(WS.register(din["wsu_s"][m, :, :], 2048))
        sh_dn = []
        for m in range(KC):
            sh_dn.append((WS.register(din["wsd_s"][m, :, 0:2048], 2048), 16, 0))
            sh_dn.append((WS.register(din["wsd_s"][m, :, 2048:2816], 768), 6, 16))
        ex_gu, ex_dn = [], []
        for e in range(8):
            gu = []
            for m in range(MC_DE):
                gu.append(WS.register(din["weg_s"][e, m, :, :], 2048))
                gu.append(WS.register(din["weu_s"][e, m, :, :], 2048))
            ex_gu.append(gu)
            ex_dn.append([WS.register(din["wed_s"][e, m, :, 0:MC_DE * 128],
                                      MC_DE * 128) for m in range(KC)])

        u = next_unit()
        pt = PP.acquire("pe", u)
        for k in range(KC):
            if xbr is not None:
                P.wait("pe", "sACT", xbr[k])
            lhs = rtr[:, k * 8:(k + 1) * 8]
            for h in range(2):
                n0 = h * 512
                P.op("pe", "matmul", pt[0:8, n0:n0 + 512], lhs,
                     XB[k][:, n0:n0 + 512], start=(k == 0), stop=(k == KC - 1),
                     inc=[("sPE", 1)] if (k == KC - 1 and h == 1) else ())
        pv = P.n("sPE")
        P.wait("act", "sPE", pv)
        P.op("act", "activation", ROWS[0:8, :], pt[0:8, :], AF.Copy,
             inc=[("sACT", 1)])
        LOGV = P.n("sACT")
        PP.release(u, "sACT", LOGV)


        r_last = {}

        def gateup(gu_sids, mc, hpool, wb=None):
            for m in range(mc):
                bufg, sg, vg = WS.consume(gu_sids[2 * m])
                ug = next_unit()
                ptg = PP.acquire("pe", ug)
                P.wait("pe", sg, vg)
                for k in range(KC):
                    lhs = bufg[:, k * 128:(k + 1) * 128]
                    for h in range(2):
                        n0 = h * 512
                        P.op("pe", "matmul", ptg[:, n0:n0 + 512], lhs,
                             XB[k][:, n0:n0 + 512], start=(k == 0),
                             stop=(k == KC - 1),
                             inc=[("sPE", 1)] if (k == KC - 1 and h == 1) else ())
                pvg = P.n("sPE")
                WS.done(gu_sids[2 * m], "sPE", pvg)
                bufu, su, vu = WS.consume(gu_sids[2 * m + 1])
                uu = next_unit()
                ptu = PP.acquire("pe", uu)
                P.wait("pe", su, vu)
                for k in range(KC):
                    lhs = bufu[:, k * 128:(k + 1) * 128]
                    for h in range(2):
                        n0 = h * 512
                        P.op("pe", "matmul", ptu[:, n0:n0 + 512], lhs,
                             XB[k][:, n0:n0 + 512], start=(k == 0),
                             stop=(k == KC - 1),
                             inc=[("sPE", 1)] if (k == KC - 1 and h == 1) else ())
                pvu = P.n("sPE")
                WS.done(gu_sids[2 * m + 1], "sPE", pvu)
                P.wait("act", "sPE", pvg)
                P.wait("act", "sDVE", r_last.get("gsil", 0))
                P.op("act", "activation", S32[0][:, :], ptg[:, :], AF.Silu,
                     inc=[("sACT", 1)])
                av = P.n("sACT")
                PP.release(ug, "sACT", av)
                P.wait("dve", "sACT", av)
                P.wait("dve", "sPE", pvu)
                if wb is None:
                    P.op("dve", "tensor_mul", hpool[m][:, :], ptu[:, :],
                         S32[0][:, :], inc=[("sDVE", 1)])
                else:
                    P.wait("dve", "sDVE", r_last.get("tt2", 0))
                    P.op("dve", "tensor_mul", wb[1][:, :], ptu[:, :],
                         S32[0][:, :], inc=[("sDVE", 1)])
                dv = P.n("sDVE")
                r_last["gsil"] = dv
                PP.release(uu, "sDVE", dv)
                if wb is not None:
                    P.wait("dve", "sDVE", dv)
                    P.op("dve", "tensor_mul", hpool[m][:, :], wb[1][:, :],
                         wb[0][:, :], inc=[("sDVE", 1)])
                    r_last["tt2"] = P.n("sDVE")

        def down(dn_specs, hsrc, h_ready, feed=False):
            P.wait("pe", "sDVE", h_ready)
            for m, specs in enumerate(dn_specs):
                u = next_unit()
                pt = PP.acquire("pe", u)
                nsp = len(specs)
                for si, (sid, kcw, kr) in enumerate(specs):
                    buf, ws, wv = WS.consume(sid)
                    P.wait("pe", ws, wv)
                    for k in range(kcw):
                        lhs = buf[:, k * 128:(k + 1) * 128]
                        for h in range(2):
                            n0 = h * 512
                            P.op("pe", "matmul", pt[:, n0:n0 + 512], lhs,
                                 hsrc[kr + k][:, n0:n0 + 512],
                                 start=(si == 0 and k == 0),
                                 stop=(si == nsp - 1 and k == kcw - 1),
                                 inc=[("sPE", 1)] if (k == kcw - 1 and h == 1)
                                 else ())
                    WS.done(sid, "sPE", P.n("sPE"))
                pv = P.n("sPE")
                P.wait("dve", "sPE", pv)
                P.wait("dve", "sDVE", r_last.get(m, 0))
                P.op("dve", "tensor_add", R[m][:, 0:512], R[m][:, 0:512],
                     pt[:, 0:512])
                P.op("dve", "tensor_add", R[m][:, 512:1024], R[m][:, 512:1024],
                     pt[:, 512:1024], inc=[("sDVE", 1)])
                r_last[m] = P.n("sDVE")
                if feed:
                    ln_feed(m, "sDVE", r_last[m], pv)
                PP.release(u, "sDVE", r_last[m])

        hsh = B[0:16] + A[0:6]
        gateup(sh_gu, MC_DS, hsh)
        for t in range(8):
            cols = slice(t * 128, (t + 1) * 128)
            u = next_unit()
            pt = PP.acquire("pe", u)
            P.wait("pe", "sACT", LOGV)
            if t > 0:
                P.wait("pe", "sDVE", P.n("sDVE"))
            P.op("pe", "transpose", pt[:, 0:8], ROWS[0:8, cols],
                 ident[0:8, 0:8], inc=[("sPE", 1)])
            pv = P.n("sPE")
            E = SMX
            W = SMX[:, 32:48]
            P.wait("dve", "sPE", pv)
            P.wait("dve", "sACT", P.n("sACT"))
            P.op("dve", "reduce_max", W[:, 8:9], pt[:, 0:8], axis=AX.X,
                 negate=True, inc=[("sDVE", 1)])
            dv = P.n("sDVE")
            P.wait("act", "sDVE", dv)
            P.op("act", "activation", E[:, 0:8], pt[:, 0:8], AF.Exp,
                 bias=W[:, 8:9], inc=[("sACT", 1)])
            av = P.n("sACT")
            PP.release(u, "sACT", av)
            P.wait("dve", "sACT", av)
            P.op("dve", "reduce_sum", W[:, 9:10], E[:, 0:8], axis=AX.X,
                 inc=[("sDVE", 1)])
            dv = P.n("sDVE")
            P.wait("act", "sDVE", dv)
            P.op("act", "activation", W[:, 10:11], W[:, 9:10], AF.Ln,
                 inc=[("sACT", 1)])
            P.wait("act", "sACT", P.n("sACT"))
            P.op("act", "activation", W[:, 10:11], W[:, 10:11], AF.Exp,
                 scale=-1.0, inc=[("sACT", 1)])
            av = P.n("sACT")
            P.wait("dve", "sACT", av)
            P.op("dve", "tensor_single_scalar", E[:, 24:32], E[:, 0:8],
                 W[:, 10:11], ALU.mult, inc=[("sDVE", 1)])
            P.wait("dve", "sDVE", P.n("sDVE"))
            P.op("dve", "reduce_max", W[:, 11:12], E[:, 24:32], axis=AX.X,
                 inc=[("sDVE", 1)])
            P.wait("dve", "sDVE", P.n("sDVE"))
            P.op("dve", "tensor_single_scalar", E[:, 8:16], E[:, 24:32],
                 W[:, 11:12], ALU.is_ge, inc=[("sDVE", 1)])
            P.wait("dve", "sDVE", P.n("sDVE"))
            P.op("dve", "tensor_scalar", E[:, 8:16], E[:, 8:16], -1.0, 1.0,
                 ALU.mult, ALU.add, inc=[("sDVE", 1)])
            P.wait("dve", "sDVE", P.n("sDVE"))
            P.op("dve", "tensor_mul", E[:, 16:24], E[:, 24:32], E[:, 8:16],
                 inc=[("sDVE", 1)])
            P.wait("dve", "sDVE", P.n("sDVE"))
            P.op("dve", "reduce_max", W[:, 12:13], E[:, 16:24], axis=AX.X,
                 inc=[("sDVE", 1)])
            P.wait("dve", "sDVE", P.n("sDVE"))
            P.op("dve", "tensor_single_scalar", E[:, 8:16], E[:, 24:32],
                 W[:, 12:13], ALU.is_ge, inc=[("sDVE", 1)])
            P.wait("dve", "sDVE", P.n("sDVE"))
            P.op("dve", "tensor_mul", W[:, 0:8], E[:, 24:32], E[:, 8:16],
                 inc=[("sDVE", 1)])
            dv = P.n("sDVE")
            u = next_unit()
            pt = PP.acquire("pe", u)
            P.wait("pe", "sDVE", dv)
            P.op("pe", "transpose", pt[0:8, 0:128], W[:, 0:8], ident[:, :],
                 inc=[("sPE", 1)])
            pv = P.n("sPE")
            P.wait("act", "sPE", pv)
            P.op("act", "activation", ROWSB[0:8, cols], pt[0:8, 0:128], AF.Copy,
                 inc=[("sACT", 1)])
            PP.release(u, "sACT", P.n("sACT"))
        WRDY = P.n("sACT")
        down([[sh_dn[2 * m], sh_dn[2 * m + 1]] for m in range(KC)], hsh,
             P.n("sDVE"))

        for e in range(8):
            pool = A if e % 2 == 0 else B
            u = next_unit()
            pt = PP.acquire("pe", u)
            P.wait("pe", "sACT", WRDY)
            for h in range(2):
                n0 = h * 512
                P.op("pe", "matmul", pt[:, n0:n0 + 512],
                     ind8[:, e * 128:(e + 1) * 128], ROWSB[0:8, n0:n0 + 512],
                     start=True, stop=True,
                     inc=[("sPE", 1)] if h == 1 else ())
            pv = P.n("sPE")
            P.wait("act", "sPE", pv)
            P.op("act", "activation", pool[15][:, :], pt[:, :], AF.Copy,
                 inc=[("sACT", 1)])
            av = P.n("sACT")
            PP.release(u, "sACT", av)
            P.wait("dve", "sACT", av)
            gateup(ex_gu[e], MC_DE, pool, wb=(pool[15], pool[12]))
            if e == 7:
                ln_begin(B[13], B[14])
            down([[(ex_dn[e][m], MC_DE, 0)] for m in range(KC)], pool,
                 P.n("sDVE"), feed=(e == 7))

    def l1_attn():
        P.wait("dve", "sDVE", P.n("sDVE"))
        P.wait("act", "sACT", P.n("sACT"))
        P.wait("act", "sDVE", P.n("sDVE"))
        xbr = LN_XB["v"]
        rl = {}
        k_sid = WS.register(din["l1k_s"][:, :], KC * 64)
        q_sids = [WS.register(din["l1q_s"][m, :, :], 2048) for m in range(KC)]
        d_sids = [WS.register(din["l1d_s"][m, :, :], 2048) for m in range(KC)]

        VW = B[2]
        P.wait("gp", "sPE", P.n("sPE"))
        vsrc = bass.AP(din["l1v_r"], 0, [[64, 128], [128 * 64, KC], [1, 64]])
        P.op("gp", "dma_start", out=VW[:, 0:KC * 64], in_=vsrc, inc=[("sGP", 16)])
        VW_GP = P.n("sGP")
        uv = next_unit()
        ptv = PP.acquire("pe", uv)
        P.wait("pe", "sGP", VW_GP)
        for t in range(8):
            for k in range(KC):
                if t == 0 and xbr is not None:
                    P.wait("pe", "sACT", xbr[k])
                P.op("pe", "matmul", ptv[:, t * 64:(t + 1) * 64],
                     XB[k][:, t * 128:(t + 1) * 128], VW[:, k * 64:(k + 1) * 64],
                     start=(k == 0), stop=(k == KC - 1),
                     inc=[("sPE", 1)] if (t == 7 and k == KC - 1) else ())
        pv = P.n("sPE")
        P.wait("act", "sPE", pv)
        P.op("act", "activation", B[1][:, 0:512], ptv[:, 0:512], AF.Copy,
             inc=[("sACT", 1)])
        av = P.n("sACT")
        PP.release(uv, "sACT", av)
        P.wait("sp", "sACT", av)
        P.op("sp", "dma_start", out=l1kv_loc[1].rearrange("p f -> (p f)").rearrange(
            "(p f) -> p f", p=128), in_=B[1][:, 0:512], inc=[("sH", 16)])

        uk = next_unit()
        ptk = PP.acquire("pe", uk)
        bufk, ws, wv = WS.consume(k_sid)
        P.wait("pe", ws, wv)
        for k in range(KC):
            lhs = bufk[:, k * 64:(k + 1) * 64]
            for h in range(2):
                n0 = h * 512
                P.op("pe", "matmul", ptk[0:64, n0:n0 + 512], lhs,
                     XB[k][:, n0:n0 + 512], start=(k == 0), stop=(k == KC - 1),
                     inc=[("sPE", 1)] if (k == KC - 1 and h == 1) else ())
        pv = P.n("sPE")
        WS.done(k_sid, "sPE", pv)

        def rope(rows, src_ps, u_src, dst):
            P.wait("act", "sPE", P.n("sPE"))
            P.wait("act", "sDVE", rl.get("ropeS1", 0))
            P.op("act", "activation", S32[0][rows, :], src_ps[rows, :], AF.Copy,
                 inc=[("sACT", 1)])
            av = P.n("sACT")
            PP.release(u_src, "sACT", av)
            up = next_unit()
            ptp = PP.acquire("pe", up)
            P.wait("pe", "sACT", av)
            n = rows.stop - rows.start
            for h in range(2):
                n0 = h * 512
                P.op("pe", "matmul", ptp[rows, n0:n0 + 512],
                     perm[rows, 0:n], S32[0][rows, n0:n0 + 512],
                     start=True, stop=True,
                     inc=[("sPE", 1)] if h == 1 else ())
            pvp = P.n("sPE")
            P.wait("dve", "sACT", av)
            P.wait("dve", "sDVE", rl.get("ropeS1", 0))
            P.op("dve", "tensor_mul", S32[1][rows, :], S32[0][rows, :],
                 cos_t[rows, :], inc=[("sDVE", 1)])
            P.wait("dve", "sDVE", P.n("sDVE"))
            P.wait("dve", "sPE", pvp)
            P.op("dve", "tensor_mul", S32[0][rows, :], ptp[rows, :],
                 sin_t[rows, :], inc=[("sDVE", 1)])
            rl["ropeS0"] = P.n("sDVE")
            PP.release(up, "sDVE", rl["ropeS0"])
            P.wait("dve", "sDVE", rl["ropeS0"])
            P.op("dve", "tensor_add", dst, S32[1][rows, :], S32[0][rows, :],
                 inc=[("sDVE", 1)])
            rl["ropeS1"] = P.n("sDVE")

        rope(slice(0, 64), ptk, uk, B[0][0:64, :])
        P.wait("sp", "sDVE", rl["ropeS1"])
        P.op("sp", "dma_start", out=l1kv_loc[0].rearrange("p f -> (p f)").rearrange(
            "(p f) -> p f", p=64), in_=B[0][0:64, :], inc=[("sH", 16)])
        GP_KV = P.n("sH")

        groups = [[2 * i, 2 * i + 1] for i in range(max(1, ncores // 2))]
        P.wait("gp", "sH", GP_KV)
        P.op("gp", "collective_compute", "AllGather", ALU.bypass,
             replica_groups=groups, ins=[l1kv_loc.ap().opt()],
             outs=[l1kv_all.ap().opt()], inc=[("sCC", 1)])
        CCV = P.n("sCC")

        def evr(m, pt, pvq):
            rope(slice(0, 128), pt, None, A[m][:, :])
            return ("sDVE", rl["ropeS1"])

        for m in range(KC):
            buf, ws, wv = WS.consume(q_sids[m])
            u = next_unit()
            pt = PP.acquire("pe", u)
            P.wait("pe", ws, wv)
            for k in range(KC):
                lhs = buf[:, k * 128:(k + 1) * 128]
                for h in range(2):
                    n0 = h * 512
                    P.op("pe", "matmul", pt[:, n0:n0 + 512], lhs,
                         XB[k][:, n0:n0 + 512], start=(k == 0),
                         stop=(k == KC - 1),
                         inc=[("sPE", 1)] if (k == KC - 1 and h == 1) else ())
            pvq = P.n("sPE")
            WS.done(q_sids[m], "sPE", pvq)
            rope(slice(0, 128), pt, u, A[m][:, :])
        Q_PE = P.n("sPE")
        Q_DVE = P.n("sDVE")

        P.wait("sp", "sCC", CCV)
        P.wait("sp", "sPE", Q_PE)
        P.wait("gp", "sPE", Q_PE)
        for g in range(KC):
            rk = g % 2
            lt = g // 2
            src = l1kv_all[rk, 0].rearrange("p f -> (p f)").rearrange(
                "(p f) -> p f", p=64)[:, lt * 128:(lt + 1) * 128]
            dst = XB[g // 8]
            c0 = (g % 8) * 128
            P.op("sp", "dma_start", out=dst[0:64, c0:c0 + 128], in_=src,
                 inc=[("sH", 16)])
            P.op("sp", "dma_start", out=dst[64:128, c0:c0 + 128], in_=src,
                 inc=[("sH", 16)])
            srcv = l1kv_all[rk, 1].rearrange("p f -> (p f)").rearrange(
                "(p f) -> p f", p=128)[:, lt * 64:(lt + 1) * 64]
            dv_ = XB[4 + g // 8]
            cv = (g % 8) * 65
            P.op("sp", "dma_start", out=dv_[:, cv:cv + 64], in_=srcv,
                 inc=[("sH", 16)])
            P.op("gp", "memset", XB[4 + g // 8][:, cv + 64:cv + 65], 1.0,
                 inc=[("sGPC", 1)])
        KV_GP = P.n("sH")
        KV_GPC = P.n("sGPC")

        P.wait("pe", "sH", P.n("sH"))
        P.wait("pe", "sGPC", P.n("sGPC"))
        P.wait("pe", "sDVE", Q_DVE)
        P.wait("act", "sACT", P.n("sACT"))
        P.wait("act", "sDVE", P.n("sDVE"))
        UC, UD = 2, 3
        NPB = 3
        PSB = [XB[6], XB[7], XB[3]]
        exp_wait = {0: 0, 1: 0, 2: 0}
        for h in range(L1_NH):
            base = (h % 2) * 64
            rows = slice(base, base + 64)
            dl = 64 - (h % 2) * 32
            qch = A[h // 2]
            ptc = PP.acquire("pe", UC)
            ptd = PP.acquire("pe", UD)
            pend = []

            def emit_ctx(g, segs, last):
                vt65 = XB[4 + g // 8][:, (g % 8) * 65:(g % 8) * 65 + 65]
                ocol = vt65[:, 64:65]
                psb = PSB[g % NPB]
                P.wait("pe", "sDVE", rl[("mask", g % NPB)])
                for i, (s0, s1) in enumerate(segs):
                    lastg = (g == (7 if s0 < 512 else KC - 1))
                    if h % 2 == 0:
                        P.op("pe", "matmul", ptc[0:65, s0:s1], vt65,
                             psb[:, s0:s1], start=(g == 0), stop=lastg,
                             inc=[("sPE", 1)] if i == len(segs) - 1 else ())
                    else:
                        P.op("pe", "matmul", ptc[rows, s0:s1], vt65[:, 0:64],
                             psb[:, s0:s1], start=(g == 0), stop=lastg)
                        P.op("pe", "matmul", ptd[dl:dl + 1, s0:s1], ocol,
                             psb[:, s0:s1], start=(g == 0), stop=lastg,
                             inc=[("sPE", 1)] if i == len(segs) - 1 else ())
                exp_wait[g % NPB] = P.n("sPE")

            for g in range(KC):
                qs = 128 * (g // 2)
                us = g % 2
                pts = PP.acquire("pe", us)
                segs = []
                s0 = qs
                while s0 < 1024:
                    s1 = min(1024, (s0 // 512 + 1) * 512)
                    segs.append((s0, s1))
                    s0 = s1
                klhs = XB[g // 8][rows, (g % 8) * 128:(g % 8) * 128 + 128]
                for i, (s0, s1) in enumerate(segs):
                    P.op("pe", "matmul", pts[:, s0:s1], klhs, qch[rows, s0:s1],
                         start=True, stop=True,
                         inc=[("sPE", 1)] if i == len(segs) - 1 else ())
                pvs = P.n("sPE")
                pb = g % NPB
                psb = PSB[pb]
                P.wait("act", "sPE", max(pvs, exp_wait[pb]))
                P.wait("act", "sDVE", rl.get(("mask", pb), 0))
                P.op("act", "activation", psb[:, qs:1024], pts[:, qs:1024],
                     AF.Exp, scale=0.125, inc=[("sACT", 1)])
                ave = P.n("sACT")
                PP.release(us, "sACT", ave)
                P.wait("dve", "sACT", ave)
                P.wait("dve", "sPE", exp_wait[pb])
                P.op("dve", "tensor_mul", psb[:, qs:qs + 128],
                     psb[:, qs:qs + 128], dmask[:, g * 128:(g + 1) * 128],
                     inc=[("sDVE", 1)])
                rl[("mask", pb)] = P.n("sDVE")
                pend.append((g, segs))
                if len(pend) > 2:
                    emit_ctx(*pend.pop(0), last=False)
            for e_ in pend:
                emit_ctx(*e_, last=True)
            pend = []
            ctx_done_val = P.n("sPE")
            den_src = ptc if h % 2 == 0 else ptd
            P.wait("act", "sPE", ctx_done_val)
            P.wait("act", "sDVE", rl.get("rden", 0))
            P.op("act", "activation", S32[0][dl:dl + 1, :], den_src[dl:dl + 1, :],
                 AF.Ln, inc=[("sACT", 1)])
            P.wait("act", "sACT", P.n("sACT"))
            P.op("act", "activation", ROWSB[dl:dl + 1, :], S32[0][dl:dl + 1, :],
                 AF.Exp, scale=-1.0, inc=[("sACT", 1)])
            avd = P.n("sACT")
            PP.release(UD, "sACT", avd)
            ur = h % 2
            ptr_ = PP.acquire("pe", ur)
            P.wait("pe", "sACT", avd)
            for hh in range(2):
                n0 = hh * 512
                P.op("pe", "matmul", ptr_[rows, n0:n0 + 512],
                     onesb[dl:dl + 1, 0:64], ROWSB[dl:dl + 1, n0:n0 + 512],
                     start=True, stop=True, inc=[("sPE", 1)] if hh == 1 else ())
            pvr = P.n("sPE")
            P.wait("act", "sPE", pvr)
            P.op("act", "activation", S32[1][rows, :], ptr_[rows, :], AF.Copy,
                 inc=[("sACT", 1)])
            avr = P.n("sACT")
            PP.release(ur, "sACT", avr)
            P.wait("dve", "sACT", avr)
            P.wait("dve", "sPE", ctx_done_val)
            P.op("dve", "tensor_mul", qch[rows, :], ptc[rows, :], S32[1][rows, :],
                 inc=[("sDVE", 1)])
            rl["rden"] = P.n("sDVE")
            PP.release(UC, "sDVE", rl["rden"])

        CTX_DVE = P.n("sDVE")
        P.wait("pe", "sDVE", CTX_DVE)

        def evd(m, pt, pv):
            P.wait("dve", "sPE", pv)
            P.wait("dve", "sDVE", rl.get(("r", m), 0))
            P.op("dve", "tensor_add", R[m][:, 0:512], R[m][:, 0:512], pt[:, 0:512])
            P.op("dve", "tensor_add", R[m][:, 512:1024], R[m][:, 512:1024],
                 pt[:, 512:1024], inc=[("sDVE", 1)])
            rl[("r", m)] = P.n("sDVE")
            ln_feed(m, "sDVE", rl[("r", m)], pv)
            return ("sDVE", rl[("r", m)])

        ln_begin(B[0], B[1])
        proj(A, d_sids, KC, evd)
        ln_finish()

    def l1_ffn():
        P.wait("dve", "sDVE", P.n("sDVE"))
        P.wait("act", "sACT", P.n("sACT"))
        P.wait("act", "sDVE", P.n("sDVE"))
        xbr = LN_XB["v"]
        rl = {}
        P.op("sp", "dma_start", out=l1sc[:, 0:MC_DI], in_=din["l1gsc"][:, :],
             inc=[("sSP", 16)])
        P.op("sp", "dma_start", out=l1sc[:, MC_DI:2 * MC_DI],
             in_=din["l1usc"][:, :], inc=[("sSP", 16)])
        P.op("sp", "dma_start", out=l1sc[:, 2 * MC_DI:2 * MC_DI + KC],
             in_=din["l1dsc"][:, :], inc=[("sSP", 16)])
        SCV = P.n("sSP")
        P.wait("act", "sSP", SCV)
        P.wait("dve", "sSP", SCV)

        f8idx = [0]
        def reg8(src_ap, ncols):
            s = WS.register(src_ap, ncols, fp8=(F8, ("sF80", "sF81"), f8idx[0]))
            f8idx[0] += 1
            return s

        specs = []
        for kg in range(3):
            gu = []
            for mm_ in range(16):
                m = kg * 16 + mm_
                gu.append((m, reg8(din["l1g_s"][m, :, :], 2048),
                           reg8(din["l1u_s"][m, :, :], 2048)))
            dn = [reg8(din["l1dn_s"][m, :, kg * 2048:(kg + 1) * 2048], 2048)
                  for m in range(KC)]
            specs.append((gu, dn))

        for kg, (gu, dn) in enumerate(specs):
            for mm_, (m, sg, su) in enumerate(gu):
                bufg, ws, wv = WS.consume(sg)
                ug = next_unit()
                ptg = PP.acquire("pe", ug)
                P.wait("pe", ws, wv)
                for k in range(KC):
                    if kg == 0 and mm_ == 0 and xbr is not None:
                        P.wait("pe", "sACT", xbr[k])
                    lhs = bufg[:, k * 128:(k + 1) * 128]
                    for h in range(2):
                        n0 = h * 512
                        P.op("pe", "matmul", ptg[:, n0:n0 + 512], lhs,
                             XB[k][:, n0:n0 + 512], start=(k == 0),
                             stop=(k == KC - 1),
                             inc=[("sPE", 1)] if (k == KC - 1 and h == 1) else ())
                pvg = P.n("sPE")
                WS.done(sg, "sPE", pvg)
                bufu, ws, wv = WS.consume(su)
                uu = next_unit()
                ptu = PP.acquire("pe", uu)
                P.wait("pe", ws, wv)
                for k in range(KC):
                    lhs = bufu[:, k * 128:(k + 1) * 128]
                    for h in range(2):
                        n0 = h * 512
                        P.op("pe", "matmul", ptu[:, n0:n0 + 512], lhs,
                             XB[k][:, n0:n0 + 512], start=(k == 0),
                             stop=(k == KC - 1),
                             inc=[("sPE", 1)] if (k == KC - 1 and h == 1) else ())
                pvu = P.n("sPE")
                WS.done(su, "sPE", pvu)
                P.wait("act", "sPE", pvg)
                P.wait("act", "sDVE", rl.get("gsil", 0))
                P.op("act", "activation", S32[0][:, :], ptg[:, :], AF.Silu,
                     scale=l1sc[:, m:m + 1], inc=[("sACT", 1)])
                av = P.n("sACT")
                PP.release(ug, "sACT", av)
                P.wait("dve", "sACT", av)
                P.wait("dve", "sPE", pvu)
                P.op("dve", "scalar_tensor_tensor", B[mm_][:, :], ptu[:, :],
                     l1sc[:, MC_DI + m:MC_DI + m + 1], S32[0][:, :],
                     ALU.mult, ALU.mult, inc=[("sDVE", 1)])
                rl["gsil"] = P.n("sDVE")
                PP.release(uu, "sDVE", rl["gsil"])
            if kg == 2:
                ln_begin(A[0], A[1])
            H_DVE = P.n("sDVE")
            P.wait("pe", "sDVE", H_DVE)
            for m in range(KC):
                buf, ws, wv = WS.consume(dn[m])
                u = next_unit()
                pt = PP.acquire("pe", u)
                P.wait("pe", ws, wv)
                for k in range(16):
                    lhs = buf[:, k * 128:(k + 1) * 128]
                    for h in range(2):
                        n0 = h * 512
                        P.op("pe", "matmul", pt[:, n0:n0 + 512], lhs,
                             B[k][:, n0:n0 + 512], start=(k == 0),
                             stop=(k == 15),
                             inc=[("sPE", 1)] if (k == 15 and h == 1) else ())
                pv = P.n("sPE")
                WS.done(dn[m], "sPE", pv)
                P.wait("dve", "sPE", pv)
                P.wait("dve", "sDVE", rl.get(("r", m), 0))
                dcol = l1sc[:, 2 * MC_DI + m:2 * MC_DI + m + 1]
                P.op("dve", "scalar_tensor_tensor", R[m][:, 0:512],
                     pt[:, 0:512], dcol, R[m][:, 0:512], ALU.mult, ALU.add)
                P.op("dve", "scalar_tensor_tensor", R[m][:, 512:1024],
                     pt[:, 512:1024], dcol, R[m][:, 512:1024], ALU.mult,
                     ALU.add, inc=[("sDVE", 1)])
                rl[("r", m)] = P.n("sDVE")
                if kg == 2:
                    ln_feed(m, "sDVE", rl[("r", m)], pv)
                PP.release(u, "sDVE", rl[("r", m)])

    l0_attn()
    if stop_after != "l0attn":
        l0_ffn()
        ln_finish()
    if stop_after not in ("l0attn", "l0ffn"):
        l1_attn()
    if stop_after is None:
        l1_ffn()
        ln_finish()

    P.wait("sp", "sACT", P.n("sACT"))
    P.wait("sp", "sDVE", P.n("sDVE"))
    ydst = bass.AP(dy, 0, [[T, 128], [128 * T, KC], [1, T]])
    P.op("sp", "dma_start", out=ydst, in_=RALL[:, :], inc=[("sOUT", 16)])
    P.wait("sp", "sOUT", P.n("sOUT"))

    with nc.Block() as block:
        @block.gpsimd
        def _(g):
            P.emit("gp", g)

        @block.tensor
        def _(t):
            P.emit("pe", t)

        @block.vector
        def _(v):
            P.emit("dve", v)

        @block.scalar
        def _(s):
            P.emit("act", s)

        @block.sync
        def _(s):
            P.emit("sp", s)

    return nc


def kernel(**inputs):
    in_maps = prepare_inputs(inputs, NCORES)
    nc = build_nc(NCORES)
    res = run_bass_kernel_spmd(nc, in_maps, core_ids=list(range(NCORES)))
    return gather_output(res.results, NCORES)


# version 06
# speedup vs baseline: 1.0397x; 1.0397x over previous
import math

import numpy as np

import concourse.bass as bass
import concourse.mybir as mybir
from concourse.bass_utils import run_bass_kernel_spmd

F32 = mybir.dt.float32
BF16 = mybir.dt.bfloat16
FP8 = mybir.dt.float8e4
AF = mybir.ActivationFunctionType
ALU = mybir.AluOpType
AX = mybir.AxisListType

D = 2048
KC = 16
T = 1024
NCORES = 8
L0_HEADS = 16
L0_SCALE = float(np.sqrt(128.0))
L1_NH, L1_HD = 32, 64
DS = 2816
DE = 1408
DI = 6144
MC_DS = DS // 128
MC_DE = DE // 128
MC_DI = DI // 128

STOP_AFTER = None



def _mstripes(w_io):
    K, M = w_io.shape
    kc, mc = K // 128, M // 128
    t = w_io.reshape(kc, 128, mc, 128).transpose(2, 1, 0, 3).reshape(mc, 128, kc * 128)
    return np.ascontiguousarray(t)


def _rowstripes(w_io):
    K, M = w_io.shape
    return np.ascontiguousarray(w_io.reshape(K // 128, 128, M))


def _rope_tables(positions):
    import jax
    import jax.numpy as jnp
    with jax.default_device(jax.devices("cpu")[0]):
        inv_freq = 1.0 / 10000 ** (jnp.arange(0, L1_HD, 2, dtype=jnp.float32) / L1_HD)
        pos = jnp.asarray(positions, dtype=jnp.float32)
        freqs = pos[:, None] * inv_freq[None, :]
        cos = np.asarray(jnp.cos(freqs)).T
        sin = np.asarray(jnp.sin(freqs)).T
    cos64 = np.concatenate([cos, cos], axis=0)
    sin64 = np.concatenate([-sin, sin], axis=0)
    cos128 = np.concatenate([cos64, cos64], axis=0)
    sin128 = np.concatenate([sin64, sin64], axis=0)
    return np.ascontiguousarray(cos128), np.ascontiguousarray(sin128)


def _consts():
    ident = np.eye(128, dtype=np.float32)
    perm = np.zeros((128, 128), dtype=np.float32)
    for m in range(128):
        p = (m // 64) * 64 + ((m % 64) + 32) % 64
        perm[p, m] = 1.0
    tri = np.triu(np.ones((128, 128), dtype=np.float32)).astype(np.float32)
    ones = np.ones((128, 128), dtype=np.float32)
    ind8 = np.zeros((8, 8 * 128), dtype=np.float32)
    for e in range(8):
        ind8[e, e * 128:(e + 1) * 128] = 1.0
    return ident, perm, tri, ones, ind8


def prepare_inputs(inputs, ncores=NCORES):
    f = {k: np.asarray(v) for k, v in inputs.items()}
    shared = {}
    shared["wq_s"] = _mstripes(np.ascontiguousarray(f["L0_attn_wq"].T))
    shared["wu_s"] = _mstripes(np.ascontiguousarray(f["L0_attn_wu"].T))
    shared["wo_s"] = _mstripes(np.ascontiguousarray(f["L0_attn_wo"].T))
    shared["wk_r"] = _rowstripes(np.ascontiguousarray(f["L0_attn_wk"].T))
    shared["wv_r"] = _rowstripes(np.ascontiguousarray(f["L0_attn_wv"].T))
    shared["wrtr"] = np.ascontiguousarray(
        f["L0_ffn_W_router"].reshape(KC, 128, 8).transpose(1, 0, 2).reshape(128, KC * 8))
    shared["wsg_s"] = _mstripes(f["L0_ffn_Ws_gate"])
    shared["wsu_s"] = _mstripes(f["L0_ffn_Ws_up"])
    shared["wsd_s"] = _mstripes(f["L0_ffn_Ws_down"])
    shared["weg_s"] = np.ascontiguousarray(
        np.stack([_mstripes(f["L0_ffn_We_gate"][e]) for e in range(8)]))
    shared["weu_s"] = np.ascontiguousarray(
        np.stack([_mstripes(f["L0_ffn_We_up"][e]) for e in range(8)]))
    shared["wed_s"] = np.ascontiguousarray(
        np.stack([_mstripes(f["L0_ffn_We_down"][e]) for e in range(8)]))
    wqkv = f["L1_attn_w_qkv"]
    shared["l1q_s"] = _mstripes(np.ascontiguousarray(wqkv[:, 0:2048]))
    shared["l1k_s"] = np.ascontiguousarray(
        wqkv[:, 2048:2112].reshape(KC, 128, 64).transpose(1, 0, 2).reshape(128, KC * 64))
    shared["l1v_r"] = _rowstripes(np.ascontiguousarray(wqkv[:, 2112:2176]))
    shared["l1d_s"] = _mstripes(f["L1_attn_w_dense"])
    shared["l1g_s"] = _mstripes(f["L1_ffn_gate_w"])
    shared["l1u_s"] = _mstripes(f["L1_ffn_up_w"])
    shared["l1dn_s"] = _mstripes(f["L1_ffn_down_w"])
    shared["l1gsc"] = np.ascontiguousarray(f["L1_ffn_gate_scale"].reshape(MC_DI, 128).T)
    shared["l1usc"] = np.ascontiguousarray(f["L1_ffn_up_scale"].reshape(MC_DI, 128).T)
    shared["l1dsc"] = np.ascontiguousarray(f["L1_ffn_down_scale"].reshape(KC, 128).T)
    ident, perm, tri, ones, ind8 = _consts()
    cb = np.zeros((128, 4), dtype=np.float32)
    cb[:, 0] = 0.5 * math.log(128.0)
    cb[:, 1] = 1e-5
    shared["c_bias"] = cb
    shared["c_ident"] = ident
    shared["c_perm"] = perm
    shared["c_tri"] = tri
    shared["c_ones"] = ones
    shared["c_ind8"] = ind8

    x = f["L0_attn_x"]
    in_maps = []
    for c in range(ncores):
        b, par = c // 2, c % 2
        blocks = [par + 2 * j for j in range(8)]
        tok = np.concatenate([np.arange(g * 128, (g + 1) * 128) for g in blocks])
        m = dict(shared)
        m["x"] = np.ascontiguousarray(x[b, tok, :].T)
        cosq, sinq = _rope_tables(tok)
        m["rope_cos"] = cosq
        m["rope_sin"] = sinq
        dm = np.zeros((128, KC * 128), dtype=np.float32)
        tri = np.triu(np.ones((128, 128), dtype=np.float32))
        for g in range(KC):
            gb = par + 2 * (g // 2)
            if gb == g:
                dm[:, g * 128:(g + 1) * 128] = tri
            elif gb > g:
                dm[:, g * 128:(g + 1) * 128] = 1.0
        m["c_dmask"] = dm
        in_maps.append(m)
    return in_maps


def gather_output(results, ncores=NCORES):
    out = np.empty((4, 2048, 2048), dtype=np.float32)
    for c in range(ncores):
        b, par = c // 2, c % 2
        blocks = [par + 2 * j for j in range(8)]
        tok = np.concatenate([np.arange(g * 128, (g + 1) * 128) for g in blocks])
        out[b, tok, :] = results[c]["y"].T
    return out



ENGS = ("gp", "pe", "dve", "act", "sp")


class Plan:
    def __init__(self, nc):
        self.nc = nc
        self.q = {e: [] for e in ENGS}
        self.cnt = {}
        self.sems = {}

    def sem(self, name):
        if name not in self.sems:
            self.sems[name] = self.nc.alloc_semaphore(name=name)
            self.cnt[name] = 0
        return name

    def wait(self, eng, s, v):
        if v > 0:
            self.q[eng].append(("w", s, int(v)))

    def op(self, eng, fn, *args, inc=(), **kw):
        for s, k in inc:
            self.cnt[s] += k
        self.q[eng].append(("o", fn, args, kw, tuple(inc)))

    def n(self, s):
        return self.cnt.get(s, 0)

    def emit(self, eng, engine):
        for item in self.q[eng]:
            if item[0] == "w":
                engine.wait_ge(self.sems[item[1]], item[2])
            else:
                _, fn, args, kw, inc = item
                inst = getattr(engine, fn)(*args, **kw)
                for s, k in inc:
                    inst = inst.then_inc(self.sems[s], k)


class PsumPool:

    def __init__(self, plan, tensors):
        self.p = plan
        self.t = tensors
        self.free_at = [None] * len(tensors)

    def acquire(self, eng, u):
        fa = self.free_at[u]
        if fa is not None:
            self.p.wait(eng, fa[0], fa[1])
        return self.t[u]

    def release(self, u, sem, val):
        self.free_at[u] = (sem, val)


class WStream:

    NB = 3
    LA = 2

    def __init__(self, plan, bufs):
        self.p = plan
        self.bufs = bufs
        self.sems = [plan.sem(f"sW{j}") for j in range(self.NB)]
        self.stripes = []
        self.loaded = 0
        self.consumed_at = {}
        self.f8_done = {}
        self.nexti = 0

    def register(self, src_ap, ncols, fp8=None):
        self.stripes.append((src_ap, ncols, fp8))
        return len(self.stripes) - 1

    def _plan_load(self):
        k = self.loaded
        if k >= len(self.stripes):
            return
        src, ncols, fp8 = self.stripes[k]
        b = k % self.NB
        if fp8 is not None:
            f8bufs, f8sems, f8idx = fp8
            j = f8idx % 2
            prev = self.f8_done.get(f8idx - 2)
            if prev is not None:
                self.p.wait("gp", prev[0], prev[1])
            self.p.op("gp", "dma_start", out=f8bufs[j][:, 0:ncols], in_=src,
                      inc=[(f8sems[j], 16)])
            self.p.wait("gp", f8sems[j], self.p.n(f8sems[j]))
            if k >= self.NB:
                dep = self.consumed_at.get(k - self.NB)
                assert dep is not None
                self.p.wait("gp", dep[0], dep[1])
            self.p.op("gp", "dma_start", out=self.bufs[b][:, 0:ncols],
                      in_=f8bufs[j][:, 0:ncols], inc=[(self.sems[b], 16)])
            self.f8_done[f8idx] = (self.sems[b], 16 * (k // self.NB + 1))
        else:
            if k >= self.NB:
                dep = self.consumed_at.get(k - self.NB)
                assert dep is not None, f"stripe {k} loads before {k - self.NB} consumed"
                self.p.wait("gp", dep[0], dep[1])
            self.p.op("gp", "dma_start", out=self.bufs[b][:, 0:ncols], in_=src,
                      inc=[(self.sems[b], 16)])
        self.loaded += 1

    def consume(self, k):
        while self.loaded <= min(k + self.LA, len(self.stripes) - 1):
            self._plan_load()
        b = k % self.NB
        return self.bufs[b], self.sems[b], 16 * (k // self.NB + 1)

    def done(self, k, sem, val):
        self.consumed_at[k] = (sem, val)



def build_nc(ncores=NCORES, stop_after=STOP_AFTER):
    nc = bass.Bass("TRN2", target_bir_lowering=False)

    dx = nc.dram_tensor("x", [D, T], F32, kind="ExternalInput")
    dy = nc.dram_tensor("y", [D, T], F32, kind="ExternalOutput")
    din = {}
    def di(name, shape):
        din[name] = nc.dram_tensor(name, shape, F32, kind="ExternalInput")
        return din[name]
    di("wq_s", [KC, 128, KC * 128]); di("wu_s", [KC, 128, KC * 128])
    di("wo_s", [KC, 128, KC * 128])
    di("wk_r", [KC, 128, D]); di("wv_r", [KC, 128, D])
    di("wrtr", [128, KC * 8])
    di("wsg_s", [MC_DS, 128, KC * 128]); di("wsu_s", [MC_DS, 128, KC * 128])
    di("wsd_s", [KC, 128, MC_DS * 128])
    di("weg_s", [8, MC_DE, 128, KC * 128]); di("weu_s", [8, MC_DE, 128, KC * 128])
    di("wed_s", [8, KC, 128, MC_DE * 128])
    di("l1q_s", [KC, 128, KC * 128]); di("l1k_s", [128, KC * 64])
    di("l1v_r", [KC, 128, 64]); di("l1d_s", [KC, 128, KC * 128])
    di("l1g_s", [MC_DI, 128, KC * 128]); di("l1u_s", [MC_DI, 128, KC * 128])
    di("l1dn_s", [KC, 128, MC_DI * 128])
    di("l1gsc", [128, MC_DI]); di("l1usc", [128, MC_DI]); di("l1dsc", [128, KC])
    di("c_bias", [128, 4])
    di("c_ident", [128, 128]); di("c_perm", [128, 128]); di("c_tri", [128, 128])
    di("c_ones", [128, 128]); di("c_ind8", [8, 8 * 128])
    di("rope_cos", [128, T]); di("rope_sin", [128, T])
    di("c_dmask", [128, KC * 128])

    kv_part = nc.dram_tensor("kv_part", [4, 128, 512], F32)
    kv_sum = nc.dram_tensor("kv_sum", [4, 128, 512], F32)
    l1kv_loc = nc.dram_tensor("l1kv_loc", [2, 64, T], BF16)
    l1kv_all = nc.dram_tensor("l1kv_all", [2, 2, 64, T], BF16)

    a = nc.alloc_sbuf_tensor
    RALL = a("RALL", [128, KC * T], F32)
    XBALL = a("XBALL", [128, KC * T], BF16)
    AALL = a("AALL", [128, KC * T], BF16)
    BALL = a("BALL", [128, KC * T], BF16)
    R = [RALL[:, i * T:(i + 1) * T] for i in range(KC)]
    XB = [XBALL[:, i * T:(i + 1) * T] for i in range(KC)]
    A = [AALL[:, i * T:(i + 1) * T] for i in range(KC)]
    B = [BALL[:, i * T:(i + 1) * T] for i in range(KC)]
    WB = [a(f"WB{j}", [128, 2048], BF16) for j in range(WStream.NB)]
    F8 = [a(f"F8{j}", [128, 2048], FP8) for j in range(2)]
    S32 = [a(f"S32{j}", [128, T], F32) for j in range(2)]
    ROWS = a("ROWS", [128, T], F32)
    ROWSB = a("ROWSB", [128, T], BF16)
    cos_t = a("cos_t", [128, T], F32)
    sin_t = a("sin_t", [128, T], F32)
    ident = a("ident", [128, 128], F32)
    perm = a("perm", [128, 128], F32)
    tri = a("tri", [128, 128], BF16)
    onesf = a("onesf", [128, 128], F32)
    onesb = a("onesb", [128, 128], BF16)
    CB = a("CB", [128, 4], F32)
    ind8 = a("ind8", [8, 8 * 128], BF16)
    rtr = a("rtr", [128, KC * 8], BF16)
    dmask = a("dmask", [128, KC * 128], BF16)
    SMX = a("SMX", [128, 48], F32)
    l1sc = a("l1sc", [128, 2 * MC_DI + KC], F32)
    psum = [nc.alloc_psum_tensor(f"PS{u}", [128, 1024], F32) for u in range(4)]

    P = Plan(nc)
    PP = PsumPool(P, psum)
    LN_XB = {"v": None}
    for s in ("sX", "sXG", "sPE", "sDVE", "sACT", "sGP", "sGPC", "sCC",
              "sOUT", "sSP", "sH", "sF80", "sF81"):
        P.sem(s)
    WS = WStream(P, WB)

    xsrc = bass.AP(dx, 0, [[T, 128], [128 * T, KC], [1, T]])
    P.op("sp", "dma_start", out=RALL[:, :], in_=xsrc, inc=[("sX", 16)])
    P.op("gp", "dma_start", out=XBALL[:, :], in_=xsrc, inc=[("sXG", 16)])
    for dst, src in ((cos_t, din["rope_cos"]), (sin_t, din["rope_sin"]),
                     (ident, din["c_ident"]), (perm, din["c_perm"]),
                     (onesf, din["c_ones"]), (CB, din["c_bias"])):
        P.op("sp", "dma_start", out=dst[:, :], in_=src[:, :], inc=[("sX", 16)])
    for dst, src in ((tri, din["c_tri"]), (onesb, din["c_ones"][:, :]),
                     (ind8, din["c_ind8"]), (rtr, din["wrtr"]),
                     (dmask, din["c_dmask"])):
        P.op("gp", "dma_start", out=dst[:, :], in_=src[:, :], inc=[("sXG", 16)])
    X_READY = P.n("sX")
    XG_READY = P.n("sXG")

    for e in ("pe", "dve", "act"):
        P.wait(e, "sX", X_READY)
        P.wait(e, "sXG", XG_READY)

    state = {"u": 0, "avoid": set()}

    def next_unit():
        while True:
            u = state["u"]
            state["u"] = (u + 1) % 4
            if u not in state["avoid"]:
                return u

    def proj(src_chunks, stripe_ids, kcw, evac, kr=0):
        for m, sid in enumerate(stripe_ids):
            buf, wsem, wval = WS.consume(sid)
            u = next_unit()
            pt = PP.acquire("pe", u)
            P.wait("pe", wsem, wval)
            for k in range(kcw):
                lhs = buf[:, k * 128:(k + 1) * 128]
                for h in range(2):
                    n0 = h * 512
                    P.op("pe", "matmul", pt[:, n0:n0 + 512], lhs,
                         src_chunks[kr + k][:, n0:n0 + 512],
                         start=(k == 0), stop=(k == kcw - 1),
                         inc=[("sPE", 1)] if (k == kcw - 1 and h == 1) else ())
            pv = P.n("sPE")
            WS.done(sid, "sPE", pv)
            sem, val = evac(m, pt, pv)
            PP.release(u, sem, val)

    def l0_attn():
        kv_sids = [[[WS.register(
                        bass.AP(w, 4 * k4 * 128 * 2048 + hg * 512,
                                [[2048, 128], [128 * 2048, 4], [1, 512]]), 2048)
                     for k4 in range(KC // 4)]
                    for w in (din["wk_r"], din["wv_r"])] for hg in range(4)]
        q_sids = [WS.register(din["wq_s"][m, :, :], 2048) for m in range(KC)]
        u_sids = [WS.register(din["wu_s"][m, :, :], 2048) for m in range(KC)]
        o_sids = [WS.register(din["wo_s"][m, :, :], 2048) for m in range(KC)]

        gp_kv_dma = 0
        for hg in range(4):
            CK = [B[t // 2][:, (t % 2) * 512:(t % 2 + 1) * 512] for t in range(8)]
            CV = [B[4 + t // 2][:, (t % 2) * 512:(t % 2 + 1) * 512] for t in range(8)]
            for which, CC, actf in ((0, CK, AF.Relu), (1, CV, AF.Copy)):
                units = [next_unit() for _ in range(4)]
                for uu in units:
                    PP.acquire("pe", uu)
                for k4 in range(KC // 4):
                    sid = kv_sids[hg][which][k4]
                    buf, wsem, wval = WS.consume(sid)
                    P.wait("pe", wsem, wval)
                    for kk in range(4):
                        k = 4 * k4 + kk
                        for t in range(8):
                            pt = psum[units[t % 4]]
                            half = (t // 4) * 512
                            P.op("pe", "matmul", pt[:, half:half + 512],
                                 XB[k][:, t * 128:(t + 1) * 128],
                                 buf[:, kk * 512:kk * 512 + 512],
                                 start=(k == 0), stop=(k == KC - 1),
                                 inc=[("sPE", 1)] if t == 7 else ())
                    WS.done(sid, "sPE", P.n("sPE"))
                pv = P.n("sPE")
                P.wait("act", "sPE", pv)
                for t in range(8):
                    pt = psum[units[t % 4]]
                    half = (t // 4) * 512
                    P.op("act", "activation", CC[t], pt[:, half:half + 512], actf,
                         scale=(1.0 / L0_SCALE) if which == 0 else 1.0,
                         inc=[("sACT", 1)])
                av = P.n("sACT")
                for uu in units:
                    PP.release(uu, "sACT", av)
            u = next_unit()
            pt = PP.acquire("pe", u)
            P.wait("pe", "sACT", P.n("sACT"))
            for hh in range(4):
                hc = slice(hh * 128, (hh + 1) * 128)
                for t in range(8):
                    P.op("pe", "matmul", pt[:, hc], CK[t][:, hc], CV[t][:, hc],
                         start=(t == 0), stop=(t == 7),
                         inc=[("sPE", 1)] if (hh == 3 and t == 7) else ())
            pv = P.n("sPE")
            slot = S32[hg % 2]
            P.wait("act", "sPE", pv)
            if hg >= 2:
                P.wait("act", "sH", 16 * hg)
            P.op("act", "activation", slot[:, 0:512], pt[:, 0:512], AF.Copy,
                 inc=[("sACT", 1)])
            av = P.n("sACT")
            PP.release(u, "sACT", av)
            P.wait("sp", "sACT", av)
            P.op("sp", "dma_start", out=kv_part[hg, :, :], in_=slot[:, 0:512],
                 inc=[("sH", 16)])
            gp_kv_dma = P.n("sH")
        groups = [[2 * i, 2 * i + 1] for i in range(max(1, ncores // 2))]
        P.wait("gp", "sH", gp_kv_dma)
        P.op("gp", "collective_compute", "AllReduce", ALU.add,
             replica_groups=groups, ins=[kv_part.ap().opt()],
             outs=[kv_sum.ap().opt()], inc=[("sCC", 1)])
        CC_DONE = P.n("sCC")

        def evq(m, pt, pv):
            P.wait("act", "sPE", pv)
            P.op("act", "activation", A[m][:, 0:512], pt[:, 0:512], AF.Relu,
                 scale=1.0 / L0_SCALE)
            P.op("act", "activation", A[m][:, 512:1024], pt[:, 512:1024], AF.Relu,
                 scale=1.0 / L0_SCALE, inc=[("sACT", 1)])
            return ("sACT", P.n("sACT"))

        def evu(m, pt, pv):
            P.wait("act", "sPE", pv)
            P.op("act", "activation", B[m][:, 0:512], pt[:, 0:512], AF.Silu)
            P.op("act", "activation", B[m][:, 512:1024], pt[:, 512:1024], AF.Silu,
                 inc=[("sACT", 1)])
            return ("sACT", P.n("sACT"))

        proj(XB, q_sids, KC, evq)
        proj(XB, u_sids, KC, evu)
        ACT_QU = P.n("sACT")
        PE_PROJ = P.n("sPE")

        P.wait("gp", "sCC", CC_DONE)
        P.wait("gp", "sPE", PE_PROJ)
        for hg in range(4):
            P.op("gp", "dma_start",
                 out=XB[hg // 2][:, (hg % 2) * 512:(hg % 2 + 1) * 512],
                 in_=kv_sum[hg, :, :], inc=[("sGP", 16)])
        KVRDY = P.n("sGP")

        P.wait("pe", "sGP", KVRDY)
        P.wait("pe", "sACT", ACT_QU)
        P.wait("dve", "sACT", ACT_QU)
        P.wait("act", "sDVE", P.n("sDVE"))
        UO = [0, 1]
        UCS, UBC = 2, 3
        st_ = {}

        def srms_o(h):
            kvt = XB[h // 8][:, (h % 8) * 128:(h % 8 + 1) * 128]
            u = UO[h % 2]
            pt = PP.acquire("pe", u)
            P.op("pe", "matmul", pt[:, 0:512], kvt, A[h][:, 0:512],
                 start=True, stop=True)
            P.op("pe", "matmul", pt[:, 512:1024], kvt, A[h][:, 512:1024],
                 start=True, stop=True, inc=[("sPE", 1)])
            st_[("opv", h)] = P.n("sPE")

        def srms_stats(h):
            u = UO[h % 2]
            pt = psum[u]
            P.wait("act", "sPE", st_[("opv", h)])
            P.wait("act", "sDVE", st_.get(("tt1", h - 2), 0))
            P.wait("act", "sPE", st_.get(("cs", h - 2), 0))
            P.op("act", "activation", S32[h % 2][:, :], pt[:, :], AF.Copy,
                 inc=[("sACT", 1)])
            P.op("act", "activation", XB[3 + h % 2][:, :], pt[:, :], AF.Square,
                 inc=[("sACT", 1)])
            av = P.n("sACT")
            st_[("ev", h)] = av
            PP.release(u, "sACT", av)
            ptc = PP.acquire("pe", UCS)
            P.wait("pe", "sACT", av)
            lane = 32 + (h % 2) * 32
            P.op("pe", "matmul", ptc[lane:lane + 1, 0:512], onesb[:, 0:1],
                 XB[3 + h % 2][:, 0:512], start=True, stop=True)
            P.op("pe", "matmul", ptc[lane:lane + 1, 512:1024], onesb[:, 0:1],
                 XB[3 + h % 2][:, 512:1024], start=True, stop=True,
                 inc=[("sPE", 1)])
            st_[("cs", h)] = P.n("sPE")
            P.wait("dve", "sPE", st_[("cs", h)])
            P.wait("dve", "sACT", st_.get(("rstd", h - 1), 0))
            P.op("dve", "tensor_scalar_max", ROWS[lane:lane + 1, :],
                 ptc[lane:lane + 1, :], 1.2829695e-22, inc=[("sDVE", 1)])
            dv = P.n("sDVE")
            PP.release(UCS, "sDVE", dv)
            P.wait("act", "sDVE", dv)
            P.op("act", "activation", ROWS[lane:lane + 1, :],
                 ROWS[lane:lane + 1, :], AF.Ln, inc=[("sACT", 1)])
            P.wait("act", "sACT", P.n("sACT"))
            P.op("act", "activation", ROWSB[lane:lane + 1, :],
                 ROWS[lane:lane + 1, :], AF.Exp, scale=-0.5,
                 bias=CB[lane:lane + 1, 0:1], inc=[("sACT", 1)])
            st_[("rstd", h)] = P.n("sACT")

        def srms_mul(h):
            lane = 32 + (h % 2) * 32
            ptb = PP.acquire("pe", UBC)
            P.wait("pe", "sACT", st_[("rstd", h)])
            P.op("pe", "matmul", ptb[:, 0:512], onesb[lane:lane + 1, 0:128],
                 ROWSB[lane:lane + 1, 0:512], start=True, stop=True)
            P.op("pe", "matmul", ptb[:, 512:1024], onesb[lane:lane + 1, 0:128],
                 ROWSB[lane:lane + 1, 512:1024], start=True, stop=True,
                 inc=[("sPE", 1)])
            pv3 = P.n("sPE")
            P.wait("dve", "sPE", pv3)
            P.wait("dve", "sDVE", st_.get(("tt2", h - 1), 0))
            P.op("dve", "tensor_mul", XB[2][:, :], S32[h % 2][:, :], ptb[:, :],
                 inc=[("sDVE", 1)])
            st_[("tt1", h)] = P.n("sDVE")
            PP.release(UBC, "sDVE", st_[("tt1", h)])
            P.wait("dve", "sDVE", st_[("tt1", h)])
            P.op("dve", "tensor_mul", B[h][:, :], XB[2][:, :], B[h][:, :],
                 inc=[("sDVE", 1)])
            st_[("tt2", h)] = P.n("sDVE")

        for h in range(L0_HEADS + 2):
            if h < L0_HEADS:
                srms_o(h)
            if 1 <= h <= L0_HEADS:
                srms_stats(h - 1)
            if h >= 2:
                srms_mul(h - 2)

        DVE_O = P.n("sDVE")
        P.wait("pe", "sDVE", DVE_O)
        P.wait("dve", "sX", X_READY)

        def evo(m, pt, pv):
            P.wait("dve", "sPE", pv)
            P.op("dve", "tensor_add", R[m][:, 0:512], R[m][:, 0:512], pt[:, 0:512])
            P.op("dve", "tensor_add", R[m][:, 512:1024], R[m][:, 512:1024],
                 pt[:, 512:1024], inc=[("sDVE", 1)])
            val = P.n("sDVE")
            ln_feed(m, "sDVE", val, pv)
            return ("sDVE", val)

        ln_begin(A[0], A[1])
        proj(B, o_sids, KC, evo)
        ln_finish()

    LNC = {"pend": []}

    def ln_begin(sq0, sq1):
        u = next_unit()
        pt = PP.acquire("pe", u)
        u2 = next_unit()
        pt2 = PP.acquire("pe", u2)
        state["avoid"] = {u, u2}
        LNC.update(u=u, u2=u2, pt=pt, pt2=pt2, sq=(sq0, sq1), cs={}, pend=[])

    def ln_chunk(m, rsem, rval, xbfree):
        pt, pt2 = LNC["pt"], LNC["pt2"]
        sq = LNC["sq"][m % 2]
        P.wait("act", rsem, rval)
        P.wait("act", "sPE", xbfree)
        P.op("act", "activation", XB[m][:, :], R[m][:, :], AF.Copy,
             inc=[("sACT", 1)])
        av = P.n("sACT")
        P.wait("dve", "sACT", av)
        P.wait("dve", "sPE", LNC["cs"].get(m - 2, 0))
        P.op("dve", "tensor_mul", sq[:, :], XB[m][:, :], XB[m][:, :],
             inc=[("sDVE", 1)])
        dv = P.n("sDVE")
        P.wait("pe", "sACT", av)
        for hh in range(2):
            n0 = hh * 512
            P.op("pe", "matmul", pt[32:33, n0:n0 + 512], onesb[:, 0:1],
                 XB[m][:, n0:n0 + 512], start=(m == 0), stop=(m == KC - 1))
        P.wait("pe", "sDVE", dv)
        for hh in range(2):
            n0 = hh * 512
            P.op("pe", "matmul", pt2[32:33, n0:n0 + 512], onesb[:, 0:1],
                 sq[:, n0:n0 + 512], start=(m == 0), stop=(m == KC - 1),
                 inc=[("sPE", 1)] if (m == KC - 1 and hh == 1) else ())
        LNC["cs"][m] = P.n("sPE")

    def ln_feed(m, rsem, rval, xbfree):
        LNC["pend"].append((m, rsem, rval, xbfree))
        if len(LNC["pend"]) > 1:
            ln_chunk(*LNC["pend"].pop(0))

    def ln_finish():
        for e_ in LNC["pend"]:
            ln_chunk(*e_)
        LNC["pend"] = []
        state["avoid"] = set()
        u, u2, pt, pt2 = LNC["u"], LNC["u2"], LNC["pt"], LNC["pt2"]
        pv = P.n("sPE")
        P.wait("act", "sPE", pv)
        P.op("act", "activation", ROWS[32:33, :], pt[32:33, :], AF.Copy,
             scale=1.0 / 2048.0, inc=[("sACT", 1)])
        P.op("act", "activation", S32[0][32:33, :], pt2[32:33, :], AF.Copy,
             scale=1.0 / 2048.0, inc=[("sACT", 1)])
        av = P.n("sACT")
        PP.release(u, "sACT", av)
        PP.release(u2, "sACT", av)
        P.wait("dve", "sACT", av)
        P.wait("dve", "sDVE", P.n("sDVE"))
        P.op("dve", "tensor_mul", S32[1][32:33, :], ROWS[32:33, :],
             ROWS[32:33, :], inc=[("sDVE", 1)])
        P.wait("dve", "sDVE", P.n("sDVE"))
        P.op("dve", "tensor_sub", S32[1][32:33, :], S32[0][32:33, :],
             S32[1][32:33, :], inc=[("sDVE", 1)])
        dv = P.n("sDVE")
        P.wait("act", "sDVE", dv)
        P.op("act", "activation", S32[1][32:33, :], S32[1][32:33, :], AF.Ln,
             bias=CB[32:33, 1:2], inc=[("sACT", 1)])
        P.wait("act", "sACT", P.n("sACT"))
        P.op("act", "activation", S32[1][32:33, :], S32[1][32:33, :], AF.Exp,
             scale=-0.5, inc=[("sACT", 1)])
        av2 = P.n("sACT")
        u3 = next_unit()
        ptm = PP.acquire("pe", u3)
        u4 = next_unit()
        ptr = PP.acquire("pe", u4)
        P.wait("pe", "sACT", av2)
        for hh in range(2):
            n0 = hh * 512
            P.op("pe", "matmul", ptm[:, n0:n0 + 512], onesf[32:33, :],
                 ROWS[32:33, n0:n0 + 512], start=True, stop=True)
            P.op("pe", "matmul", ptr[:, n0:n0 + 512], onesf[32:33, :],
                 S32[1][32:33, n0:n0 + 512], start=True, stop=True,
                 inc=[("sPE", 1)] if hh == 1 else ())
        pv2 = P.n("sPE")
        xbv = []
        P.wait("dve", "sPE", pv2)
        for m in range(KC):
            if m > 0:
                P.wait("dve", "sDVE", P.n("sDVE"))
            P.op("dve", "tensor_sub", S32[0][:, :], R[m][:, :], ptm[:, :],
                 inc=[("sDVE", 1)])
            P.wait("dve", "sDVE", P.n("sDVE"))
            P.op("dve", "tensor_mul", R[m][:, :], S32[0][:, :], ptr[:, :],
                 inc=[("sDVE", 1)])
            dv3 = P.n("sDVE")
            P.wait("act", "sDVE", dv3)
            P.op("act", "activation", XB[m][:, :], R[m][:, :], AF.Copy,
                 inc=[("sACT", 1)])
            xbv.append(P.n("sACT"))
        LN_XB["v"] = xbv
        PP.release(u3, "sDVE", P.n("sDVE"))
        PP.release(u4, "sDVE", P.n("sDVE"))

    def l0_ffn():
        P.wait("dve", "sDVE", P.n("sDVE"))
        P.wait("act", "sACT", P.n("sACT"))
        xbr = LN_XB["v"]
        sh_gu = []
        for m in range(MC_DS):
            sh_gu.append(WS.register(din["wsg_s"][m, :, :], 2048))
            sh_gu.append(WS.register(din["wsu_s"][m, :, :], 2048))
        sh_dn = []
        for m in range(KC):
            sh_dn.append((WS.register(din["wsd_s"][m, :, 0:2048], 2048), 16, 0))
            sh_dn.append((WS.register(din["wsd_s"][m, :, 2048:2816], 768), 6, 16))
        ex_gu, ex_dn = [], []
        for e in range(8):
            gu = []
            for m in range(MC_DE):
                gu.append(WS.register(din["weg_s"][e, m, :, :], 2048))
                gu.append(WS.register(din["weu_s"][e, m, :, :], 2048))
            ex_gu.append(gu)
            ex_dn.append([WS.register(din["wed_s"][e, m, :, 0:MC_DE * 128],
                                      MC_DE * 128) for m in range(KC)])

        u = next_unit()
        pt = PP.acquire("pe", u)
        for k in range(KC):
            if xbr is not None:
                P.wait("pe", "sACT", xbr[k])
            lhs = rtr[:, k * 8:(k + 1) * 8]
            for h in range(2):
                n0 = h * 512
                P.op("pe", "matmul", pt[0:8, n0:n0 + 512], lhs,
                     XB[k][:, n0:n0 + 512], start=(k == 0), stop=(k == KC - 1),
                     inc=[("sPE", 1)] if (k == KC - 1 and h == 1) else ())
        pv = P.n("sPE")
        P.wait("act", "sPE", pv)
        P.op("act", "activation", ROWS[0:8, :], pt[0:8, :], AF.Copy,
             inc=[("sACT", 1)])
        LOGV = P.n("sACT")
        PP.release(u, "sACT", LOGV)


        r_last = {}

        def gateup(gu_sids, mc, hpool, wb=None):
            for m in range(mc):
                bufg, sg, vg = WS.consume(gu_sids[2 * m])
                ug = next_unit()
                ptg = PP.acquire("pe", ug)
                P.wait("pe", sg, vg)
                for k in range(KC):
                    lhs = bufg[:, k * 128:(k + 1) * 128]
                    for h in range(2):
                        n0 = h * 512
                        P.op("pe", "matmul", ptg[:, n0:n0 + 512], lhs,
                             XB[k][:, n0:n0 + 512], start=(k == 0),
                             stop=(k == KC - 1),
                             inc=[("sPE", 1)] if (k == KC - 1 and h == 1) else ())
                pvg = P.n("sPE")
                WS.done(gu_sids[2 * m], "sPE", pvg)
                bufu, su, vu = WS.consume(gu_sids[2 * m + 1])
                uu = next_unit()
                ptu = PP.acquire("pe", uu)
                P.wait("pe", su, vu)
                for k in range(KC):
                    lhs = bufu[:, k * 128:(k + 1) * 128]
                    for h in range(2):
                        n0 = h * 512
                        P.op("pe", "matmul", ptu[:, n0:n0 + 512], lhs,
                             XB[k][:, n0:n0 + 512], start=(k == 0),
                             stop=(k == KC - 1),
                             inc=[("sPE", 1)] if (k == KC - 1 and h == 1) else ())
                pvu = P.n("sPE")
                WS.done(gu_sids[2 * m + 1], "sPE", pvu)
                P.wait("act", "sPE", pvg)
                P.wait("act", "sDVE", r_last.get("gsil", 0))
                P.op("act", "activation", S32[0][:, :], ptg[:, :], AF.Silu,
                     inc=[("sACT", 1)])
                av = P.n("sACT")
                PP.release(ug, "sACT", av)
                P.wait("dve", "sACT", av)
                P.wait("dve", "sPE", pvu)
                if wb is None:
                    P.op("dve", "tensor_mul", hpool[m][:, :], ptu[:, :],
                         S32[0][:, :], inc=[("sDVE", 1)])
                else:
                    P.wait("dve", "sDVE", r_last.get("tt2", 0))
                    P.op("dve", "tensor_mul", wb[1][:, :], ptu[:, :],
                         S32[0][:, :], inc=[("sDVE", 1)])
                dv = P.n("sDVE")
                r_last["gsil"] = dv
                PP.release(uu, "sDVE", dv)
                if wb is not None:
                    P.wait("dve", "sDVE", dv)
                    P.op("dve", "tensor_mul", hpool[m][:, :], wb[1][:, :],
                         wb[0][:, :], inc=[("sDVE", 1)])
                    r_last["tt2"] = P.n("sDVE")

        def down(dn_specs, hsrc, h_ready, feed=False):
            P.wait("pe", "sDVE", h_ready)
            for m, specs in enumerate(dn_specs):
                u = next_unit()
                pt = PP.acquire("pe", u)
                nsp = len(specs)
                for si, (sid, kcw, kr) in enumerate(specs):
                    buf, ws, wv = WS.consume(sid)
                    P.wait("pe", ws, wv)
                    for k in range(kcw):
                        lhs = buf[:, k * 128:(k + 1) * 128]
                        for h in range(2):
                            n0 = h * 512
                            P.op("pe", "matmul", pt[:, n0:n0 + 512], lhs,
                                 hsrc[kr + k][:, n0:n0 + 512],
                                 start=(si == 0 and k == 0),
                                 stop=(si == nsp - 1 and k == kcw - 1),
                                 inc=[("sPE", 1)] if (k == kcw - 1 and h == 1)
                                 else ())
                    WS.done(sid, "sPE", P.n("sPE"))
                pv = P.n("sPE")
                P.wait("dve", "sPE", pv)
                P.wait("dve", "sDVE", r_last.get(m, 0))
                P.op("dve", "tensor_add", R[m][:, 0:512], R[m][:, 0:512],
                     pt[:, 0:512])
                P.op("dve", "tensor_add", R[m][:, 512:1024], R[m][:, 512:1024],
                     pt[:, 512:1024], inc=[("sDVE", 1)])
                r_last[m] = P.n("sDVE")
                if feed:
                    ln_feed(m, "sDVE", r_last[m], pv)
                PP.release(u, "sDVE", r_last[m])

        hsh = B[0:16] + A[0:6]
        gateup(sh_gu, MC_DS, hsh)
        for t in range(8):
            cols = slice(t * 128, (t + 1) * 128)
            u = next_unit()
            pt = PP.acquire("pe", u)
            P.wait("pe", "sACT", LOGV)
            if t > 0:
                P.wait("pe", "sDVE", P.n("sDVE"))
            P.op("pe", "transpose", pt[:, 0:8], ROWS[0:8, cols],
                 ident[0:8, 0:8], inc=[("sPE", 1)])
            pv = P.n("sPE")
            E = SMX
            W = SMX[:, 32:48]
            P.wait("dve", "sPE", pv)
            P.wait("dve", "sACT", P.n("sACT"))
            P.op("dve", "reduce_max", W[:, 8:9], pt[:, 0:8], axis=AX.X,
                 negate=True, inc=[("sDVE", 1)])
            dv = P.n("sDVE")
            P.wait("act", "sDVE", dv)
            P.op("act", "activation", E[:, 0:8], pt[:, 0:8], AF.Exp,
                 bias=W[:, 8:9], inc=[("sACT", 1)])
            av = P.n("sACT")
            PP.release(u, "sACT", av)
            P.wait("dve", "sACT", av)
            P.op("dve", "reduce_sum", W[:, 9:10], E[:, 0:8], axis=AX.X,
                 inc=[("sDVE", 1)])
            dv = P.n("sDVE")
            P.wait("act", "sDVE", dv)
            P.op("act", "activation", W[:, 10:11], W[:, 9:10], AF.Ln,
                 inc=[("sACT", 1)])
            P.wait("act", "sACT", P.n("sACT"))
            P.op("act", "activation", W[:, 10:11], W[:, 10:11], AF.Exp,
                 scale=-1.0, inc=[("sACT", 1)])
            av = P.n("sACT")
            P.wait("dve", "sACT", av)
            P.op("dve", "tensor_single_scalar", E[:, 24:32], E[:, 0:8],
                 W[:, 10:11], ALU.mult, inc=[("sDVE", 1)])
            P.wait("dve", "sDVE", P.n("sDVE"))
            P.op("dve", "reduce_max", W[:, 11:12], E[:, 24:32], axis=AX.X,
                 inc=[("sDVE", 1)])
            P.wait("dve", "sDVE", P.n("sDVE"))
            P.op("dve", "tensor_single_scalar", E[:, 8:16], E[:, 24:32],
                 W[:, 11:12], ALU.is_ge, inc=[("sDVE", 1)])
            P.wait("dve", "sDVE", P.n("sDVE"))
            P.op("dve", "tensor_scalar", E[:, 8:16], E[:, 8:16], -1.0, 1.0,
                 ALU.mult, ALU.add, inc=[("sDVE", 1)])
            P.wait("dve", "sDVE", P.n("sDVE"))
            P.op("dve", "tensor_mul", E[:, 16:24], E[:, 24:32], E[:, 8:16],
                 inc=[("sDVE", 1)])
            P.wait("dve", "sDVE", P.n("sDVE"))
            P.op("dve", "reduce_max", W[:, 12:13], E[:, 16:24], axis=AX.X,
                 inc=[("sDVE", 1)])
            P.wait("dve", "sDVE", P.n("sDVE"))
            P.op("dve", "tensor_single_scalar", E[:, 8:16], E[:, 24:32],
                 W[:, 12:13], ALU.is_ge, inc=[("sDVE", 1)])
            P.wait("dve", "sDVE", P.n("sDVE"))
            P.op("dve", "tensor_mul", W[:, 0:8], E[:, 24:32], E[:, 8:16],
                 inc=[("sDVE", 1)])
            dv = P.n("sDVE")
            u = next_unit()
            pt = PP.acquire("pe", u)
            P.wait("pe", "sDVE", dv)
            P.op("pe", "transpose", pt[0:8, 0:128], W[:, 0:8], ident[:, :],
                 inc=[("sPE", 1)])
            pv = P.n("sPE")
            P.wait("act", "sPE", pv)
            P.op("act", "activation", ROWSB[0:8, cols], pt[0:8, 0:128], AF.Copy,
                 inc=[("sACT", 1)])
            PP.release(u, "sACT", P.n("sACT"))
        WRDY = P.n("sACT")
        down([[sh_dn[2 * m], sh_dn[2 * m + 1]] for m in range(KC)], hsh,
             P.n("sDVE"))

        for e in range(8):
            pool = A if e % 2 == 0 else B
            u = next_unit()
            pt = PP.acquire("pe", u)
            P.wait("pe", "sACT", WRDY)
            for h in range(2):
                n0 = h * 512
                P.op("pe", "matmul", pt[:, n0:n0 + 512],
                     ind8[:, e * 128:(e + 1) * 128], ROWSB[0:8, n0:n0 + 512],
                     start=True, stop=True,
                     inc=[("sPE", 1)] if h == 1 else ())
            pv = P.n("sPE")
            P.wait("act", "sPE", pv)
            P.op("act", "activation", pool[15][:, :], pt[:, :], AF.Copy,
                 inc=[("sACT", 1)])
            av = P.n("sACT")
            PP.release(u, "sACT", av)
            P.wait("dve", "sACT", av)
            gateup(ex_gu[e], MC_DE, pool, wb=(pool[15], pool[12]))
            if e == 7:
                ln_begin(B[13], B[14])
            down([[(ex_dn[e][m], MC_DE, 0)] for m in range(KC)], pool,
                 P.n("sDVE"), feed=(e == 7))

    def l1_attn():
        P.wait("dve", "sDVE", P.n("sDVE"))
        P.wait("act", "sACT", P.n("sACT"))
        P.wait("act", "sDVE", P.n("sDVE"))
        xbr = LN_XB["v"]
        rl = {}
        k_sid = WS.register(din["l1k_s"][:, :], KC * 64)
        q_sids = [WS.register(din["l1q_s"][m, :, :], 2048) for m in range(KC)]
        d_sids = [WS.register(din["l1d_s"][m, :, :], 2048) for m in range(KC)]

        VW = B[2]
        P.wait("gp", "sPE", P.n("sPE"))
        vsrc = bass.AP(din["l1v_r"], 0, [[64, 128], [128 * 64, KC], [1, 64]])
        P.op("gp", "dma_start", out=VW[:, 0:KC * 64], in_=vsrc, inc=[("sGP", 16)])
        VW_GP = P.n("sGP")
        uv = next_unit()
        ptv = PP.acquire("pe", uv)
        P.wait("pe", "sGP", VW_GP)
        for t in range(8):
            for k in range(KC):
                if t == 0 and xbr is not None:
                    P.wait("pe", "sACT", xbr[k])
                P.op("pe", "matmul", ptv[:, t * 64:(t + 1) * 64],
                     XB[k][:, t * 128:(t + 1) * 128], VW[:, k * 64:(k + 1) * 64],
                     start=(k == 0), stop=(k == KC - 1),
                     inc=[("sPE", 1)] if (t == 7 and k == KC - 1) else ())
        pv = P.n("sPE")
        P.wait("act", "sPE", pv)
        P.op("act", "activation", B[1][:, 0:512], ptv[:, 0:512], AF.Copy,
             inc=[("sACT", 1)])
        av = P.n("sACT")
        PP.release(uv, "sACT", av)
        P.wait("sp", "sACT", av)
        P.op("sp", "dma_start", out=l1kv_loc[1].rearrange("p f -> (p f)").rearrange(
            "(p f) -> p f", p=128), in_=B[1][:, 0:512], inc=[("sH", 16)])

        uk = next_unit()
        ptk = PP.acquire("pe", uk)
        bufk, ws, wv = WS.consume(k_sid)
        P.wait("pe", ws, wv)
        for k in range(KC):
            lhs = bufk[:, k * 64:(k + 1) * 64]
            for h in range(2):
                n0 = h * 512
                P.op("pe", "matmul", ptk[0:64, n0:n0 + 512], lhs,
                     XB[k][:, n0:n0 + 512], start=(k == 0), stop=(k == KC - 1),
                     inc=[("sPE", 1)] if (k == KC - 1 and h == 1) else ())
        pv = P.n("sPE")
        WS.done(k_sid, "sPE", pv)

        def rope(rows, src_ps, u_src, dst):
            P.wait("act", "sPE", P.n("sPE"))
            P.wait("act", "sDVE", rl.get("ropeS1", 0))
            P.op("act", "activation", S32[0][rows, :], src_ps[rows, :], AF.Copy,
                 inc=[("sACT", 1)])
            av = P.n("sACT")
            PP.release(u_src, "sACT", av)
            up = next_unit()
            ptp = PP.acquire("pe", up)
            P.wait("pe", "sACT", av)
            n = rows.stop - rows.start
            for h in range(2):
                n0 = h * 512
                P.op("pe", "matmul", ptp[rows, n0:n0 + 512],
                     perm[rows, 0:n], S32[0][rows, n0:n0 + 512],
                     start=True, stop=True,
                     inc=[("sPE", 1)] if h == 1 else ())
            pvp = P.n("sPE")
            P.wait("dve", "sACT", av)
            P.wait("dve", "sDVE", rl.get("ropeS1", 0))
            P.op("dve", "tensor_mul", S32[1][rows, :], S32[0][rows, :],
                 cos_t[rows, :], inc=[("sDVE", 1)])
            P.wait("dve", "sDVE", P.n("sDVE"))
            P.wait("dve", "sPE", pvp)
            P.op("dve", "tensor_mul", S32[0][rows, :], ptp[rows, :],
                 sin_t[rows, :], inc=[("sDVE", 1)])
            rl["ropeS0"] = P.n("sDVE")
            PP.release(up, "sDVE", rl["ropeS0"])
            P.wait("dve", "sDVE", rl["ropeS0"])
            P.op("dve", "tensor_add", dst, S32[1][rows, :], S32[0][rows, :],
                 inc=[("sDVE", 1)])
            rl["ropeS1"] = P.n("sDVE")

        rope(slice(0, 64), ptk, uk, B[0][0:64, :])
        P.wait("sp", "sDVE", rl["ropeS1"])
        P.op("sp", "dma_start", out=l1kv_loc[0].rearrange("p f -> (p f)").rearrange(
            "(p f) -> p f", p=64), in_=B[0][0:64, :], inc=[("sH", 16)])
        GP_KV = P.n("sH")

        groups = [[2 * i, 2 * i + 1] for i in range(max(1, ncores // 2))]
        P.wait("gp", "sH", GP_KV)
        P.op("gp", "collective_compute", "AllGather", ALU.bypass,
             replica_groups=groups, ins=[l1kv_loc.ap().opt()],
             outs=[l1kv_all.ap().opt()], inc=[("sCC", 1)])
        CCV = P.n("sCC")

        KH = [B[4], B[5]]
        VT = [B[6], B[7]]
        PE0 = P.n("sPE")
        P.wait("sp", "sCC", CCV)
        P.wait("sp", "sPE", PE0)
        P.wait("gp", "sPE", PE0)
        for g in range(KC):
            rk = g % 2
            lt = g // 2
            src = l1kv_all[rk, 0].rearrange("p f -> (p f)").rearrange(
                "(p f) -> p f", p=64)[:, lt * 128:(lt + 1) * 128]
            dst = KH[g // 8]
            c0 = (g % 8) * 128
            P.op("sp", "dma_start", out=dst[0:64, c0:c0 + 128], in_=src,
                 inc=[("sH", 16)])
            P.op("sp", "dma_start", out=dst[64:128, c0:c0 + 128], in_=src,
                 inc=[("sH", 16)])
            srcv = l1kv_all[rk, 1].rearrange("p f -> (p f)").rearrange(
                "(p f) -> p f", p=128)[:, lt * 64:(lt + 1) * 64]
            dv_ = VT[g // 8]
            cv = (g % 8) * 65
            P.op("sp", "dma_start", out=dv_[:, cv:cv + 64], in_=srcv,
                 inc=[("sH", 16)])
            P.op("gp", "memset", VT[g // 8][:, cv + 64:cv + 65], 1.0,
                 inc=[("sGPC", 1)])
        KV_GP = P.n("sH")
        KV_GPC = P.n("sGPC")

        for m in range(KC):
            buf, ws, wv = WS.consume(q_sids[m])
            u = next_unit()
            pt = PP.acquire("pe", u)
            P.wait("pe", ws, wv)
            for k in range(KC):
                lhs = buf[:, k * 128:(k + 1) * 128]
                for h in range(2):
                    n0 = h * 512
                    P.op("pe", "matmul", pt[:, n0:n0 + 512], lhs,
                         XB[k][:, n0:n0 + 512], start=(k == 0),
                         stop=(k == KC - 1),
                         inc=[("sPE", 1)] if (k == KC - 1 and h == 1) else ())
            pvq = P.n("sPE")
            WS.done(q_sids[m], "sPE", pvq)
            rope(slice(0, 128), pt, u, A[m][:, :])
        Q_DVE = P.n("sDVE")

        P.wait("pe", "sH", KV_GP)
        P.wait("pe", "sGPC", KV_GPC)
        P.wait("pe", "sDVE", Q_DVE)
        P.wait("act", "sACT", P.n("sACT"))
        P.wait("act", "sDVE", P.n("sDVE"))
        UC, UD = 2, 3
        NPB = 3
        PSB = [B[8], B[9], B[10]]
        exp_wait = {0: 0, 1: 0, 2: 0}
        for h in range(L1_NH):
            base = (h % 2) * 64
            rows = slice(base, base + 64)
            dl = 64 - (h % 2) * 32
            qch = A[h // 2]
            ptc = PP.acquire("pe", UC)
            ptd = PP.acquire("pe", UD)
            pend = []

            def emit_ctx(g, segs, last):
                vt65 = VT[g // 8][:, (g % 8) * 65:(g % 8) * 65 + 65]
                ocol = vt65[:, 64:65]
                psb = PSB[g % NPB]
                P.wait("pe", "sDVE", rl[("mask", g % NPB)])
                for i, (s0, s1) in enumerate(segs):
                    lastg = (g == (7 if s0 < 512 else KC - 1))
                    if h % 2 == 0:
                        P.op("pe", "matmul", ptc[0:65, s0:s1], vt65,
                             psb[:, s0:s1], start=(g == 0), stop=lastg,
                             inc=[("sPE", 1)] if i == len(segs) - 1 else ())
                    else:
                        P.op("pe", "matmul", ptc[rows, s0:s1], vt65[:, 0:64],
                             psb[:, s0:s1], start=(g == 0), stop=lastg)
                        P.op("pe", "matmul", ptd[dl:dl + 1, s0:s1], ocol,
                             psb[:, s0:s1], start=(g == 0), stop=lastg,
                             inc=[("sPE", 1)] if i == len(segs) - 1 else ())
                exp_wait[g % NPB] = P.n("sPE")

            for g in range(KC):
                qs = 128 * (g // 2)
                us = g % 2
                pts = PP.acquire("pe", us)
                segs = []
                s0 = qs
                while s0 < 1024:
                    s1 = min(1024, (s0 // 512 + 1) * 512)
                    segs.append((s0, s1))
                    s0 = s1
                klhs = KH[g // 8][rows, (g % 8) * 128:(g % 8) * 128 + 128]
                for i, (s0, s1) in enumerate(segs):
                    P.op("pe", "matmul", pts[:, s0:s1], klhs, qch[rows, s0:s1],
                         start=True, stop=True,
                         inc=[("sPE", 1)] if i == len(segs) - 1 else ())
                pvs = P.n("sPE")
                pb = g % NPB
                psb = PSB[pb]
                P.wait("act", "sPE", max(pvs, exp_wait[pb]))
                P.wait("act", "sDVE", rl.get(("mask", pb), 0))
                P.op("act", "activation", psb[:, qs:1024], pts[:, qs:1024],
                     AF.Exp, scale=0.125, inc=[("sACT", 1)])
                ave = P.n("sACT")
                PP.release(us, "sACT", ave)
                P.wait("dve", "sACT", ave)
                P.wait("dve", "sPE", exp_wait[pb])
                P.op("dve", "tensor_mul", psb[:, qs:qs + 128],
                     psb[:, qs:qs + 128], dmask[:, g * 128:(g + 1) * 128],
                     inc=[("sDVE", 1)])
                rl[("mask", pb)] = P.n("sDVE")
                pend.append((g, segs))
                if len(pend) > 2:
                    emit_ctx(*pend.pop(0), last=False)
            for e_ in pend:
                emit_ctx(*e_, last=True)
            pend = []
            ctx_done_val = P.n("sPE")
            den_src = ptc if h % 2 == 0 else ptd
            P.wait("act", "sPE", ctx_done_val)
            P.wait("act", "sDVE", rl.get("rden", 0))
            P.op("act", "activation", S32[0][dl:dl + 1, :], den_src[dl:dl + 1, :],
                 AF.Ln, inc=[("sACT", 1)])
            P.wait("act", "sACT", P.n("sACT"))
            P.op("act", "activation", ROWSB[dl:dl + 1, :], S32[0][dl:dl + 1, :],
                 AF.Exp, scale=-1.0, inc=[("sACT", 1)])
            avd = P.n("sACT")
            PP.release(UD, "sACT", avd)
            ur = h % 2
            ptr_ = PP.acquire("pe", ur)
            P.wait("pe", "sACT", avd)
            for hh in range(2):
                n0 = hh * 512
                P.op("pe", "matmul", ptr_[rows, n0:n0 + 512],
                     onesb[dl:dl + 1, 0:64], ROWSB[dl:dl + 1, n0:n0 + 512],
                     start=True, stop=True, inc=[("sPE", 1)] if hh == 1 else ())
            pvr = P.n("sPE")
            P.wait("act", "sPE", pvr)
            P.op("act", "activation", S32[1][rows, :], ptr_[rows, :], AF.Copy,
                 inc=[("sACT", 1)])
            avr = P.n("sACT")
            PP.release(ur, "sACT", avr)
            P.wait("dve", "sACT", avr)
            P.wait("dve", "sPE", ctx_done_val)
            P.op("dve", "tensor_mul", qch[rows, :], ptc[rows, :], S32[1][rows, :],
                 inc=[("sDVE", 1)])
            rl["rden"] = P.n("sDVE")
            PP.release(UC, "sDVE", rl["rden"])

        CTX_DVE = P.n("sDVE")
        P.wait("pe", "sDVE", CTX_DVE)

        def evd(m, pt, pv):
            P.wait("dve", "sPE", pv)
            P.wait("dve", "sDVE", rl.get(("r", m), 0))
            P.op("dve", "tensor_add", R[m][:, 0:512], R[m][:, 0:512], pt[:, 0:512])
            P.op("dve", "tensor_add", R[m][:, 512:1024], R[m][:, 512:1024],
                 pt[:, 512:1024], inc=[("sDVE", 1)])
            rl[("r", m)] = P.n("sDVE")
            ln_feed(m, "sDVE", rl[("r", m)], pv)
            return ("sDVE", rl[("r", m)])

        ln_begin(B[12], B[13])
        proj(A, d_sids, KC, evd)
        ln_finish()

    def l1_ffn():
        P.wait("dve", "sDVE", P.n("sDVE"))
        P.wait("act", "sACT", P.n("sACT"))
        P.wait("act", "sDVE", P.n("sDVE"))
        xbr = LN_XB["v"]
        rl = {}
        P.op("sp", "dma_start", out=l1sc[:, 0:MC_DI], in_=din["l1gsc"][:, :],
             inc=[("sSP", 16)])
        P.op("sp", "dma_start", out=l1sc[:, MC_DI:2 * MC_DI],
             in_=din["l1usc"][:, :], inc=[("sSP", 16)])
        P.op("sp", "dma_start", out=l1sc[:, 2 * MC_DI:2 * MC_DI + KC],
             in_=din["l1dsc"][:, :], inc=[("sSP", 16)])
        SCV = P.n("sSP")
        P.wait("act", "sSP", SCV)
        P.wait("dve", "sSP", SCV)

        f8idx = [0]
        def reg8(src_ap, ncols):
            s = WS.register(src_ap, ncols, fp8=(F8, ("sF80", "sF81"), f8idx[0]))
            f8idx[0] += 1
            return s

        specs = []
        for kg in range(3):
            gu = []
            for mm_ in range(16):
                m = kg * 16 + mm_
                gu.append((m, reg8(din["l1g_s"][m, :, :], 2048),
                           reg8(din["l1u_s"][m, :, :], 2048)))
            dn = [reg8(din["l1dn_s"][m, :, kg * 2048:(kg + 1) * 2048], 2048)
                  for m in range(KC)]
            specs.append((gu, dn))

        for kg, (gu, dn) in enumerate(specs):
            for mm_, (m, sg, su) in enumerate(gu):
                bufg, ws, wv = WS.consume(sg)
                ug = next_unit()
                ptg = PP.acquire("pe", ug)
                P.wait("pe", ws, wv)
                for k in range(KC):
                    if kg == 0 and mm_ == 0 and xbr is not None:
                        P.wait("pe", "sACT", xbr[k])
                    lhs = bufg[:, k * 128:(k + 1) * 128]
                    for h in range(2):
                        n0 = h * 512
                        P.op("pe", "matmul", ptg[:, n0:n0 + 512], lhs,
                             XB[k][:, n0:n0 + 512], start=(k == 0),
                             stop=(k == KC - 1),
                             inc=[("sPE", 1)] if (k == KC - 1 and h == 1) else ())
                pvg = P.n("sPE")
                WS.done(sg, "sPE", pvg)
                bufu, ws, wv = WS.consume(su)
                uu = next_unit()
                ptu = PP.acquire("pe", uu)
                P.wait("pe", ws, wv)
                for k in range(KC):
                    lhs = bufu[:, k * 128:(k + 1) * 128]
                    for h in range(2):
                        n0 = h * 512
                        P.op("pe", "matmul", ptu[:, n0:n0 + 512], lhs,
                             XB[k][:, n0:n0 + 512], start=(k == 0),
                             stop=(k == KC - 1),
                             inc=[("sPE", 1)] if (k == KC - 1 and h == 1) else ())
                pvu = P.n("sPE")
                WS.done(su, "sPE", pvu)
                P.wait("act", "sPE", pvg)
                P.wait("act", "sDVE", rl.get("gsil", 0))
                P.op("act", "activation", S32[0][:, :], ptg[:, :], AF.Silu,
                     scale=l1sc[:, m:m + 1], inc=[("sACT", 1)])
                av = P.n("sACT")
                PP.release(ug, "sACT", av)
                P.wait("dve", "sACT", av)
                P.wait("dve", "sPE", pvu)
                P.op("dve", "scalar_tensor_tensor", B[mm_][:, :], ptu[:, :],
                     l1sc[:, MC_DI + m:MC_DI + m + 1], S32[0][:, :],
                     ALU.mult, ALU.mult, inc=[("sDVE", 1)])
                rl["gsil"] = P.n("sDVE")
                PP.release(uu, "sDVE", rl["gsil"])
            if kg == 2:
                ln_begin(A[0], A[1])
            H_DVE = P.n("sDVE")
            P.wait("pe", "sDVE", H_DVE)
            for m in range(KC):
                buf, ws, wv = WS.consume(dn[m])
                u = next_unit()
                pt = PP.acquire("pe", u)
                P.wait("pe", ws, wv)
                for k in range(16):
                    lhs = buf[:, k * 128:(k + 1) * 128]
                    for h in range(2):
                        n0 = h * 512
                        P.op("pe", "matmul", pt[:, n0:n0 + 512], lhs,
                             B[k][:, n0:n0 + 512], start=(k == 0),
                             stop=(k == 15),
                             inc=[("sPE", 1)] if (k == 15 and h == 1) else ())
                pv = P.n("sPE")
                WS.done(dn[m], "sPE", pv)
                P.wait("dve", "sPE", pv)
                P.wait("dve", "sDVE", rl.get(("r", m), 0))
                dcol = l1sc[:, 2 * MC_DI + m:2 * MC_DI + m + 1]
                P.op("dve", "scalar_tensor_tensor", R[m][:, 0:512],
                     pt[:, 0:512], dcol, R[m][:, 0:512], ALU.mult, ALU.add)
                P.op("dve", "scalar_tensor_tensor", R[m][:, 512:1024],
                     pt[:, 512:1024], dcol, R[m][:, 512:1024], ALU.mult,
                     ALU.add, inc=[("sDVE", 1)])
                rl[("r", m)] = P.n("sDVE")
                if kg == 2:
                    ln_feed(m, "sDVE", rl[("r", m)], pv)
                PP.release(u, "sDVE", rl[("r", m)])

    l0_attn()
    if stop_after != "l0attn":
        l0_ffn()
        ln_finish()
    if stop_after not in ("l0attn", "l0ffn"):
        l1_attn()
    if stop_after is None:
        l1_ffn()
        ln_finish()

    P.wait("sp", "sACT", P.n("sACT"))
    P.wait("sp", "sDVE", P.n("sDVE"))
    ydst = bass.AP(dy, 0, [[T, 128], [128 * T, KC], [1, T]])
    P.op("sp", "dma_start", out=ydst, in_=RALL[:, :], inc=[("sOUT", 16)])
    P.wait("sp", "sOUT", P.n("sOUT"))

    with nc.Block() as block:
        @block.gpsimd
        def _(g):
            P.emit("gp", g)

        @block.tensor
        def _(t):
            P.emit("pe", t)

        @block.vector
        def _(v):
            P.emit("dve", v)

        @block.scalar
        def _(s):
            P.emit("act", s)

        @block.sync
        def _(s):
            P.emit("sp", s)

    return nc


def kernel(**inputs):
    in_maps = prepare_inputs(inputs, NCORES)
    nc = build_nc(NCORES)
    res = run_bass_kernel_spmd(nc, in_maps, core_ids=list(range(NCORES)))
    return gather_output(res.results, NCORES)


# version 07
# speedup vs baseline: 1.1301x; 1.0870x over previous
import math

import numpy as np

import concourse.bass as bass
import concourse.mybir as mybir
from concourse.bass_utils import run_bass_kernel_spmd

F32 = mybir.dt.float32
BF16 = mybir.dt.bfloat16
FP8 = mybir.dt.float8e4
AF = mybir.ActivationFunctionType
ALU = mybir.AluOpType
AX = mybir.AxisListType

D = 2048
KC = 16
T = 1024
NCORES = 8
L0_HEADS = 16
L0_SCALE = float(np.sqrt(128.0))
L1_NH, L1_HD = 32, 64
DS = 2816
DE = 1408
DI = 6144
MC_DS = DS // 128
MC_DE = DE // 128
MC_DI = DI // 128

STOP_AFTER = None



def _mstripes(w_io):
    K, M = w_io.shape
    kc, mc = K // 128, M // 128
    t = w_io.reshape(kc, 128, mc, 128).transpose(2, 1, 0, 3).reshape(mc, 128, kc * 128)
    return np.ascontiguousarray(t)


def _rowstripes(w_io):
    K, M = w_io.shape
    return np.ascontiguousarray(w_io.reshape(K // 128, 128, M))


def _rope_tables(positions):
    import jax
    import jax.numpy as jnp
    with jax.default_device(jax.devices("cpu")[0]):
        inv_freq = 1.0 / 10000 ** (jnp.arange(0, L1_HD, 2, dtype=jnp.float32) / L1_HD)
        pos = jnp.asarray(positions, dtype=jnp.float32)
        freqs = pos[:, None] * inv_freq[None, :]
        cos = np.asarray(jnp.cos(freqs)).T
        sin = np.asarray(jnp.sin(freqs)).T
    cos64 = np.concatenate([cos, cos], axis=0)
    sin64 = np.concatenate([-sin, sin], axis=0)
    cos128 = np.concatenate([cos64, cos64], axis=0)
    sin128 = np.concatenate([sin64, sin64], axis=0)
    return np.ascontiguousarray(cos128), np.ascontiguousarray(sin128)


def _consts():
    ident = np.eye(128, dtype=np.float32)
    perm = np.zeros((128, 128), dtype=np.float32)
    for m in range(128):
        p = (m // 64) * 64 + ((m % 64) + 32) % 64
        perm[p, m] = 1.0
    tri = np.triu(np.ones((128, 128), dtype=np.float32)).astype(np.float32)
    ones = np.ones((128, 128), dtype=np.float32)
    ind8 = np.zeros((8, 8 * 128), dtype=np.float32)
    for e in range(8):
        ind8[e, e * 128:(e + 1) * 128] = 1.0
    return ident, perm, tri, ones, ind8


def prepare_inputs(inputs, ncores=NCORES):
    f = {k: np.asarray(v) for k, v in inputs.items()}
    shared = {}
    shared["wq_s"] = _mstripes(np.ascontiguousarray(f["L0_attn_wq"].T))
    shared["wu_s"] = _mstripes(np.ascontiguousarray(f["L0_attn_wu"].T))
    shared["wo_s"] = _mstripes(np.ascontiguousarray(f["L0_attn_wo"].T))
    shared["wk_r"] = _rowstripes(np.ascontiguousarray(f["L0_attn_wk"].T))
    shared["wv_r"] = _rowstripes(np.ascontiguousarray(f["L0_attn_wv"].T))
    shared["wrtr"] = np.ascontiguousarray(
        f["L0_ffn_W_router"].reshape(KC, 128, 8).transpose(1, 0, 2).reshape(128, KC * 8))
    shared["wsg_s"] = _mstripes(f["L0_ffn_Ws_gate"])
    shared["wsu_s"] = _mstripes(f["L0_ffn_Ws_up"])
    shared["wsd_s"] = _mstripes(f["L0_ffn_Ws_down"])
    shared["weg_s"] = np.ascontiguousarray(
        np.stack([_mstripes(f["L0_ffn_We_gate"][e]) for e in range(8)]))
    shared["weu_s"] = np.ascontiguousarray(
        np.stack([_mstripes(f["L0_ffn_We_up"][e]) for e in range(8)]))
    shared["wed_s"] = np.ascontiguousarray(
        np.stack([_mstripes(f["L0_ffn_We_down"][e]) for e in range(8)]))
    wqkv = f["L1_attn_w_qkv"]
    shared["l1q_s"] = _mstripes(np.ascontiguousarray(wqkv[:, 0:2048]))
    shared["l1k_s"] = np.ascontiguousarray(
        wqkv[:, 2048:2112].reshape(KC, 128, 64).transpose(1, 0, 2).reshape(128, KC * 64))
    shared["l1v_r"] = _rowstripes(np.ascontiguousarray(wqkv[:, 2112:2176]))
    shared["l1d_s"] = _mstripes(f["L1_attn_w_dense"])
    shared["l1g_s"] = _mstripes(f["L1_ffn_gate_w"])
    shared["l1u_s"] = _mstripes(f["L1_ffn_up_w"])
    shared["l1dn_s"] = _mstripes(f["L1_ffn_down_w"])
    shared["l1gsc"] = np.ascontiguousarray(f["L1_ffn_gate_scale"].reshape(MC_DI, 128).T)
    shared["l1usc"] = np.ascontiguousarray(f["L1_ffn_up_scale"].reshape(MC_DI, 128).T)
    shared["l1dsc"] = np.ascontiguousarray(f["L1_ffn_down_scale"].reshape(KC, 128).T)
    ident, perm, tri, ones, ind8 = _consts()
    cb = np.zeros((128, 4), dtype=np.float32)
    cb[:, 0] = 0.5 * math.log(128.0)
    cb[:, 1] = 1e-5
    shared["c_bias"] = cb
    shared["c_ident"] = ident
    shared["c_perm"] = perm
    shared["c_tri"] = tri
    shared["c_ones"] = ones
    shared["c_ind8"] = ind8

    x = f["L0_attn_x"]
    in_maps = []
    for c in range(ncores):
        b, par = c // 2, c % 2
        blocks = [par + 2 * j for j in range(8)]
        tok = np.concatenate([np.arange(g * 128, (g + 1) * 128) for g in blocks])
        m = dict(shared)
        m["x"] = np.ascontiguousarray(x[b, tok, :].T)
        cosq, sinq = _rope_tables(tok)
        m["rope_cos"] = cosq
        m["rope_sin"] = sinq
        dm = np.zeros((128, KC * 128), dtype=np.float32)
        tri = np.triu(np.ones((128, 128), dtype=np.float32))
        for g in range(KC):
            gb = par + 2 * (g // 2)
            if gb == g:
                dm[:, g * 128:(g + 1) * 128] = tri
            elif gb > g:
                dm[:, g * 128:(g + 1) * 128] = 1.0
        m["c_dmask"] = dm
        in_maps.append(m)
    return in_maps


def gather_output(results, ncores=NCORES):
    out = np.empty((4, 2048, 2048), dtype=np.float32)
    for c in range(ncores):
        b, par = c // 2, c % 2
        blocks = [par + 2 * j for j in range(8)]
        tok = np.concatenate([np.arange(g * 128, (g + 1) * 128) for g in blocks])
        out[b, tok, :] = results[c]["y"].T
    return out



ENGS = ("gp", "pe", "dve", "act", "sp")


class Plan:
    def __init__(self, nc):
        self.nc = nc
        self.q = {e: [] for e in ENGS}
        self.cnt = {}
        self.sems = {}

    def sem(self, name):
        if name not in self.sems:
            self.sems[name] = self.nc.alloc_semaphore(name=name)
            self.cnt[name] = 0
        return name

    def wait(self, eng, s, v):
        if v > 0:
            self.q[eng].append(("w", s, int(v)))

    def op(self, eng, fn, *args, inc=(), **kw):
        for s, k in inc:
            self.cnt[s] += k
        self.q[eng].append(("o", fn, args, kw, tuple(inc)))

    def n(self, s):
        return self.cnt.get(s, 0)

    def emit(self, eng, engine):
        for item in self.q[eng]:
            if item[0] == "w":
                engine.wait_ge(self.sems[item[1]], item[2])
            else:
                _, fn, args, kw, inc = item
                inst = getattr(engine, fn)(*args, **kw)
                for s, k in inc:
                    inst = inst.then_inc(self.sems[s], k)


class PsumPool:

    def __init__(self, plan, tensors):
        self.p = plan
        self.t = tensors
        self.free_at = [None] * len(tensors)

    def acquire(self, eng, u):
        fa = self.free_at[u]
        if fa is not None:
            self.p.wait(eng, fa[0], fa[1])
        return self.t[u]

    def release(self, u, sem, val):
        self.free_at[u] = (sem, val)


class WStream:

    NB = 3
    LA = 2

    def __init__(self, plan, bufs):
        self.p = plan
        self.bufs = bufs
        self.sems = [plan.sem(f"sW{j}") for j in range(self.NB)]
        self.stripes = []
        self.loaded = 0
        self.consumed_at = {}
        self.f8_done = {}
        self.nexti = 0

    def register(self, src_ap, ncols, fp8=None):
        self.stripes.append((src_ap, ncols, fp8))
        return len(self.stripes) - 1

    def _plan_load(self):
        k = self.loaded
        if k >= len(self.stripes):
            return
        src, ncols, fp8 = self.stripes[k]
        b = k % self.NB
        if fp8 is not None:
            f8bufs, f8sems, f8idx = fp8
            j = f8idx % 2
            prev = self.f8_done.get(f8idx - 2)
            if prev is not None:
                self.p.wait("gp", prev[0], prev[1])
            self.p.op("gp", "dma_start", out=f8bufs[j][:, 0:ncols], in_=src,
                      inc=[(f8sems[j], 16)])
            self.p.wait("gp", f8sems[j], self.p.n(f8sems[j]))
            if k >= self.NB:
                dep = self.consumed_at.get(k - self.NB)
                assert dep is not None
                self.p.wait("gp", dep[0], dep[1])
            self.p.op("gp", "dma_start", out=self.bufs[b][:, 0:ncols],
                      in_=f8bufs[j][:, 0:ncols], inc=[(self.sems[b], 16)])
            self.f8_done[f8idx] = (self.sems[b], 16 * (k // self.NB + 1))
        else:
            if k >= self.NB:
                dep = self.consumed_at.get(k - self.NB)
                assert dep is not None, f"stripe {k} loads before {k - self.NB} consumed"
                self.p.wait("gp", dep[0], dep[1])
            self.p.op("gp", "dma_start", out=self.bufs[b][:, 0:ncols], in_=src,
                      inc=[(self.sems[b], 16)])
        self.loaded += 1

    def consume(self, k):
        while self.loaded <= min(k + self.LA, len(self.stripes) - 1):
            self._plan_load()
        b = k % self.NB
        return self.bufs[b], self.sems[b], 16 * (k // self.NB + 1)

    def done(self, k, sem, val):
        self.consumed_at[k] = (sem, val)



def build_nc(ncores=NCORES, stop_after=STOP_AFTER):
    nc = bass.Bass("TRN2", target_bir_lowering=False)

    dx = nc.dram_tensor("x", [D, T], F32, kind="ExternalInput")
    dy = nc.dram_tensor("y", [D, T], F32, kind="ExternalOutput")
    din = {}
    def di(name, shape):
        din[name] = nc.dram_tensor(name, shape, F32, kind="ExternalInput")
        return din[name]
    di("wq_s", [KC, 128, KC * 128]); di("wu_s", [KC, 128, KC * 128])
    di("wo_s", [KC, 128, KC * 128])
    di("wk_r", [KC, 128, D]); di("wv_r", [KC, 128, D])
    di("wrtr", [128, KC * 8])
    di("wsg_s", [MC_DS, 128, KC * 128]); di("wsu_s", [MC_DS, 128, KC * 128])
    di("wsd_s", [KC, 128, MC_DS * 128])
    di("weg_s", [8, MC_DE, 128, KC * 128]); di("weu_s", [8, MC_DE, 128, KC * 128])
    di("wed_s", [8, KC, 128, MC_DE * 128])
    di("l1q_s", [KC, 128, KC * 128]); di("l1k_s", [128, KC * 64])
    di("l1v_r", [KC, 128, 64]); di("l1d_s", [KC, 128, KC * 128])
    di("l1g_s", [MC_DI, 128, KC * 128]); di("l1u_s", [MC_DI, 128, KC * 128])
    di("l1dn_s", [KC, 128, MC_DI * 128])
    di("l1gsc", [128, MC_DI]); di("l1usc", [128, MC_DI]); di("l1dsc", [128, KC])
    di("c_bias", [128, 4])
    di("c_ident", [128, 128]); di("c_perm", [128, 128]); di("c_tri", [128, 128])
    di("c_ones", [128, 128]); di("c_ind8", [8, 8 * 128])
    di("rope_cos", [128, T]); di("rope_sin", [128, T])
    di("c_dmask", [128, KC * 128])

    kv_part = nc.dram_tensor("kv_part", [4, 128, 512], F32)
    kv_sum = nc.dram_tensor("kv_sum", [4, 128, 512], F32)
    l1kv_loc = nc.dram_tensor("l1kv_loc", [2, 64, T], BF16)
    l1kv_all = nc.dram_tensor("l1kv_all", [2, 2, 64, T], BF16)

    a = nc.alloc_sbuf_tensor
    RALL = a("RALL", [128, KC * T], F32)
    XBALL = a("XBALL", [128, KC * T], BF16)
    AALL = a("AALL", [128, KC * T], BF16)
    BALL = a("BALL", [128, KC * T], BF16)
    R = [RALL[:, i * T:(i + 1) * T] for i in range(KC)]
    XB = [XBALL[:, i * T:(i + 1) * T] for i in range(KC)]
    A = [AALL[:, i * T:(i + 1) * T] for i in range(KC)]
    B = [BALL[:, i * T:(i + 1) * T] for i in range(KC)]
    WB = [a(f"WB{j}", [128, 2048], BF16) for j in range(WStream.NB)]
    F8 = [a(f"F8{j}", [128, 2048], FP8) for j in range(2)]
    S32 = [a(f"S32{j}", [128, T], F32) for j in range(2)]
    ROWS = a("ROWS", [128, T], F32)
    ROWSB = a("ROWSB", [128, T], BF16)
    cos_t = a("cos_t", [128, T], F32)
    sin_t = a("sin_t", [128, T], F32)
    ident = a("ident", [128, 128], F32)
    perm = a("perm", [128, 128], F32)
    tri = a("tri", [128, 128], BF16)
    onesf = a("onesf", [128, 128], F32)
    onesb = a("onesb", [128, 128], BF16)
    CB = a("CB", [128, 4], F32)
    ind8 = a("ind8", [8, 8 * 128], BF16)
    rtr = a("rtr", [128, KC * 8], BF16)
    dmask = a("dmask", [128, KC * 128], BF16)
    SMX = a("SMX", [128, 48], F32)
    l1sc = a("l1sc", [128, 2 * MC_DI + KC], F32)
    psum = [nc.alloc_psum_tensor(f"PS{u}", [128, 1024], F32) for u in range(4)]

    P = Plan(nc)
    PP = PsumPool(P, psum)
    LN_XB = {"v": None}
    for s in ("sX", "sXG", "sPE", "sDVE", "sACT", "sGP", "sGPC", "sCC",
              "sOUT", "sSP", "sH", "sF80", "sF81"):
        P.sem(s)
    WS = WStream(P, WB)

    xsrc = bass.AP(dx, 0, [[T, 128], [128 * T, KC], [1, T]])
    P.op("sp", "dma_start", out=RALL[:, :], in_=xsrc, inc=[("sX", 16)])
    P.op("gp", "dma_start", out=XBALL[:, :], in_=xsrc, inc=[("sXG", 16)])
    for dst, src in ((cos_t, din["rope_cos"]), (sin_t, din["rope_sin"]),
                     (ident, din["c_ident"]), (perm, din["c_perm"]),
                     (onesf, din["c_ones"]), (CB, din["c_bias"])):
        P.op("sp", "dma_start", out=dst[:, :], in_=src[:, :], inc=[("sX", 16)])
    for dst, src in ((tri, din["c_tri"]), (onesb, din["c_ones"][:, :]),
                     (ind8, din["c_ind8"]), (rtr, din["wrtr"]),
                     (dmask, din["c_dmask"])):
        P.op("gp", "dma_start", out=dst[:, :], in_=src[:, :], inc=[("sXG", 16)])
    X_READY = P.n("sX")
    XG_READY = P.n("sXG")

    for e in ("pe", "dve", "act"):
        P.wait(e, "sX", X_READY)
        P.wait(e, "sXG", XG_READY)

    state = {"u": 0, "avoid": set()}

    def next_unit():
        while True:
            u = state["u"]
            state["u"] = (u + 1) % 4
            if u not in state["avoid"]:
                return u

    def proj(src_chunks, stripe_ids, kcw, evac, kr=0):
        for m, sid in enumerate(stripe_ids):
            buf, wsem, wval = WS.consume(sid)
            u = next_unit()
            pt = PP.acquire("pe", u)
            P.wait("pe", wsem, wval)
            for k in range(kcw):
                lhs = buf[:, k * 128:(k + 1) * 128]
                for h in range(2):
                    n0 = h * 512
                    P.op("pe", "matmul", pt[:, n0:n0 + 512], lhs,
                         src_chunks[kr + k][:, n0:n0 + 512],
                         start=(k == 0), stop=(k == kcw - 1),
                         inc=[("sPE", 1)] if (k == kcw - 1 and h == 1) else ())
            pv = P.n("sPE")
            WS.done(sid, "sPE", pv)
            sem, val = evac(m, pt, pv)
            PP.release(u, sem, val)

    def l0_attn():
        kv_sids = [[[WS.register(
                        bass.AP(w, 4 * k4 * 128 * 2048 + hg * 512,
                                [[2048, 128], [128 * 2048, 4], [1, 512]]), 2048)
                     for k4 in range(KC // 4)]
                    for w in (din["wk_r"], din["wv_r"])] for hg in range(4)]
        q_sids = [WS.register(din["wq_s"][m, :, :], 2048) for m in range(KC)]
        u_sids = [WS.register(din["wu_s"][m, :, :], 2048) for m in range(KC)]
        o_sids = [WS.register(din["wo_s"][m, :, :], 2048) for m in range(KC)]

        gp_kv_dma = 0
        for hg in range(4):
            CK = [B[t // 2][:, (t % 2) * 512:(t % 2 + 1) * 512] for t in range(8)]
            CV = [B[4 + t // 2][:, (t % 2) * 512:(t % 2 + 1) * 512] for t in range(8)]
            for which, CC, actf in ((0, CK, AF.Relu), (1, CV, AF.Copy)):
                units = [next_unit() for _ in range(4)]
                for uu in units:
                    PP.acquire("pe", uu)
                for k4 in range(KC // 4):
                    sid = kv_sids[hg][which][k4]
                    buf, wsem, wval = WS.consume(sid)
                    P.wait("pe", wsem, wval)
                    for kk in range(4):
                        k = 4 * k4 + kk
                        for t in range(8):
                            pt = psum[units[t % 4]]
                            half = (t // 4) * 512
                            P.op("pe", "matmul", pt[:, half:half + 512],
                                 XB[k][:, t * 128:(t + 1) * 128],
                                 buf[:, kk * 512:kk * 512 + 512],
                                 start=(k == 0), stop=(k == KC - 1),
                                 inc=[("sPE", 1)] if t == 7 else ())
                    WS.done(sid, "sPE", P.n("sPE"))
                pv = P.n("sPE")
                P.wait("act", "sPE", pv)
                for t in range(8):
                    pt = psum[units[t % 4]]
                    half = (t // 4) * 512
                    P.op("act", "activation", CC[t], pt[:, half:half + 512], actf,
                         scale=(1.0 / L0_SCALE) if which == 0 else 1.0,
                         inc=[("sACT", 1)])
                av = P.n("sACT")
                for uu in units:
                    PP.release(uu, "sACT", av)
            u = next_unit()
            pt = PP.acquire("pe", u)
            P.wait("pe", "sACT", P.n("sACT"))
            for hh in range(4):
                hc = slice(hh * 128, (hh + 1) * 128)
                for t in range(8):
                    P.op("pe", "matmul", pt[:, hc], CK[t][:, hc], CV[t][:, hc],
                         start=(t == 0), stop=(t == 7),
                         inc=[("sPE", 1)] if (hh == 3 and t == 7) else ())
            pv = P.n("sPE")
            slot = S32[hg % 2]
            P.wait("act", "sPE", pv)
            if hg >= 2:
                P.wait("act", "sH", 16 * hg)
            P.op("act", "activation", slot[:, 0:512], pt[:, 0:512], AF.Copy,
                 inc=[("sACT", 1)])
            av = P.n("sACT")
            PP.release(u, "sACT", av)
            P.wait("sp", "sACT", av)
            P.op("sp", "dma_start", out=kv_part[hg, :, :], in_=slot[:, 0:512],
                 inc=[("sH", 16)])
            gp_kv_dma = P.n("sH")
        groups = [[2 * i, 2 * i + 1] for i in range(max(1, ncores // 2))]
        P.wait("gp", "sH", gp_kv_dma)
        P.op("gp", "collective_compute", "AllReduce", ALU.add,
             replica_groups=groups, ins=[kv_part.ap().opt()],
             outs=[kv_sum.ap().opt()], inc=[("sCC", 1)])
        CC_DONE = P.n("sCC")

        def evq(m, pt, pv):
            P.wait("act", "sPE", pv)
            P.op("act", "activation", A[m][:, 0:512], pt[:, 0:512], AF.Relu,
                 scale=1.0 / L0_SCALE)
            P.op("act", "activation", A[m][:, 512:1024], pt[:, 512:1024], AF.Relu,
                 scale=1.0 / L0_SCALE, inc=[("sACT", 1)])
            return ("sACT", P.n("sACT"))

        def evu(m, pt, pv):
            P.wait("act", "sPE", pv)
            P.op("act", "activation", B[m][:, 0:512], pt[:, 0:512], AF.Silu)
            P.op("act", "activation", B[m][:, 512:1024], pt[:, 512:1024], AF.Silu,
                 inc=[("sACT", 1)])
            return ("sACT", P.n("sACT"))

        proj(XB, q_sids, KC, evq)
        proj(XB, u_sids, KC, evu)
        ACT_QU = P.n("sACT")
        PE_PROJ = P.n("sPE")

        P.wait("gp", "sCC", CC_DONE)
        P.wait("gp", "sPE", PE_PROJ)
        for hg in range(4):
            P.op("gp", "dma_start",
                 out=XB[hg // 2][:, (hg % 2) * 512:(hg % 2 + 1) * 512],
                 in_=kv_sum[hg, :, :], inc=[("sGP", 16)])
        KVRDY = P.n("sGP")

        P.wait("pe", "sGP", KVRDY)
        P.wait("pe", "sACT", ACT_QU)
        P.wait("dve", "sACT", ACT_QU)
        P.wait("act", "sDVE", P.n("sDVE"))
        UO = [0, 1]
        UCS, UBC = 2, 3
        st_ = {}

        def srms_o(h):
            kvt = XB[h // 8][:, (h % 8) * 128:(h % 8 + 1) * 128]
            u = UO[h % 2]
            pt = PP.acquire("pe", u)
            P.op("pe", "matmul", pt[:, 0:512], kvt, A[h][:, 0:512],
                 start=True, stop=True)
            P.op("pe", "matmul", pt[:, 512:1024], kvt, A[h][:, 512:1024],
                 start=True, stop=True, inc=[("sPE", 1)])
            st_[("opv", h)] = P.n("sPE")

        def srms_stats(h):
            u = UO[h % 2]
            pt = psum[u]
            P.wait("act", "sPE", st_[("opv", h)])
            P.wait("act", "sDVE", st_.get(("tt1", h - 2), 0))
            P.wait("act", "sPE", st_.get(("cs", h - 2), 0))
            P.op("act", "activation", S32[h % 2][:, :], pt[:, :], AF.Copy,
                 inc=[("sACT", 1)])
            P.op("act", "activation", XB[3 + h % 2][:, :], pt[:, :], AF.Square,
                 inc=[("sACT", 1)])
            av = P.n("sACT")
            st_[("ev", h)] = av
            PP.release(u, "sACT", av)
            ptc = PP.acquire("pe", UCS)
            P.wait("pe", "sACT", av)
            lane = 32 + (h % 2) * 32
            P.op("pe", "matmul", ptc[lane:lane + 1, 0:512], onesb[:, 0:1],
                 XB[3 + h % 2][:, 0:512], start=True, stop=True)
            P.op("pe", "matmul", ptc[lane:lane + 1, 512:1024], onesb[:, 0:1],
                 XB[3 + h % 2][:, 512:1024], start=True, stop=True,
                 inc=[("sPE", 1)])
            st_[("cs", h)] = P.n("sPE")
            P.wait("dve", "sPE", st_[("cs", h)])
            P.wait("dve", "sACT", st_.get(("rstd", h - 1), 0))
            P.op("dve", "tensor_scalar_max", ROWS[lane:lane + 1, :],
                 ptc[lane:lane + 1, :], 1.2829695e-22, inc=[("sDVE", 1)])
            dv = P.n("sDVE")
            PP.release(UCS, "sDVE", dv)
            P.wait("act", "sDVE", dv)
            P.op("act", "activation", ROWS[lane:lane + 1, :],
                 ROWS[lane:lane + 1, :], AF.Ln, inc=[("sACT", 1)])
            P.wait("act", "sACT", P.n("sACT"))
            P.op("act", "activation", ROWSB[lane:lane + 1, :],
                 ROWS[lane:lane + 1, :], AF.Exp, scale=-0.5,
                 bias=CB[lane:lane + 1, 0:1], inc=[("sACT", 1)])
            st_[("rstd", h)] = P.n("sACT")

        def srms_mul(h):
            lane = 32 + (h % 2) * 32
            ptb = PP.acquire("pe", UBC)
            P.wait("pe", "sACT", st_[("rstd", h)])
            P.op("pe", "matmul", ptb[:, 0:512], onesb[lane:lane + 1, 0:128],
                 ROWSB[lane:lane + 1, 0:512], start=True, stop=True)
            P.op("pe", "matmul", ptb[:, 512:1024], onesb[lane:lane + 1, 0:128],
                 ROWSB[lane:lane + 1, 512:1024], start=True, stop=True,
                 inc=[("sPE", 1)])
            pv3 = P.n("sPE")
            P.wait("dve", "sPE", pv3)
            P.wait("dve", "sDVE", st_.get(("tt2", h - 1), 0))
            P.op("dve", "tensor_mul", XB[2][:, :], S32[h % 2][:, :], ptb[:, :],
                 inc=[("sDVE", 1)])
            st_[("tt1", h)] = P.n("sDVE")
            PP.release(UBC, "sDVE", st_[("tt1", h)])
            P.wait("dve", "sDVE", st_[("tt1", h)])
            P.op("dve", "tensor_mul", B[h][:, :], XB[2][:, :], B[h][:, :],
                 inc=[("sDVE", 1)])
            st_[("tt2", h)] = P.n("sDVE")

        for h in range(L0_HEADS + 2):
            if h < L0_HEADS:
                srms_o(h)
            if 1 <= h <= L0_HEADS:
                srms_stats(h - 1)
            if h >= 2:
                srms_mul(h - 2)

        DVE_O = P.n("sDVE")
        P.wait("pe", "sDVE", DVE_O)
        P.wait("dve", "sX", X_READY)

        def evo(m, pt, pv):
            P.wait("dve", "sPE", pv)
            P.op("dve", "tensor_add", R[m][:, 0:512], R[m][:, 0:512], pt[:, 0:512])
            P.op("dve", "tensor_add", R[m][:, 512:1024], R[m][:, 512:1024],
                 pt[:, 512:1024], inc=[("sDVE", 1)])
            val = P.n("sDVE")
            ln_feed(m, "sDVE", val, pv)
            return ("sDVE", val)

        ln_begin(A[0], A[1])
        proj(B, o_sids, KC, evo)
        ln_finish()

    LNC = {"pend": []}

    def ln_begin(sq0, sq1):
        u = next_unit()
        pt = PP.acquire("pe", u)
        u2 = next_unit()
        pt2 = PP.acquire("pe", u2)
        state["avoid"] = {u, u2}
        LNC.update(u=u, u2=u2, pt=pt, pt2=pt2, sq=(sq0, sq1), cs={}, pend=[])

    def ln_chunk(m, rsem, rval, xbfree):
        pt, pt2 = LNC["pt"], LNC["pt2"]
        sq = LNC["sq"][m % 2]
        P.wait("act", rsem, rval)
        P.wait("act", "sPE", xbfree)
        P.op("act", "activation", XB[m][:, :], R[m][:, :], AF.Copy,
             inc=[("sACT", 1)])
        av = P.n("sACT")
        P.wait("dve", "sACT", av)
        P.wait("dve", "sPE", LNC["cs"].get(m - 2, 0))
        P.op("dve", "tensor_mul", sq[:, :], XB[m][:, :], XB[m][:, :],
             inc=[("sDVE", 1)])
        dv = P.n("sDVE")
        P.wait("pe", "sACT", av)
        for hh in range(2):
            n0 = hh * 512
            P.op("pe", "matmul", pt[32:33, n0:n0 + 512], onesb[:, 0:1],
                 XB[m][:, n0:n0 + 512], start=(m == 0), stop=(m == KC - 1))
        P.wait("pe", "sDVE", dv)
        for hh in range(2):
            n0 = hh * 512
            P.op("pe", "matmul", pt2[32:33, n0:n0 + 512], onesb[:, 0:1],
                 sq[:, n0:n0 + 512], start=(m == 0), stop=(m == KC - 1),
                 inc=[("sPE", 1)] if (m == KC - 1 and hh == 1) else ())
        LNC["cs"][m] = P.n("sPE")

    def ln_feed(m, rsem, rval, xbfree):
        LNC["pend"].append((m, rsem, rval, xbfree))
        if len(LNC["pend"]) > 1:
            ln_chunk(*LNC["pend"].pop(0))

    def ln_finish():
        for e_ in LNC["pend"]:
            ln_chunk(*e_)
        LNC["pend"] = []
        state["avoid"] = set()
        u, u2, pt, pt2 = LNC["u"], LNC["u2"], LNC["pt"], LNC["pt2"]
        pv = P.n("sPE")
        P.wait("act", "sPE", pv)
        P.op("act", "activation", ROWS[32:33, :], pt[32:33, :], AF.Copy,
             scale=1.0 / 2048.0, inc=[("sACT", 1)])
        P.op("act", "activation", S32[0][32:33, :], pt2[32:33, :], AF.Copy,
             scale=1.0 / 2048.0, inc=[("sACT", 1)])
        av = P.n("sACT")
        PP.release(u, "sACT", av)
        PP.release(u2, "sACT", av)
        P.wait("dve", "sACT", av)
        P.wait("dve", "sDVE", P.n("sDVE"))
        P.op("dve", "tensor_mul", S32[1][32:33, :], ROWS[32:33, :],
             ROWS[32:33, :], inc=[("sDVE", 1)])
        P.wait("dve", "sDVE", P.n("sDVE"))
        P.op("dve", "tensor_sub", S32[1][32:33, :], S32[0][32:33, :],
             S32[1][32:33, :], inc=[("sDVE", 1)])
        dv = P.n("sDVE")
        P.wait("act", "sDVE", dv)
        P.op("act", "activation", S32[1][32:33, :], S32[1][32:33, :], AF.Ln,
             bias=CB[32:33, 1:2], inc=[("sACT", 1)])
        P.wait("act", "sACT", P.n("sACT"))
        P.op("act", "activation", S32[1][32:33, :], S32[1][32:33, :], AF.Exp,
             scale=-0.5, inc=[("sACT", 1)])
        av2 = P.n("sACT")
        u3 = next_unit()
        ptm = PP.acquire("pe", u3)
        u4 = next_unit()
        ptr = PP.acquire("pe", u4)
        P.wait("pe", "sACT", av2)
        for hh in range(2):
            n0 = hh * 512
            P.op("pe", "matmul", ptm[:, n0:n0 + 512], onesf[32:33, :],
                 ROWS[32:33, n0:n0 + 512], start=True, stop=True)
            P.op("pe", "matmul", ptr[:, n0:n0 + 512], onesf[32:33, :],
                 S32[1][32:33, n0:n0 + 512], start=True, stop=True,
                 inc=[("sPE", 1)] if hh == 1 else ())
        pv2 = P.n("sPE")
        xbv = []
        P.wait("dve", "sPE", pv2)
        for m in range(KC):
            if m > 0:
                P.wait("dve", "sDVE", P.n("sDVE"))
            P.op("dve", "tensor_sub", S32[0][:, :], R[m][:, :], ptm[:, :],
                 inc=[("sDVE", 1)])
            P.wait("dve", "sDVE", P.n("sDVE"))
            P.op("dve", "tensor_mul", R[m][:, :], S32[0][:, :], ptr[:, :],
                 inc=[("sDVE", 1)])
            dv3 = P.n("sDVE")
            P.wait("act", "sDVE", dv3)
            P.op("act", "activation", XB[m][:, :], R[m][:, :], AF.Copy,
                 inc=[("sACT", 1)])
            xbv.append(P.n("sACT"))
        LN_XB["v"] = xbv
        PP.release(u3, "sDVE", P.n("sDVE"))
        PP.release(u4, "sDVE", P.n("sDVE"))

    def l0_ffn():
        P.wait("dve", "sDVE", P.n("sDVE"))
        P.wait("act", "sACT", P.n("sACT"))
        xbr = LN_XB["v"]
        sh_gu = []
        for m in range(MC_DS):
            sh_gu.append(WS.register(din["wsg_s"][m, :, :], 2048))
            sh_gu.append(WS.register(din["wsu_s"][m, :, :], 2048))
        sh_dn = []
        for m in range(KC):
            sh_dn.append((WS.register(din["wsd_s"][m, :, 0:2048], 2048), 16, 0))
            sh_dn.append((WS.register(din["wsd_s"][m, :, 2048:2816], 768), 6, 16))
        ex_gu, ex_dn = [], []
        for e in range(8):
            gu = []
            for m in range(MC_DE):
                gu.append(WS.register(din["weg_s"][e, m, :, :], 2048))
                gu.append(WS.register(din["weu_s"][e, m, :, :], 2048))
            ex_gu.append(gu)
            ex_dn.append([WS.register(din["wed_s"][e, m, :, 0:MC_DE * 128],
                                      MC_DE * 128) for m in range(KC)])

        u = next_unit()
        pt = PP.acquire("pe", u)
        for k in range(KC):
            if xbr is not None:
                P.wait("pe", "sACT", xbr[k])
            lhs = rtr[:, k * 8:(k + 1) * 8]
            for h in range(2):
                n0 = h * 512
                P.op("pe", "matmul", pt[0:8, n0:n0 + 512], lhs,
                     XB[k][:, n0:n0 + 512], start=(k == 0), stop=(k == KC - 1),
                     inc=[("sPE", 1)] if (k == KC - 1 and h == 1) else ())
        pv = P.n("sPE")
        P.wait("act", "sPE", pv)
        P.op("act", "activation", ROWS[0:8, :], pt[0:8, :], AF.Copy,
             inc=[("sACT", 1)])
        LOGV = P.n("sACT")
        PP.release(u, "sACT", LOGV)


        r_last = {}

        def gateup(gu_sids, mc, hpool, wb=None):
            for m in range(mc):
                bufg, sg, vg = WS.consume(gu_sids[2 * m])
                ug = next_unit()
                ptg = PP.acquire("pe", ug)
                P.wait("pe", sg, vg)
                for k in range(KC):
                    lhs = bufg[:, k * 128:(k + 1) * 128]
                    for h in range(2):
                        n0 = h * 512
                        P.op("pe", "matmul", ptg[:, n0:n0 + 512], lhs,
                             XB[k][:, n0:n0 + 512], start=(k == 0),
                             stop=(k == KC - 1),
                             inc=[("sPE", 1)] if (k == KC - 1 and h == 1) else ())
                pvg = P.n("sPE")
                WS.done(gu_sids[2 * m], "sPE", pvg)
                bufu, su, vu = WS.consume(gu_sids[2 * m + 1])
                uu = next_unit()
                ptu = PP.acquire("pe", uu)
                P.wait("pe", su, vu)
                for k in range(KC):
                    lhs = bufu[:, k * 128:(k + 1) * 128]
                    for h in range(2):
                        n0 = h * 512
                        P.op("pe", "matmul", ptu[:, n0:n0 + 512], lhs,
                             XB[k][:, n0:n0 + 512], start=(k == 0),
                             stop=(k == KC - 1),
                             inc=[("sPE", 1)] if (k == KC - 1 and h == 1) else ())
                pvu = P.n("sPE")
                WS.done(gu_sids[2 * m + 1], "sPE", pvu)
                P.wait("act", "sPE", pvg)
                P.wait("act", "sDVE", r_last.get("gsil", 0))
                P.op("act", "activation", S32[0][:, :], ptg[:, :], AF.Silu,
                     inc=[("sACT", 1)])
                av = P.n("sACT")
                PP.release(ug, "sACT", av)
                P.wait("dve", "sACT", av)
                P.wait("dve", "sPE", pvu)
                if wb is None:
                    P.op("dve", "tensor_mul", hpool[m][:, :], ptu[:, :],
                         S32[0][:, :], inc=[("sDVE", 1)])
                else:
                    P.wait("dve", "sDVE", r_last.get("tt2", 0))
                    P.op("dve", "tensor_mul", wb[1][:, :], ptu[:, :],
                         S32[0][:, :], inc=[("sDVE", 1)])
                dv = P.n("sDVE")
                r_last["gsil"] = dv
                PP.release(uu, "sDVE", dv)
                if wb is not None:
                    P.wait("dve", "sDVE", dv)
                    P.op("dve", "tensor_mul", hpool[m][:, :], wb[1][:, :],
                         wb[0][:, :], inc=[("sDVE", 1)])
                    r_last["tt2"] = P.n("sDVE")

        def down(dn_specs, hsrc, h_ready, feed=False):
            P.wait("pe", "sDVE", h_ready)
            for m, specs in enumerate(dn_specs):
                u = next_unit()
                pt = PP.acquire("pe", u)
                nsp = len(specs)
                for si, (sid, kcw, kr) in enumerate(specs):
                    buf, ws, wv = WS.consume(sid)
                    P.wait("pe", ws, wv)
                    for k in range(kcw):
                        lhs = buf[:, k * 128:(k + 1) * 128]
                        for h in range(2):
                            n0 = h * 512
                            P.op("pe", "matmul", pt[:, n0:n0 + 512], lhs,
                                 hsrc[kr + k][:, n0:n0 + 512],
                                 start=(si == 0 and k == 0),
                                 stop=(si == nsp - 1 and k == kcw - 1),
                                 inc=[("sPE", 1)] if (k == kcw - 1 and h == 1)
                                 else ())
                    WS.done(sid, "sPE", P.n("sPE"))
                pv = P.n("sPE")
                P.wait("dve", "sPE", pv)
                P.wait("dve", "sDVE", r_last.get(m, 0))
                P.op("dve", "tensor_add", R[m][:, 0:512], R[m][:, 0:512],
                     pt[:, 0:512])
                P.op("dve", "tensor_add", R[m][:, 512:1024], R[m][:, 512:1024],
                     pt[:, 512:1024], inc=[("sDVE", 1)])
                r_last[m] = P.n("sDVE")
                if feed:
                    ln_feed(m, "sDVE", r_last[m], pv)
                PP.release(u, "sDVE", r_last[m])

        hsh = B[0:16] + A[0:6]
        gateup(sh_gu, MC_DS, hsh)
        for t in range(8):
            cols = slice(t * 128, (t + 1) * 128)
            u = next_unit()
            pt = PP.acquire("pe", u)
            P.wait("pe", "sACT", LOGV)
            if t > 0:
                P.wait("pe", "sDVE", P.n("sDVE"))
            P.op("pe", "transpose", pt[:, 0:8], ROWS[0:8, cols],
                 ident[0:8, 0:8], inc=[("sPE", 1)])
            pv = P.n("sPE")
            E = SMX
            W = SMX[:, 32:48]
            P.wait("dve", "sPE", pv)
            P.wait("dve", "sACT", P.n("sACT"))
            P.op("dve", "reduce_max", W[:, 8:9], pt[:, 0:8], axis=AX.X,
                 negate=True, inc=[("sDVE", 1)])
            dv = P.n("sDVE")
            P.wait("act", "sDVE", dv)
            P.op("act", "activation", E[:, 0:8], pt[:, 0:8], AF.Exp,
                 bias=W[:, 8:9], inc=[("sACT", 1)])
            av = P.n("sACT")
            PP.release(u, "sACT", av)
            P.wait("dve", "sACT", av)
            P.op("dve", "reduce_sum", W[:, 9:10], E[:, 0:8], axis=AX.X,
                 inc=[("sDVE", 1)])
            dv = P.n("sDVE")
            P.wait("act", "sDVE", dv)
            P.op("act", "activation", W[:, 10:11], W[:, 9:10], AF.Ln,
                 inc=[("sACT", 1)])
            P.wait("act", "sACT", P.n("sACT"))
            P.op("act", "activation", W[:, 10:11], W[:, 10:11], AF.Exp,
                 scale=-1.0, inc=[("sACT", 1)])
            av = P.n("sACT")
            P.wait("dve", "sACT", av)
            P.op("dve", "tensor_single_scalar", E[:, 24:32], E[:, 0:8],
                 W[:, 10:11], ALU.mult, inc=[("sDVE", 1)])
            P.wait("dve", "sDVE", P.n("sDVE"))
            P.op("dve", "reduce_max", W[:, 11:12], E[:, 24:32], axis=AX.X,
                 inc=[("sDVE", 1)])
            P.wait("dve", "sDVE", P.n("sDVE"))
            P.op("dve", "tensor_single_scalar", E[:, 8:16], E[:, 24:32],
                 W[:, 11:12], ALU.is_ge, inc=[("sDVE", 1)])
            P.wait("dve", "sDVE", P.n("sDVE"))
            P.op("dve", "tensor_scalar", E[:, 8:16], E[:, 8:16], -1.0, 1.0,
                 ALU.mult, ALU.add, inc=[("sDVE", 1)])
            P.wait("dve", "sDVE", P.n("sDVE"))
            P.op("dve", "tensor_mul", E[:, 16:24], E[:, 24:32], E[:, 8:16],
                 inc=[("sDVE", 1)])
            P.wait("dve", "sDVE", P.n("sDVE"))
            P.op("dve", "reduce_max", W[:, 12:13], E[:, 16:24], axis=AX.X,
                 inc=[("sDVE", 1)])
            P.wait("dve", "sDVE", P.n("sDVE"))
            P.op("dve", "tensor_single_scalar", E[:, 8:16], E[:, 24:32],
                 W[:, 12:13], ALU.is_ge, inc=[("sDVE", 1)])
            P.wait("dve", "sDVE", P.n("sDVE"))
            P.op("dve", "tensor_mul", W[:, 0:8], E[:, 24:32], E[:, 8:16],
                 inc=[("sDVE", 1)])
            dv = P.n("sDVE")
            u = next_unit()
            pt = PP.acquire("pe", u)
            P.wait("pe", "sDVE", dv)
            P.op("pe", "transpose", pt[0:8, 0:128], W[:, 0:8], ident[:, :],
                 inc=[("sPE", 1)])
            pv = P.n("sPE")
            P.wait("act", "sPE", pv)
            P.op("act", "activation", ROWSB[0:8, cols], pt[0:8, 0:128], AF.Copy,
                 inc=[("sACT", 1)])
            PP.release(u, "sACT", P.n("sACT"))
        WRDY = P.n("sACT")
        down([[sh_dn[2 * m], sh_dn[2 * m + 1]] for m in range(KC)], hsh,
             P.n("sDVE"))

        for e in range(8):
            pool = A if e % 2 == 0 else B
            u = next_unit()
            pt = PP.acquire("pe", u)
            P.wait("pe", "sACT", WRDY)
            for h in range(2):
                n0 = h * 512
                P.op("pe", "matmul", pt[:, n0:n0 + 512],
                     ind8[:, e * 128:(e + 1) * 128], ROWSB[0:8, n0:n0 + 512],
                     start=True, stop=True,
                     inc=[("sPE", 1)] if h == 1 else ())
            pv = P.n("sPE")
            P.wait("act", "sPE", pv)
            P.op("act", "activation", pool[15][:, :], pt[:, :], AF.Copy,
                 inc=[("sACT", 1)])
            av = P.n("sACT")
            PP.release(u, "sACT", av)
            P.wait("dve", "sACT", av)
            gateup(ex_gu[e], MC_DE, pool, wb=(pool[15], pool[12]))
            if e == 7:
                ln_begin(B[13], B[14])
            down([[(ex_dn[e][m], MC_DE, 0)] for m in range(KC)], pool,
                 P.n("sDVE"), feed=(e == 7))

    def l1_attn():
        P.wait("dve", "sDVE", P.n("sDVE"))
        P.wait("act", "sACT", P.n("sACT"))
        P.wait("act", "sDVE", P.n("sDVE"))
        xbr = LN_XB["v"]
        rl = {}
        k_sid = WS.register(din["l1k_s"][:, :], KC * 64)
        q_sids = [WS.register(din["l1q_s"][m, :, :], 2048) for m in range(KC)]
        d_sids = [WS.register(din["l1d_s"][m, :, :], 2048) for m in range(KC)]

        VW = B[2]
        P.wait("gp", "sPE", P.n("sPE"))
        vsrc = bass.AP(din["l1v_r"], 0, [[64, 128], [128 * 64, KC], [1, 64]])
        P.op("gp", "dma_start", out=VW[:, 0:KC * 64], in_=vsrc, inc=[("sGP", 16)])
        VW_GP = P.n("sGP")
        uv = next_unit()
        ptv = PP.acquire("pe", uv)
        P.wait("pe", "sGP", VW_GP)
        for t in range(8):
            for k in range(KC):
                if t == 0 and xbr is not None:
                    P.wait("pe", "sACT", xbr[k])
                P.op("pe", "matmul", ptv[:, t * 64:(t + 1) * 64],
                     XB[k][:, t * 128:(t + 1) * 128], VW[:, k * 64:(k + 1) * 64],
                     start=(k == 0), stop=(k == KC - 1),
                     inc=[("sPE", 1)] if (t == 7 and k == KC - 1) else ())
        pv = P.n("sPE")
        P.wait("act", "sPE", pv)
        P.op("act", "activation", B[1][:, 0:512], ptv[:, 0:512], AF.Copy,
             inc=[("sACT", 1)])
        av = P.n("sACT")
        PP.release(uv, "sACT", av)
        P.wait("sp", "sACT", av)
        P.op("sp", "dma_start", out=l1kv_loc[1].rearrange("p f -> (p f)").rearrange(
            "(p f) -> p f", p=128), in_=B[1][:, 0:512], inc=[("sH", 16)])

        uk = next_unit()
        ptk = PP.acquire("pe", uk)
        bufk, ws, wv = WS.consume(k_sid)
        P.wait("pe", ws, wv)
        for k in range(KC):
            lhs = bufk[:, k * 64:(k + 1) * 64]
            for h in range(2):
                n0 = h * 512
                P.op("pe", "matmul", ptk[0:64, n0:n0 + 512], lhs,
                     XB[k][:, n0:n0 + 512], start=(k == 0), stop=(k == KC - 1),
                     inc=[("sPE", 1)] if (k == KC - 1 and h == 1) else ())
        pv = P.n("sPE")
        WS.done(k_sid, "sPE", pv)

        def rope(rows, src_ps, u_src, dst):
            P.wait("act", "sPE", P.n("sPE"))
            P.wait("act", "sDVE", rl.get("ropeS1", 0))
            P.op("act", "activation", S32[0][rows, :], src_ps[rows, :], AF.Copy,
                 inc=[("sACT", 1)])
            av = P.n("sACT")
            PP.release(u_src, "sACT", av)
            up = next_unit()
            ptp = PP.acquire("pe", up)
            P.wait("pe", "sACT", av)
            n = rows.stop - rows.start
            for h in range(2):
                n0 = h * 512
                P.op("pe", "matmul", ptp[rows, n0:n0 + 512],
                     perm[rows, 0:n], S32[0][rows, n0:n0 + 512],
                     start=True, stop=True,
                     inc=[("sPE", 1)] if h == 1 else ())
            pvp = P.n("sPE")
            P.wait("dve", "sACT", av)
            P.wait("dve", "sDVE", rl.get("ropeS1", 0))
            P.op("dve", "tensor_mul", S32[1][rows, :], S32[0][rows, :],
                 cos_t[rows, :], inc=[("sDVE", 1)])
            P.wait("dve", "sDVE", P.n("sDVE"))
            P.wait("dve", "sPE", pvp)
            P.op("dve", "tensor_mul", S32[0][rows, :], ptp[rows, :],
                 sin_t[rows, :], inc=[("sDVE", 1)])
            rl["ropeS0"] = P.n("sDVE")
            PP.release(up, "sDVE", rl["ropeS0"])
            P.wait("dve", "sDVE", rl["ropeS0"])
            P.op("dve", "tensor_add", dst, S32[1][rows, :], S32[0][rows, :],
                 inc=[("sDVE", 1)])
            rl["ropeS1"] = P.n("sDVE")

        rope(slice(0, 64), ptk, uk, B[0][0:64, :])
        P.wait("sp", "sDVE", rl["ropeS1"])
        P.op("sp", "dma_start", out=l1kv_loc[0].rearrange("p f -> (p f)").rearrange(
            "(p f) -> p f", p=64), in_=B[0][0:64, :], inc=[("sH", 16)])
        GP_KV = P.n("sH")

        groups = [[2 * i, 2 * i + 1] for i in range(max(1, ncores // 2))]
        P.wait("gp", "sH", GP_KV)
        P.op("gp", "collective_compute", "AllGather", ALU.bypass,
             replica_groups=groups, ins=[l1kv_loc.ap().opt()],
             outs=[l1kv_all.ap().opt()], inc=[("sCC", 1)])
        CCV = P.n("sCC")

        KH = [B[4], B[5]]
        VT = [B[6], B[7]]
        PE0 = P.n("sPE")
        P.wait("sp", "sCC", CCV)
        P.wait("sp", "sPE", PE0)
        P.wait("gp", "sPE", PE0)
        for g in range(KC):
            rk = g % 2
            lt = g // 2
            src = l1kv_all[rk, 0].rearrange("p f -> (p f)").rearrange(
                "(p f) -> p f", p=64)[:, lt * 128:(lt + 1) * 128]
            dst = KH[g // 8]
            c0 = (g % 8) * 128
            P.op("sp", "dma_start", out=dst[0:64, c0:c0 + 128], in_=src,
                 inc=[("sH", 16)])
            P.op("sp", "dma_start", out=dst[64:128, c0:c0 + 128], in_=src,
                 inc=[("sH", 16)])
            srcv = l1kv_all[rk, 1].rearrange("p f -> (p f)").rearrange(
                "(p f) -> p f", p=128)[:, lt * 64:(lt + 1) * 64]
            dv_ = VT[g // 8]
            cv = (g % 8) * 65
            P.op("sp", "dma_start", out=dv_[:, cv:cv + 64], in_=srcv,
                 inc=[("sH", 16)])
            P.op("gp", "memset", VT[g // 8][:, cv + 64:cv + 65], 1.0,
                 inc=[("sGPC", 1)])
        KV_GP = P.n("sH")
        KV_GPC = P.n("sGPC")

        for m in range(KC):
            buf, ws, wv = WS.consume(q_sids[m])
            u = next_unit()
            pt = PP.acquire("pe", u)
            P.wait("pe", ws, wv)
            for k in range(KC):
                lhs = buf[:, k * 128:(k + 1) * 128]
                for h in range(2):
                    n0 = h * 512
                    P.op("pe", "matmul", pt[:, n0:n0 + 512], lhs,
                         XB[k][:, n0:n0 + 512], start=(k == 0),
                         stop=(k == KC - 1),
                         inc=[("sPE", 1)] if (k == KC - 1 and h == 1) else ())
            pvq = P.n("sPE")
            WS.done(q_sids[m], "sPE", pvq)
            rope(slice(0, 128), pt, u, A[m][:, :])
        Q_DVE = P.n("sDVE")

        P.wait("pe", "sH", KV_GP)
        P.wait("pe", "sGPC", KV_GPC)
        P.wait("pe", "sDVE", Q_DVE)
        P.wait("act", "sACT", P.n("sACT"))
        P.wait("act", "sDVE", P.n("sDVE"))
        UC0, UC1 = 2, 3
        NPBUF = 6
        PSB = [B[8 + i] for i in range(NPBUF)]
        exp_wait = {i: 0 for i in range(NPBUF)}
        prev_tail = [None]
        for hp in range(L1_NH // 2):
            h0 = 2 * hp
            qch = A[hp]
            pair = {"acq": False}

            def emit_ctx_pair(g, segs, hp=hp, qch=qch, pair=pair):
                if not pair["acq"]:
                    PP.acquire("pe", UC0)
                    PP.acquire("pe", UC1)
                    pair["acq"] = True
                vt65 = VT[g // 8][:, (g % 8) * 65:(g % 8) * 65 + 65]
                for hh in range(2):
                    pb = (2 * g + hh) % NPBUF
                    psb = PSB[pb]
                    P.wait("pe", "sDVE", rl[("mask", pb)])
                    for i, (s0, s1) in enumerate(segs):
                        lastg = (g == (7 if s0 < 512 else KC - 1))
                        if hh == 0:
                            P.op("pe", "matmul", psum[UC0][0:65, s0:s1], vt65,
                                 psb[:, s0:s1], start=(g == 0), stop=lastg,
                                 inc=[("sPE", 1)] if i == len(segs) - 1 else ())
                        else:
                            P.op("pe", "matmul", psum[UC1][64:128, s0:s1],
                                 vt65[:, 0:64], psb[:, s0:s1],
                                 start=(g == 0), stop=lastg)
                            P.op("pe", "matmul", psum[UC1][32:33, s0:s1],
                                 vt65[:, 64:65], psb[:, s0:s1],
                                 start=(g == 0), stop=lastg,
                                 inc=[("sPE", 1)] if i == len(segs) - 1 else ())
                    exp_wait[pb] = P.n("sPE")

            pend = []
            for g in range(KC):
                qs = 128 * (g // 2)
                segs = []
                s0 = qs
                while s0 < 1024:
                    s1 = min(1024, (s0 // 512 + 1) * 512)
                    segs.append((s0, s1))
                    s0 = s1
                for hh in range(2):
                    base = hh * 64
                    rows = slice(base, base + 64)
                    klhs = KH[g // 8][rows, (g % 8) * 128:(g % 8) * 128 + 128]
                    us = hh
                    pts = PP.acquire("pe", us)
                    for i, (s0, s1) in enumerate(segs):
                        P.op("pe", "matmul", pts[:, s0:s1], klhs,
                             qch[rows, s0:s1], start=True, stop=True,
                             inc=[("sPE", 1)] if i == len(segs) - 1 else ())
                    pvs = P.n("sPE")
                    pb = (2 * g + hh) % NPBUF
                    psb = PSB[pb]
                    P.wait("act", "sPE", max(pvs, exp_wait[pb]))
                    P.wait("act", "sDVE", rl.get(("mask", pb), 0))
                    P.op("act", "activation", psb[:, qs:1024], pts[:, qs:1024],
                         AF.Exp, scale=0.125, inc=[("sACT", 1)])
                    ave = P.n("sACT")
                    PP.release(us, "sACT", ave)
                    P.wait("dve", "sACT", ave)
                    P.wait("dve", "sPE", exp_wait[pb])
                    P.op("dve", "tensor_mul", psb[:, qs:qs + 128],
                         psb[:, qs:qs + 128], dmask[:, g * 128:(g + 1) * 128],
                         inc=[("sDVE", 1)])
                    rl[("mask", pb)] = P.n("sDVE")
                if g == 2 and prev_tail[0] is not None:
                    prev_tail[0]()
                pend.append((g, segs))
                if len(pend) > 2:
                    emit_ctx_pair(*pend.pop(0))
            for e_ in pend:
                emit_ctx_pair(*e_)
            ctx_done_val = P.n("sPE")

            def tail(hp=hp, qch=qch, ctx_done_val=ctx_done_val):
                ptc0, ptc1 = psum[UC0], psum[UC1]
                P.wait("act", "sPE", ctx_done_val)
                P.wait("act", "sDVE", rl.get("rden", 0))
                for dl, ptd in ((64, ptc0), (32, ptc1)):
                    P.op("act", "activation", S32[0][dl:dl + 1, :],
                         ptd[dl:dl + 1, :], AF.Ln, inc=[("sACT", 1)])
                    P.wait("act", "sACT", P.n("sACT"))
                    P.op("act", "activation", ROWSB[dl:dl + 1, :],
                         S32[0][dl:dl + 1, :], AF.Exp, scale=-1.0,
                         inc=[("sACT", 1)])
                avd = P.n("sACT")
                P.wait("pe", "sACT", avd)
                for hh in range(2):
                    n0 = hh * 512
                    P.op("pe", "matmul", ptc1[0:64, n0:n0 + 512],
                         onesb[64:65, 0:64], ROWSB[64:65, n0:n0 + 512],
                         start=True, stop=True)
                    P.op("pe", "matmul", ptc0[64:128, n0:n0 + 512],
                         onesb[32:33, 0:64], ROWSB[32:33, n0:n0 + 512],
                         start=True, stop=True,
                         inc=[("sPE", 1)] if hh == 1 else ())
                pvr = P.n("sPE")
                P.wait("act", "sPE", pvr)
                P.op("act", "activation", S32[1][0:64, :], ptc1[0:64, :],
                     AF.Copy, inc=[("sACT", 1)])
                P.op("act", "activation", S32[1][64:128, :], ptc0[64:128, :],
                     AF.Copy, inc=[("sACT", 1)])
                avr = P.n("sACT")
                P.wait("dve", "sACT", avr)
                P.wait("dve", "sPE", pvr)
                P.op("dve", "tensor_mul", qch[0:64, :], ptc0[0:64, :],
                     S32[1][0:64, :], inc=[("sDVE", 1)])
                P.wait("dve", "sDVE", P.n("sDVE"))
                P.op("dve", "tensor_mul", qch[64:128, :], ptc1[64:128, :],
                     S32[1][64:128, :], inc=[("sDVE", 1)])
                rl["rden"] = P.n("sDVE")
                PP.release(UC0, "sDVE", rl["rden"])
                PP.release(UC1, "sDVE", rl["rden"])

            prev_tail[0] = tail
        prev_tail[0]()

        CTX_DVE = P.n("sDVE")
        P.wait("pe", "sDVE", CTX_DVE)

        def evd(m, pt, pv):
            P.wait("dve", "sPE", pv)
            P.wait("dve", "sDVE", rl.get(("r", m), 0))
            P.op("dve", "tensor_add", R[m][:, 0:512], R[m][:, 0:512], pt[:, 0:512])
            P.op("dve", "tensor_add", R[m][:, 512:1024], R[m][:, 512:1024],
                 pt[:, 512:1024], inc=[("sDVE", 1)])
            rl[("r", m)] = P.n("sDVE")
            ln_feed(m, "sDVE", rl[("r", m)], pv)
            return ("sDVE", rl[("r", m)])

        ln_begin(B[12], B[13])
        proj(A, d_sids, KC, evd)
        ln_finish()

    def l1_ffn():
        P.wait("dve", "sDVE", P.n("sDVE"))
        P.wait("act", "sACT", P.n("sACT"))
        P.wait("act", "sDVE", P.n("sDVE"))
        xbr = LN_XB["v"]
        rl = {}
        P.op("sp", "dma_start", out=l1sc[:, 0:MC_DI], in_=din["l1gsc"][:, :],
             inc=[("sSP", 16)])
        P.op("sp", "dma_start", out=l1sc[:, MC_DI:2 * MC_DI],
             in_=din["l1usc"][:, :], inc=[("sSP", 16)])
        P.op("sp", "dma_start", out=l1sc[:, 2 * MC_DI:2 * MC_DI + KC],
             in_=din["l1dsc"][:, :], inc=[("sSP", 16)])
        SCV = P.n("sSP")
        P.wait("act", "sSP", SCV)
        P.wait("dve", "sSP", SCV)

        f8idx = [0]
        def reg8(src_ap, ncols):
            s = WS.register(src_ap, ncols, fp8=(F8, ("sF80", "sF81"), f8idx[0]))
            f8idx[0] += 1
            return s

        specs = []
        for kg in range(3):
            gu = []
            for mm_ in range(16):
                m = kg * 16 + mm_
                gu.append((m, reg8(din["l1g_s"][m, :, :], 2048),
                           reg8(din["l1u_s"][m, :, :], 2048)))
            dn = [reg8(din["l1dn_s"][m, :, kg * 2048:(kg + 1) * 2048], 2048)
                  for m in range(KC)]
            specs.append((gu, dn))

        for kg, (gu, dn) in enumerate(specs):
            for mm_, (m, sg, su) in enumerate(gu):
                bufg, ws, wv = WS.consume(sg)
                ug = next_unit()
                ptg = PP.acquire("pe", ug)
                P.wait("pe", ws, wv)
                for k in range(KC):
                    if kg == 0 and mm_ == 0 and xbr is not None:
                        P.wait("pe", "sACT", xbr[k])
                    lhs = bufg[:, k * 128:(k + 1) * 128]
                    for h in range(2):
                        n0 = h * 512
                        P.op("pe", "matmul", ptg[:, n0:n0 + 512], lhs,
                             XB[k][:, n0:n0 + 512], start=(k == 0),
                             stop=(k == KC - 1),
                             inc=[("sPE", 1)] if (k == KC - 1 and h == 1) else ())
                pvg = P.n("sPE")
                WS.done(sg, "sPE", pvg)
                bufu, ws, wv = WS.consume(su)
                uu = next_unit()
                ptu = PP.acquire("pe", uu)
                P.wait("pe", ws, wv)
                for k in range(KC):
                    lhs = bufu[:, k * 128:(k + 1) * 128]
                    for h in range(2):
                        n0 = h * 512
                        P.op("pe", "matmul", ptu[:, n0:n0 + 512], lhs,
                             XB[k][:, n0:n0 + 512], start=(k == 0),
                             stop=(k == KC - 1),
                             inc=[("sPE", 1)] if (k == KC - 1 and h == 1) else ())
                pvu = P.n("sPE")
                WS.done(su, "sPE", pvu)
                P.wait("act", "sPE", pvg)
                P.wait("act", "sDVE", rl.get("gsil", 0))
                P.op("act", "activation", S32[0][:, :], ptg[:, :], AF.Silu,
                     scale=l1sc[:, m:m + 1], inc=[("sACT", 1)])
                av = P.n("sACT")
                PP.release(ug, "sACT", av)
                P.wait("dve", "sACT", av)
                P.wait("dve", "sPE", pvu)
                P.op("dve", "scalar_tensor_tensor", B[mm_][:, :], ptu[:, :],
                     l1sc[:, MC_DI + m:MC_DI + m + 1], S32[0][:, :],
                     ALU.mult, ALU.mult, inc=[("sDVE", 1)])
                rl["gsil"] = P.n("sDVE")
                PP.release(uu, "sDVE", rl["gsil"])
            if kg == 2:
                ln_begin(A[0], A[1])
            H_DVE = P.n("sDVE")
            P.wait("pe", "sDVE", H_DVE)
            for m in range(KC):
                buf, ws, wv = WS.consume(dn[m])
                u = next_unit()
                pt = PP.acquire("pe", u)
                P.wait("pe", ws, wv)
                for k in range(16):
                    lhs = buf[:, k * 128:(k + 1) * 128]
                    for h in range(2):
                        n0 = h * 512
                        P.op("pe", "matmul", pt[:, n0:n0 + 512], lhs,
                             B[k][:, n0:n0 + 512], start=(k == 0),
                             stop=(k == 15),
                             inc=[("sPE", 1)] if (k == 15 and h == 1) else ())
                pv = P.n("sPE")
                WS.done(dn[m], "sPE", pv)
                P.wait("dve", "sPE", pv)
                P.wait("dve", "sDVE", rl.get(("r", m), 0))
                dcol = l1sc[:, 2 * MC_DI + m:2 * MC_DI + m + 1]
                P.op("dve", "scalar_tensor_tensor", R[m][:, 0:512],
                     pt[:, 0:512], dcol, R[m][:, 0:512], ALU.mult, ALU.add)
                P.op("dve", "scalar_tensor_tensor", R[m][:, 512:1024],
                     pt[:, 512:1024], dcol, R[m][:, 512:1024], ALU.mult,
                     ALU.add, inc=[("sDVE", 1)])
                rl[("r", m)] = P.n("sDVE")
                if kg == 2:
                    ln_feed(m, "sDVE", rl[("r", m)], pv)
                PP.release(u, "sDVE", rl[("r", m)])

    l0_attn()
    if stop_after != "l0attn":
        l0_ffn()
        ln_finish()
    if stop_after not in ("l0attn", "l0ffn"):
        l1_attn()
    if stop_after is None:
        l1_ffn()
        ln_finish()

    P.wait("sp", "sACT", P.n("sACT"))
    P.wait("sp", "sDVE", P.n("sDVE"))
    ydst = bass.AP(dy, 0, [[T, 128], [128 * T, KC], [1, T]])
    P.op("sp", "dma_start", out=ydst, in_=RALL[:, :], inc=[("sOUT", 16)])
    P.wait("sp", "sOUT", P.n("sOUT"))

    with nc.Block() as block:
        @block.gpsimd
        def _(g):
            P.emit("gp", g)

        @block.tensor
        def _(t):
            P.emit("pe", t)

        @block.vector
        def _(v):
            P.emit("dve", v)

        @block.scalar
        def _(s):
            P.emit("act", s)

        @block.sync
        def _(s):
            P.emit("sp", s)

    return nc


def kernel(**inputs):
    in_maps = prepare_inputs(inputs, NCORES)
    nc = build_nc(NCORES)
    res = run_bass_kernel_spmd(nc, in_maps, core_ids=list(range(NCORES)))
    return gather_output(res.results, NCORES)


# version 09 (final)
# speedup vs baseline: 1.2081x; 1.0690x over previous
import math

import numpy as np

import concourse.bass as bass
import concourse.mybir as mybir
from concourse.bass_utils import run_bass_kernel_spmd

F32 = mybir.dt.float32
BF16 = mybir.dt.bfloat16
FP8 = mybir.dt.float8e4
AF = mybir.ActivationFunctionType
ALU = mybir.AluOpType
AX = mybir.AxisListType

D = 2048
KC = 16
T = 1024
NCORES = 8
L0_HEADS = 16
L0_SCALE = float(np.sqrt(128.0))
L1_NH, L1_HD = 32, 64
DS = 2816
DE = 1408
DI = 6144
MC_DS = DS // 128
MC_DE = DE // 128
MC_DI = DI // 128

STOP_AFTER = None



def _mstripes(w_io):
    K, M = w_io.shape
    kc, mc = K // 128, M // 128
    t = w_io.reshape(kc, 128, mc, 128).transpose(2, 1, 0, 3).reshape(mc, 128, kc * 128)
    return np.ascontiguousarray(t)


def _rowstripes(w_io):
    K, M = w_io.shape
    return np.ascontiguousarray(w_io.reshape(K // 128, 128, M))


def _rope_tables(positions):
    import jax
    import jax.numpy as jnp
    with jax.default_device(jax.devices("cpu")[0]):
        inv_freq = 1.0 / 10000 ** (jnp.arange(0, L1_HD, 2, dtype=jnp.float32) / L1_HD)
        pos = jnp.asarray(positions, dtype=jnp.float32)
        freqs = pos[:, None] * inv_freq[None, :]
        cos = np.asarray(jnp.cos(freqs)).T
        sin = np.asarray(jnp.sin(freqs)).T
    cos64 = np.concatenate([cos, cos], axis=0)
    sin64 = np.concatenate([-sin, sin], axis=0)
    cos128 = np.concatenate([cos64, cos64], axis=0)
    sin128 = np.concatenate([sin64, sin64], axis=0)
    return np.ascontiguousarray(cos128), np.ascontiguousarray(sin128)


def _consts():
    ident = np.eye(128, dtype=np.float32)
    perm = np.zeros((128, 128), dtype=np.float32)
    for m in range(128):
        p = (m // 64) * 64 + ((m % 64) + 32) % 64
        perm[p, m] = 1.0
    tri = np.triu(np.ones((128, 128), dtype=np.float32)).astype(np.float32)
    ones = np.ones((128, 128), dtype=np.float32)
    ind8 = np.zeros((8, 8 * 128), dtype=np.float32)
    for e in range(8):
        ind8[e, e * 128:(e + 1) * 128] = 1.0
    return ident, perm, tri, ones, ind8


def prepare_inputs(inputs, ncores=NCORES):
    f = {k: np.asarray(v) for k, v in inputs.items()}
    shared = {}
    shared["wq_s"] = _mstripes(np.ascontiguousarray(f["L0_attn_wq"].T))
    shared["wu_s"] = _mstripes(np.ascontiguousarray(f["L0_attn_wu"].T))
    shared["wo_s"] = _mstripes(np.ascontiguousarray(f["L0_attn_wo"].T))
    shared["wk_r"] = _rowstripes(np.ascontiguousarray(f["L0_attn_wk"].T))
    shared["wv_r"] = _rowstripes(np.ascontiguousarray(f["L0_attn_wv"].T))
    shared["wrtr"] = np.ascontiguousarray(
        f["L0_ffn_W_router"].reshape(KC, 128, 8).transpose(1, 0, 2).reshape(128, KC * 8))
    shared["wsg_s"] = _mstripes(f["L0_ffn_Ws_gate"])
    shared["wsu_s"] = _mstripes(f["L0_ffn_Ws_up"])
    shared["wsd_s"] = _mstripes(f["L0_ffn_Ws_down"])
    shared["weg_s"] = np.ascontiguousarray(
        np.stack([_mstripes(f["L0_ffn_We_gate"][e]) for e in range(8)]))
    shared["weu_s"] = np.ascontiguousarray(
        np.stack([_mstripes(f["L0_ffn_We_up"][e]) for e in range(8)]))
    shared["wed_s"] = np.ascontiguousarray(
        np.stack([_mstripes(f["L0_ffn_We_down"][e]) for e in range(8)]))
    wqkv = f["L1_attn_w_qkv"]
    shared["l1q_s"] = _mstripes(np.ascontiguousarray(wqkv[:, 0:2048]))
    shared["l1k_s"] = np.ascontiguousarray(
        wqkv[:, 2048:2112].reshape(KC, 128, 64).transpose(1, 0, 2).reshape(128, KC * 64))
    shared["l1v_r"] = _rowstripes(np.ascontiguousarray(wqkv[:, 2112:2176]))
    shared["l1d_s"] = _mstripes(f["L1_attn_w_dense"])
    shared["l1g_s"] = _mstripes(f["L1_ffn_gate_w"])
    shared["l1u_s"] = _mstripes(f["L1_ffn_up_w"])
    shared["l1dn_s"] = _mstripes(f["L1_ffn_down_w"])
    shared["l1gsc"] = np.ascontiguousarray(f["L1_ffn_gate_scale"].reshape(MC_DI, 128).T)
    shared["l1usc"] = np.ascontiguousarray(f["L1_ffn_up_scale"].reshape(MC_DI, 128).T)
    shared["l1dsc"] = np.ascontiguousarray(f["L1_ffn_down_scale"].reshape(KC, 128).T)
    ident, perm, tri, ones, ind8 = _consts()
    cb = np.zeros((128, 4), dtype=np.float32)
    cb[:, 0] = 0.5 * math.log(128.0)
    cb[:, 1] = 1e-5
    shared["c_bias"] = cb
    shared["c_ident"] = ident
    shared["c_perm"] = perm
    shared["c_tri"] = tri
    shared["c_ones"] = ones
    shared["c_ind8"] = ind8

    x = f["L0_attn_x"]
    in_maps = []
    for c in range(ncores):
        b, par = c // 2, c % 2
        blocks = [par + 2 * j for j in range(8)]
        tok = np.concatenate([np.arange(g * 128, (g + 1) * 128) for g in blocks])
        m = dict(shared)
        m["x"] = np.ascontiguousarray(x[b, tok, :].T)
        cosq, sinq = _rope_tables(tok)
        m["rope_cos"] = cosq
        m["rope_sin"] = sinq
        dm = np.zeros((128, KC * 128), dtype=np.float32)
        tri = np.triu(np.ones((128, 128), dtype=np.float32))
        for g in range(KC):
            gb = par + 2 * (g // 2)
            if gb == g:
                dm[:, g * 128:(g + 1) * 128] = tri
            elif gb > g:
                dm[:, g * 128:(g + 1) * 128] = 1.0
        m["c_dmask"] = dm
        in_maps.append(m)
    return in_maps


def gather_output(results, ncores=NCORES):
    out = np.empty((4, 2048, 2048), dtype=np.float32)
    for c in range(ncores):
        b, par = c // 2, c % 2
        blocks = [par + 2 * j for j in range(8)]
        tok = np.concatenate([np.arange(g * 128, (g + 1) * 128) for g in blocks])
        out[b, tok, :] = results[c]["y"].T
    return out



ENGS = ("gp", "pe", "dve", "act", "sp")


class Plan:
    def __init__(self, nc):
        self.nc = nc
        self.q = {e: [] for e in ENGS}
        self.cnt = {}
        self.sems = {}

    def sem(self, name):
        if name not in self.sems:
            self.sems[name] = self.nc.alloc_semaphore(name=name)
            self.cnt[name] = 0
        return name

    def wait(self, eng, s, v):
        if v > 0:
            self.q[eng].append(("w", s, int(v)))

    def op(self, eng, fn, *args, inc=(), **kw):
        for s, k in inc:
            self.cnt[s] += k
        self.q[eng].append(("o", fn, args, kw, tuple(inc)))

    def n(self, s):
        return self.cnt.get(s, 0)

    def emit(self, eng, engine):
        for item in self.q[eng]:
            if item[0] == "w":
                engine.wait_ge(self.sems[item[1]], item[2])
            else:
                _, fn, args, kw, inc = item
                inst = getattr(engine, fn)(*args, **kw)
                for s, k in inc:
                    inst = inst.then_inc(self.sems[s], k)


class PsumPool:

    def __init__(self, plan, tensors):
        self.p = plan
        self.t = tensors
        self.free_at = [None] * len(tensors)

    def acquire(self, eng, u):
        fa = self.free_at[u]
        if fa is not None:
            self.p.wait(eng, fa[0], fa[1])
        return self.t[u]

    def release(self, u, sem, val):
        self.free_at[u] = (sem, val)


class WStream:

    NB = 3
    LA = 2

    def __init__(self, plan, bufs):
        self.p = plan
        self.bufs = bufs
        self.sems = [plan.sem(f"sW{j}") for j in range(self.NB)]
        self.stripes = []
        self.loaded = 0
        self.consumed_at = {}
        self.f8_done = {}
        self.nexti = 0

    def register(self, src_ap, ncols, fp8=None):
        self.stripes.append((src_ap, ncols, fp8))
        return len(self.stripes) - 1

    def _plan_load(self):
        k = self.loaded
        if k >= len(self.stripes):
            return
        src, ncols, fp8 = self.stripes[k]
        b = k % self.NB
        if fp8 is not None:
            f8bufs, f8sems, f8idx = fp8
            j = f8idx % 2
            prev = self.f8_done.get(f8idx - 2)
            if prev is not None:
                self.p.wait("gp", prev[0], prev[1])
            self.p.op("gp", "dma_start", out=f8bufs[j][:, 0:ncols], in_=src,
                      inc=[(f8sems[j], 16)])
            self.p.wait("gp", f8sems[j], self.p.n(f8sems[j]))
            if k >= self.NB:
                dep = self.consumed_at.get(k - self.NB)
                assert dep is not None
                self.p.wait("gp", dep[0], dep[1])
            self.p.op("gp", "dma_start", out=self.bufs[b][:, 0:ncols],
                      in_=f8bufs[j][:, 0:ncols], inc=[(self.sems[b], 16)])
            self.f8_done[f8idx] = (self.sems[b], 16 * (k // self.NB + 1))
        else:
            if k >= self.NB:
                dep = self.consumed_at.get(k - self.NB)
                assert dep is not None, f"stripe {k} loads before {k - self.NB} consumed"
                self.p.wait("gp", dep[0], dep[1])
            self.p.op("gp", "dma_start", out=self.bufs[b][:, 0:ncols], in_=src,
                      inc=[(self.sems[b], 16)])
        self.loaded += 1

    def consume(self, k):
        while self.loaded <= min(k + self.LA, len(self.stripes) - 1):
            self._plan_load()
        b = k % self.NB
        return self.bufs[b], self.sems[b], 16 * (k // self.NB + 1)

    def done(self, k, sem, val):
        self.consumed_at[k] = (sem, val)



def build_nc(ncores=NCORES, stop_after=STOP_AFTER):
    nc = bass.Bass("TRN2", target_bir_lowering=False)

    dx = nc.dram_tensor("x", [D, T], F32, kind="ExternalInput")
    dy = nc.dram_tensor("y", [D, T], F32, kind="ExternalOutput")
    din = {}
    def di(name, shape):
        din[name] = nc.dram_tensor(name, shape, F32, kind="ExternalInput")
        return din[name]
    di("wq_s", [KC, 128, KC * 128]); di("wu_s", [KC, 128, KC * 128])
    di("wo_s", [KC, 128, KC * 128])
    di("wk_r", [KC, 128, D]); di("wv_r", [KC, 128, D])
    di("wrtr", [128, KC * 8])
    di("wsg_s", [MC_DS, 128, KC * 128]); di("wsu_s", [MC_DS, 128, KC * 128])
    di("wsd_s", [KC, 128, MC_DS * 128])
    di("weg_s", [8, MC_DE, 128, KC * 128]); di("weu_s", [8, MC_DE, 128, KC * 128])
    di("wed_s", [8, KC, 128, MC_DE * 128])
    di("l1q_s", [KC, 128, KC * 128]); di("l1k_s", [128, KC * 64])
    di("l1v_r", [KC, 128, 64]); di("l1d_s", [KC, 128, KC * 128])
    di("l1g_s", [MC_DI, 128, KC * 128]); di("l1u_s", [MC_DI, 128, KC * 128])
    di("l1dn_s", [KC, 128, MC_DI * 128])
    di("l1gsc", [128, MC_DI]); di("l1usc", [128, MC_DI]); di("l1dsc", [128, KC])
    di("c_bias", [128, 4])
    di("c_ident", [128, 128]); di("c_perm", [128, 128]); di("c_tri", [128, 128])
    di("c_ones", [128, 128]); di("c_ind8", [8, 8 * 128])
    di("rope_cos", [128, T]); di("rope_sin", [128, T])
    di("c_dmask", [128, KC * 128])

    kv_part = nc.dram_tensor("kv_part", [4, 128, 512], F32)
    kv_sum = nc.dram_tensor("kv_sum", [4, 128, 512], F32)
    l1kv_loc = nc.dram_tensor("l1kv_loc", [2, 64, T], BF16)
    l1kv_all = nc.dram_tensor("l1kv_all", [2, 2, 64, T], BF16)

    a = nc.alloc_sbuf_tensor
    RALL = a("RALL", [128, KC * T], F32)
    XBALL = a("XBALL", [128, KC * T], BF16)
    AALL = a("AALL", [128, KC * T], BF16)
    BALL = a("BALL", [128, KC * T], BF16)
    R = [RALL[:, i * T:(i + 1) * T] for i in range(KC)]
    XB = [XBALL[:, i * T:(i + 1) * T] for i in range(KC)]
    A = [AALL[:, i * T:(i + 1) * T] for i in range(KC)]
    B = [BALL[:, i * T:(i + 1) * T] for i in range(KC)]
    WB = [a(f"WB{j}", [128, 2048], BF16) for j in range(WStream.NB)]
    F8 = [a(f"F8{j}", [128, 2048], FP8) for j in range(2)]
    S32 = [a(f"S32{j}", [128, T], F32) for j in range(2)]
    ROWS = a("ROWS", [128, T], F32)
    ROWSB = a("ROWSB", [128, T], BF16)
    cos_t = a("cos_t", [128, T], F32)
    sin_t = a("sin_t", [128, T], F32)
    ident = a("ident", [128, 128], F32)
    perm = a("perm", [128, 128], F32)
    tri = a("tri", [128, 128], BF16)
    onesf = a("onesf", [128, 128], F32)
    onesb = a("onesb", [128, 128], BF16)
    CB = a("CB", [128, 4], F32)
    ind8 = a("ind8", [8, 8 * 128], BF16)
    rtr = a("rtr", [128, KC * 8], BF16)
    dmask = a("dmask", [128, KC * 128], BF16)
    SMX = a("SMX", [128, 48], F32)
    l1sc = a("l1sc", [128, 2 * MC_DI + KC], F32)
    psum = [nc.alloc_psum_tensor(f"PS{u}", [128, 1024], F32) for u in range(4)]

    P = Plan(nc)
    PP = PsumPool(P, psum)
    LN_XB = {"v": None}
    for s in ("sX", "sXG", "sXG2", "sPE", "sDVE", "sACT", "sGP", "sGPC",
              "sCC", "sOUT", "sSP", "sH", "sF80", "sF81"):
        P.sem(s)
    WS = WStream(P, WB)

    xsrc0 = bass.AP(dx, 0, [[T, 128], [128 * T, 2], [1, T]])
    xsrc1 = bass.AP(dx, 2 * 128 * T, [[T, 128], [128 * T, KC - 2], [1, T]])
    P.op("sp", "dma_start", out=RALL[:, :],
         in_=bass.AP(dx, 0, [[T, 128], [128 * T, KC], [1, T]]), inc=[("sX", 16)])
    P.op("gp", "dma_start", out=XBALL[:, 0:2 * T], in_=xsrc0, inc=[("sXG2", 16)])
    P.op("gp", "dma_start", out=XBALL[:, 2 * T:], in_=xsrc1, inc=[("sXG", 16)])
    for dst, src in ((cos_t, din["rope_cos"]), (sin_t, din["rope_sin"]),
                     (ident, din["c_ident"]), (perm, din["c_perm"]),
                     (onesf, din["c_ones"]), (CB, din["c_bias"])):
        P.op("sp", "dma_start", out=dst[:, :], in_=src[:, :], inc=[("sX", 16)])
    for dst, src in ((tri, din["c_tri"]), (onesb, din["c_ones"][:, :]),
                     (ind8, din["c_ind8"]), (rtr, din["wrtr"]),
                     (dmask, din["c_dmask"])):
        P.op("gp", "dma_start", out=dst[:, :], in_=src[:, :], inc=[("sXG", 16)])
    X_READY = P.n("sX")
    XG_READY = P.n("sXG")

    for e in ("dve", "act"):
        P.wait(e, "sX", X_READY)
        P.wait(e, "sXG", XG_READY)
    P.wait("pe", "sXG2", 16)

    state = {"u": 0, "avoid": set()}

    def next_unit():
        while True:
            u = state["u"]
            state["u"] = (u + 1) % 4
            if u not in state["avoid"]:
                return u

    def proj(src_chunks, stripe_ids, kcw, evac, kr=0):
        for m, sid in enumerate(stripe_ids):
            buf, wsem, wval = WS.consume(sid)
            u = next_unit()
            pt = PP.acquire("pe", u)
            P.wait("pe", wsem, wval)
            for k in range(kcw):
                lhs = buf[:, k * 128:(k + 1) * 128]
                for h in range(2):
                    n0 = h * 512
                    P.op("pe", "matmul", pt[:, n0:n0 + 512], lhs,
                         src_chunks[kr + k][:, n0:n0 + 512],
                         start=(k == 0), stop=(k == kcw - 1),
                         inc=[("sPE", 1)] if (k == kcw - 1 and h == 1) else ())
            pv = P.n("sPE")
            WS.done(sid, "sPE", pv)
            sem, val = evac(m, pt, pv)
            PP.release(u, sem, val)

    def l0_attn():
        kv_sids = [[[WS.register(
                        bass.AP(w, 4 * k4 * 128 * 2048 + hg * 512,
                                [[2048, 128], [128 * 2048, 4], [1, 512]]), 2048)
                     for k4 in range(KC // 4)]
                    for w in (din["wk_r"], din["wv_r"])] for hg in range(4)]
        q_sids = [WS.register(din["wq_s"][m, :, :], 2048) for m in range(KC)]
        u_sids = [WS.register(din["wu_s"][m, :, :], 2048) for m in range(KC)]
        o_sids = [WS.register(din["wo_s"][m, :, :], 2048) for m in range(KC)]

        gp_kv_dma = 0
        for hg in range(4):
            CK = [B[t // 2][:, (t % 2) * 512:(t % 2 + 1) * 512] for t in range(8)]
            CV = [B[4 + t // 2][:, (t % 2) * 512:(t % 2 + 1) * 512] for t in range(8)]
            for which, CC, actf in ((0, CK, AF.Relu), (1, CV, AF.Copy)):
                units = [next_unit() for _ in range(4)]
                for uu in units:
                    PP.acquire("pe", uu)
                for k4 in range(KC // 4):
                    sid = kv_sids[hg][which][k4]
                    buf, wsem, wval = WS.consume(sid)
                    P.wait("pe", wsem, wval)
                    if hg == 0 and which == 0 and k4 == 0:
                        P.wait("pe", "sXG", XG_READY)
                    for kk in range(4):
                        k = 4 * k4 + kk
                        for t in range(8):
                            pt = psum[units[t % 4]]
                            half = (t // 4) * 512
                            P.op("pe", "matmul", pt[:, half:half + 512],
                                 XB[k][:, t * 128:(t + 1) * 128],
                                 buf[:, kk * 512:kk * 512 + 512],
                                 start=(k == 0), stop=(k == KC - 1),
                                 inc=[("sPE", 1)] if t == 7 else ())
                    WS.done(sid, "sPE", P.n("sPE"))
                pv = P.n("sPE")
                P.wait("act", "sPE", pv)
                for t in range(8):
                    pt = psum[units[t % 4]]
                    half = (t // 4) * 512
                    P.op("act", "activation", CC[t], pt[:, half:half + 512], actf,
                         scale=(1.0 / L0_SCALE) if which == 0 else 1.0,
                         inc=[("sACT", 1)])
                av = P.n("sACT")
                for uu in units:
                    PP.release(uu, "sACT", av)
            u = next_unit()
            pt = PP.acquire("pe", u)
            P.wait("pe", "sACT", P.n("sACT"))
            for hh in range(4):
                hc = slice(hh * 128, (hh + 1) * 128)
                for t in range(8):
                    P.op("pe", "matmul", pt[:, hc], CK[t][:, hc], CV[t][:, hc],
                         start=(t == 0), stop=(t == 7),
                         inc=[("sPE", 1)] if (hh == 3 and t == 7) else ())
            pv = P.n("sPE")
            slot = S32[hg % 2]
            P.wait("act", "sPE", pv)
            if hg >= 2:
                P.wait("act", "sH", 16 * hg)
            P.op("act", "activation", slot[:, 0:512], pt[:, 0:512], AF.Copy,
                 inc=[("sACT", 1)])
            av = P.n("sACT")
            PP.release(u, "sACT", av)
            P.wait("sp", "sACT", av)
            P.op("sp", "dma_start", out=kv_part[hg, :, :], in_=slot[:, 0:512],
                 inc=[("sH", 16)])
            gp_kv_dma = P.n("sH")
        groups = [[2 * i, 2 * i + 1] for i in range(max(1, ncores // 2))]
        P.wait("gp", "sH", gp_kv_dma)
        P.op("gp", "collective_compute", "AllReduce", ALU.add,
             replica_groups=groups, ins=[kv_part.ap().opt()],
             outs=[kv_sum.ap().opt()], inc=[("sCC", 1)])
        CC_DONE = P.n("sCC")

        def evq(m, pt, pv):
            P.wait("act", "sPE", pv)
            P.op("act", "activation", A[m][:, 0:512], pt[:, 0:512], AF.Relu,
                 scale=1.0 / L0_SCALE)
            P.op("act", "activation", A[m][:, 512:1024], pt[:, 512:1024], AF.Relu,
                 scale=1.0 / L0_SCALE, inc=[("sACT", 1)])
            return ("sACT", P.n("sACT"))

        def evu(m, pt, pv):
            P.wait("act", "sPE", pv)
            P.op("act", "activation", B[m][:, 0:512], pt[:, 0:512], AF.Silu)
            P.op("act", "activation", B[m][:, 512:1024], pt[:, 512:1024], AF.Silu,
                 inc=[("sACT", 1)])
            return ("sACT", P.n("sACT"))

        proj(XB, q_sids, KC, evq)
        proj(XB, u_sids, KC, evu)
        ACT_QU = P.n("sACT")
        PE_PROJ = P.n("sPE")

        P.wait("gp", "sCC", CC_DONE)
        P.wait("gp", "sPE", PE_PROJ)
        for hg in range(4):
            P.op("gp", "dma_start",
                 out=XB[hg // 2][:, (hg % 2) * 512:(hg % 2 + 1) * 512],
                 in_=kv_sum[hg, :, :], inc=[("sGP", 16)])
        KVRDY = P.n("sGP")

        P.wait("pe", "sGP", KVRDY)
        P.wait("pe", "sACT", ACT_QU)
        P.wait("dve", "sACT", ACT_QU)
        P.wait("act", "sDVE", P.n("sDVE"))
        UO = [0, 1]
        UCS, UBC = 2, 3
        st_ = {}

        def srms_o(h):
            kvt = XB[h // 8][:, (h % 8) * 128:(h % 8 + 1) * 128]
            u = UO[h % 2]
            pt = PP.acquire("pe", u)
            P.op("pe", "matmul", pt[:, 0:512], kvt, A[h][:, 0:512],
                 start=True, stop=True)
            P.op("pe", "matmul", pt[:, 512:1024], kvt, A[h][:, 512:1024],
                 start=True, stop=True, inc=[("sPE", 1)])
            st_[("opv", h)] = P.n("sPE")

        def srms_stats(h):
            u = UO[h % 2]
            pt = psum[u]
            P.wait("act", "sPE", st_[("opv", h)])
            P.wait("act", "sDVE", st_.get(("tt1", h - 2), 0))
            P.wait("act", "sPE", st_.get(("cs", h - 2), 0))
            P.op("act", "activation", S32[h % 2][:, :], pt[:, :], AF.Copy,
                 inc=[("sACT", 1)])
            P.op("act", "activation", XB[3 + h % 2][:, :], pt[:, :], AF.Square,
                 inc=[("sACT", 1)])
            av = P.n("sACT")
            st_[("ev", h)] = av
            PP.release(u, "sACT", av)
            ptc = PP.acquire("pe", UCS)
            P.wait("pe", "sACT", av)
            lane = 32 + (h % 2) * 32
            P.op("pe", "matmul", ptc[lane:lane + 1, 0:512], onesb[:, 0:1],
                 XB[3 + h % 2][:, 0:512], start=True, stop=True)
            P.op("pe", "matmul", ptc[lane:lane + 1, 512:1024], onesb[:, 0:1],
                 XB[3 + h % 2][:, 512:1024], start=True, stop=True,
                 inc=[("sPE", 1)])
            st_[("cs", h)] = P.n("sPE")
            P.wait("dve", "sPE", st_[("cs", h)])
            P.wait("dve", "sACT", st_.get(("rstd", h - 1), 0))
            P.op("dve", "tensor_scalar_max", ROWS[lane:lane + 1, :],
                 ptc[lane:lane + 1, :], 1.2829695e-22, inc=[("sDVE", 1)])
            dv = P.n("sDVE")
            PP.release(UCS, "sDVE", dv)
            P.wait("act", "sDVE", dv)
            P.op("act", "activation", ROWS[lane:lane + 1, :],
                 ROWS[lane:lane + 1, :], AF.Ln, inc=[("sACT", 1)])
            P.wait("act", "sACT", P.n("sACT"))
            P.op("act", "activation", ROWSB[lane:lane + 1, :],
                 ROWS[lane:lane + 1, :], AF.Exp, scale=-0.5,
                 bias=CB[lane:lane + 1, 0:1], inc=[("sACT", 1)])
            st_[("rstd", h)] = P.n("sACT")

        def srms_mul(h):
            lane = 32 + (h % 2) * 32
            ptb = PP.acquire("pe", UBC)
            P.wait("pe", "sACT", st_[("rstd", h)])
            P.op("pe", "matmul", ptb[:, 0:512], onesb[lane:lane + 1, 0:128],
                 ROWSB[lane:lane + 1, 0:512], start=True, stop=True)
            P.op("pe", "matmul", ptb[:, 512:1024], onesb[lane:lane + 1, 0:128],
                 ROWSB[lane:lane + 1, 512:1024], start=True, stop=True,
                 inc=[("sPE", 1)])
            pv3 = P.n("sPE")
            P.wait("dve", "sPE", pv3)
            P.wait("dve", "sDVE", st_.get(("tt2", h - 1), 0))
            P.op("dve", "tensor_mul", XB[2][:, :], S32[h % 2][:, :], ptb[:, :],
                 inc=[("sDVE", 1)])
            st_[("tt1", h)] = P.n("sDVE")
            PP.release(UBC, "sDVE", st_[("tt1", h)])
            P.wait("dve", "sDVE", st_[("tt1", h)])
            P.op("dve", "tensor_mul", B[h][:, :], XB[2][:, :], B[h][:, :],
                 inc=[("sDVE", 1)])
            st_[("tt2", h)] = P.n("sDVE")

        for h in range(L0_HEADS + 2):
            if h < L0_HEADS:
                srms_o(h)
            if 1 <= h <= L0_HEADS:
                srms_stats(h - 1)
            if h >= 2:
                srms_mul(h - 2)

        DVE_O = P.n("sDVE")
        P.wait("pe", "sDVE", DVE_O)
        P.wait("dve", "sX", X_READY)

        def evo(m, pt, pv):
            P.wait("dve", "sPE", pv)
            P.op("dve", "tensor_add", R[m][:, 0:512], R[m][:, 0:512], pt[:, 0:512])
            P.op("dve", "tensor_add", R[m][:, 512:1024], R[m][:, 512:1024],
                 pt[:, 512:1024], inc=[("sDVE", 1)])
            val = P.n("sDVE")
            ln_feed(m, "sDVE", val, pv)
            return ("sDVE", val)

        ln_begin(A[0], A[1])
        proj(B, o_sids, KC, evo)
        ln_finish()

    LNC = {"pend": []}

    def ln_begin(sq0, sq1):
        u = next_unit()
        pt = PP.acquire("pe", u)
        u2 = next_unit()
        pt2 = PP.acquire("pe", u2)
        state["avoid"] = {u, u2}
        LNC.update(u=u, u2=u2, pt=pt, pt2=pt2, sq=(sq0, sq1), cs={}, pend=[])

    def ln_chunk(m, rsem, rval, xbfree):
        pt, pt2 = LNC["pt"], LNC["pt2"]
        sq = LNC["sq"][m % 2]
        P.wait("act", rsem, rval)
        P.wait("act", "sPE", xbfree)
        P.op("act", "activation", XB[m][:, :], R[m][:, :], AF.Copy,
             inc=[("sACT", 1)])
        av = P.n("sACT")
        P.wait("dve", "sACT", av)
        P.wait("dve", "sPE", LNC["cs"].get(m - 2, 0))
        P.op("dve", "tensor_mul", sq[:, :], XB[m][:, :], XB[m][:, :],
             inc=[("sDVE", 1)])
        dv = P.n("sDVE")
        P.wait("pe", "sACT", av)
        for hh in range(2):
            n0 = hh * 512
            P.op("pe", "matmul", pt[32:33, n0:n0 + 512], onesb[:, 0:1],
                 XB[m][:, n0:n0 + 512], start=(m == 0), stop=(m == KC - 1))
        P.wait("pe", "sDVE", dv)
        for hh in range(2):
            n0 = hh * 512
            P.op("pe", "matmul", pt2[32:33, n0:n0 + 512], onesb[:, 0:1],
                 sq[:, n0:n0 + 512], start=(m == 0), stop=(m == KC - 1),
                 inc=[("sPE", 1)] if (m == KC - 1 and hh == 1) else ())
        LNC["cs"][m] = P.n("sPE")

    def ln_feed(m, rsem, rval, xbfree):
        LNC["pend"].append((m, rsem, rval, xbfree))
        if len(LNC["pend"]) > 1:
            ln_chunk(*LNC["pend"].pop(0))

    def ln_finish(final=False):
        for e_ in LNC["pend"]:
            ln_chunk(*e_)
        LNC["pend"] = []
        state["avoid"] = set()
        u, u2, pt, pt2 = LNC["u"], LNC["u2"], LNC["pt"], LNC["pt2"]
        pv = P.n("sPE")
        P.wait("act", "sPE", pv)
        P.op("act", "activation", ROWS[32:33, :], pt[32:33, :], AF.Copy,
             scale=1.0 / 2048.0, inc=[("sACT", 1)])
        P.op("act", "activation", S32[0][32:33, :], pt2[32:33, :], AF.Copy,
             scale=1.0 / 2048.0, inc=[("sACT", 1)])
        av = P.n("sACT")
        PP.release(u, "sACT", av)
        PP.release(u2, "sACT", av)
        P.wait("dve", "sACT", av)
        P.wait("dve", "sDVE", P.n("sDVE"))
        P.op("dve", "tensor_mul", S32[1][32:33, :], ROWS[32:33, :],
             ROWS[32:33, :], inc=[("sDVE", 1)])
        P.wait("dve", "sDVE", P.n("sDVE"))
        P.op("dve", "tensor_sub", S32[1][32:33, :], S32[0][32:33, :],
             S32[1][32:33, :], inc=[("sDVE", 1)])
        dv = P.n("sDVE")
        P.wait("act", "sDVE", dv)
        P.op("act", "activation", S32[1][32:33, :], S32[1][32:33, :], AF.Ln,
             bias=CB[32:33, 1:2], inc=[("sACT", 1)])
        P.wait("act", "sACT", P.n("sACT"))
        P.op("act", "activation", S32[1][32:33, :], S32[1][32:33, :], AF.Exp,
             scale=-0.5, inc=[("sACT", 1)])
        av2 = P.n("sACT")
        u3 = next_unit()
        ptm = PP.acquire("pe", u3)
        u4 = next_unit()
        ptr = PP.acquire("pe", u4)
        P.wait("pe", "sACT", av2)
        for hh in range(2):
            n0 = hh * 512
            P.op("pe", "matmul", ptm[:, n0:n0 + 512], onesf[32:33, :],
                 ROWS[32:33, n0:n0 + 512], start=True, stop=True)
            P.op("pe", "matmul", ptr[:, n0:n0 + 512], onesf[32:33, :],
                 S32[1][32:33, n0:n0 + 512], start=True, stop=True,
                 inc=[("sPE", 1)] if hh == 1 else ())
        pv2 = P.n("sPE")
        xbv = []
        P.wait("dve", "sPE", pv2)
        for m in range(KC):
            if m > 0:
                P.wait("dve", "sDVE", P.n("sDVE"))
            P.op("dve", "tensor_sub", S32[0][:, :], R[m][:, :], ptm[:, :],
                 inc=[("sDVE", 1)])
            P.wait("dve", "sDVE", P.n("sDVE"))
            P.op("dve", "tensor_mul", R[m][:, :], S32[0][:, :], ptr[:, :],
                 inc=[("sDVE", 1)])
            dv3 = P.n("sDVE")
            if final:
                P.wait("sp", "sDVE", dv3)
                P.op("sp", "dma_start", out=bass.AP(dy, m * 128 * T,
                     [[T, 128], [1, T]]), in_=R[m][:, :], inc=[("sOUT", 16)])
            else:
                P.wait("act", "sDVE", dv3)
                P.op("act", "activation", XB[m][:, :], R[m][:, :], AF.Copy,
                     inc=[("sACT", 1)])
                xbv.append(P.n("sACT"))
        if not final:
            LN_XB["v"] = xbv
        PP.release(u3, "sDVE", P.n("sDVE"))
        PP.release(u4, "sDVE", P.n("sDVE"))

    def l0_ffn():
        P.wait("dve", "sDVE", P.n("sDVE"))
        P.wait("act", "sACT", P.n("sACT"))
        xbr = LN_XB["v"]
        sh_gu = []
        for m in range(MC_DS):
            sh_gu.append(WS.register(din["wsg_s"][m, :, :], 2048))
            sh_gu.append(WS.register(din["wsu_s"][m, :, :], 2048))
        sh_dn = []
        for m in range(KC):
            sh_dn.append((WS.register(din["wsd_s"][m, :, 0:2048], 2048), 16, 0))
            sh_dn.append((WS.register(din["wsd_s"][m, :, 2048:2816], 768), 6, 16))
        ex_gu, ex_dn = [], []
        for e in range(8):
            gu = []
            for m in range(MC_DE):
                gu.append(WS.register(din["weg_s"][e, m, :, :], 2048))
                gu.append(WS.register(din["weu_s"][e, m, :, :], 2048))
            ex_gu.append(gu)
            ex_dn.append([WS.register(din["wed_s"][e, m, :, 0:MC_DE * 128],
                                      MC_DE * 128) for m in range(KC)])

        u = next_unit()
        pt = PP.acquire("pe", u)
        for k in range(KC):
            if xbr is not None:
                P.wait("pe", "sACT", xbr[k])
            lhs = rtr[:, k * 8:(k + 1) * 8]
            for h in range(2):
                n0 = h * 512
                P.op("pe", "matmul", pt[0:8, n0:n0 + 512], lhs,
                     XB[k][:, n0:n0 + 512], start=(k == 0), stop=(k == KC - 1),
                     inc=[("sPE", 1)] if (k == KC - 1 and h == 1) else ())
        pv = P.n("sPE")
        P.wait("act", "sPE", pv)
        P.op("act", "activation", ROWS[0:8, :], pt[0:8, :], AF.Copy,
             inc=[("sACT", 1)])
        LOGV = P.n("sACT")
        PP.release(u, "sACT", LOGV)


        r_last = {}

        def gateup(gu_sids, mc, hpool, wb=None):
            for m in range(mc):
                bufg, sg, vg = WS.consume(gu_sids[2 * m])
                ug = next_unit()
                ptg = PP.acquire("pe", ug)
                P.wait("pe", sg, vg)
                for k in range(KC):
                    lhs = bufg[:, k * 128:(k + 1) * 128]
                    for h in range(2):
                        n0 = h * 512
                        P.op("pe", "matmul", ptg[:, n0:n0 + 512], lhs,
                             XB[k][:, n0:n0 + 512], start=(k == 0),
                             stop=(k == KC - 1),
                             inc=[("sPE", 1)] if (k == KC - 1 and h == 1) else ())
                pvg = P.n("sPE")
                WS.done(gu_sids[2 * m], "sPE", pvg)
                bufu, su, vu = WS.consume(gu_sids[2 * m + 1])
                uu = next_unit()
                ptu = PP.acquire("pe", uu)
                P.wait("pe", su, vu)
                for k in range(KC):
                    lhs = bufu[:, k * 128:(k + 1) * 128]
                    for h in range(2):
                        n0 = h * 512
                        P.op("pe", "matmul", ptu[:, n0:n0 + 512], lhs,
                             XB[k][:, n0:n0 + 512], start=(k == 0),
                             stop=(k == KC - 1),
                             inc=[("sPE", 1)] if (k == KC - 1 and h == 1) else ())
                pvu = P.n("sPE")
                WS.done(gu_sids[2 * m + 1], "sPE", pvu)
                P.wait("act", "sPE", pvg)
                P.wait("act", "sDVE", r_last.get("gsil", 0))
                P.op("act", "activation", S32[0][:, :], ptg[:, :], AF.Silu,
                     inc=[("sACT", 1)])
                av = P.n("sACT")
                PP.release(ug, "sACT", av)
                P.wait("dve", "sACT", av)
                P.wait("dve", "sPE", pvu)
                if wb is None:
                    P.op("dve", "tensor_mul", hpool[m][:, :], ptu[:, :],
                         S32[0][:, :], inc=[("sDVE", 1)])
                else:
                    P.wait("dve", "sDVE", r_last.get("tt2", 0))
                    P.op("dve", "tensor_mul", wb[1][:, :], ptu[:, :],
                         S32[0][:, :], inc=[("sDVE", 1)])
                dv = P.n("sDVE")
                r_last["gsil"] = dv
                PP.release(uu, "sDVE", dv)
                if wb is not None:
                    P.wait("dve", "sDVE", dv)
                    P.op("dve", "tensor_mul", hpool[m][:, :], wb[1][:, :],
                         wb[0][:, :], inc=[("sDVE", 1)])
                    r_last["tt2"] = P.n("sDVE")

        def down(dn_specs, hsrc, h_ready, feed=False):
            P.wait("pe", "sDVE", h_ready)
            for m, specs in enumerate(dn_specs):
                u = next_unit()
                pt = PP.acquire("pe", u)
                nsp = len(specs)
                for si, (sid, kcw, kr) in enumerate(specs):
                    buf, ws, wv = WS.consume(sid)
                    P.wait("pe", ws, wv)
                    for k in range(kcw):
                        lhs = buf[:, k * 128:(k + 1) * 128]
                        for h in range(2):
                            n0 = h * 512
                            P.op("pe", "matmul", pt[:, n0:n0 + 512], lhs,
                                 hsrc[kr + k][:, n0:n0 + 512],
                                 start=(si == 0 and k == 0),
                                 stop=(si == nsp - 1 and k == kcw - 1),
                                 inc=[("sPE", 1)] if (k == kcw - 1 and h == 1)
                                 else ())
                    WS.done(sid, "sPE", P.n("sPE"))
                pv = P.n("sPE")
                P.wait("dve", "sPE", pv)
                P.wait("dve", "sDVE", r_last.get(m, 0))
                P.op("dve", "tensor_add", R[m][:, 0:512], R[m][:, 0:512],
                     pt[:, 0:512])
                P.op("dve", "tensor_add", R[m][:, 512:1024], R[m][:, 512:1024],
                     pt[:, 512:1024], inc=[("sDVE", 1)])
                r_last[m] = P.n("sDVE")
                if feed:
                    ln_feed(m, "sDVE", r_last[m], pv)
                PP.release(u, "sDVE", r_last[m])

        hsh = B[0:16] + A[0:6]
        gateup(sh_gu, MC_DS, hsh)
        for t in range(8):
            cols = slice(t * 128, (t + 1) * 128)
            u = next_unit()
            pt = PP.acquire("pe", u)
            P.wait("pe", "sACT", LOGV)
            if t > 0:
                P.wait("pe", "sDVE", P.n("sDVE"))
            P.op("pe", "transpose", pt[:, 0:8], ROWS[0:8, cols],
                 ident[0:8, 0:8], inc=[("sPE", 1)])
            pv = P.n("sPE")
            E = SMX
            W = SMX[:, 32:48]
            P.wait("dve", "sPE", pv)
            P.wait("dve", "sACT", P.n("sACT"))
            P.op("dve", "reduce_max", W[:, 8:9], pt[:, 0:8], axis=AX.X,
                 negate=True, inc=[("sDVE", 1)])
            dv = P.n("sDVE")
            P.wait("act", "sDVE", dv)
            P.op("act", "activation", E[:, 0:8], pt[:, 0:8], AF.Exp,
                 bias=W[:, 8:9], inc=[("sACT", 1)])
            av = P.n("sACT")
            PP.release(u, "sACT", av)
            P.wait("dve", "sACT", av)
            P.op("dve", "reduce_sum", W[:, 9:10], E[:, 0:8], axis=AX.X,
                 inc=[("sDVE", 1)])
            dv = P.n("sDVE")
            P.wait("act", "sDVE", dv)
            P.op("act", "activation", W[:, 10:11], W[:, 9:10], AF.Ln,
                 inc=[("sACT", 1)])
            P.wait("act", "sACT", P.n("sACT"))
            P.op("act", "activation", W[:, 10:11], W[:, 10:11], AF.Exp,
                 scale=-1.0, inc=[("sACT", 1)])
            av = P.n("sACT")
            P.wait("dve", "sACT", av)
            P.op("dve", "tensor_single_scalar", E[:, 24:32], E[:, 0:8],
                 W[:, 10:11], ALU.mult, inc=[("sDVE", 1)])
            P.wait("dve", "sDVE", P.n("sDVE"))
            P.op("dve", "reduce_max", W[:, 11:12], E[:, 24:32], axis=AX.X,
                 inc=[("sDVE", 1)])
            P.wait("dve", "sDVE", P.n("sDVE"))
            P.op("dve", "tensor_single_scalar", E[:, 8:16], E[:, 24:32],
                 W[:, 11:12], ALU.is_ge, inc=[("sDVE", 1)])
            P.wait("dve", "sDVE", P.n("sDVE"))
            P.op("dve", "tensor_scalar", E[:, 8:16], E[:, 8:16], -1.0, 1.0,
                 ALU.mult, ALU.add, inc=[("sDVE", 1)])
            P.wait("dve", "sDVE", P.n("sDVE"))
            P.op("dve", "tensor_mul", E[:, 16:24], E[:, 24:32], E[:, 8:16],
                 inc=[("sDVE", 1)])
            P.wait("dve", "sDVE", P.n("sDVE"))
            P.op("dve", "reduce_max", W[:, 12:13], E[:, 16:24], axis=AX.X,
                 inc=[("sDVE", 1)])
            P.wait("dve", "sDVE", P.n("sDVE"))
            P.op("dve", "tensor_single_scalar", E[:, 8:16], E[:, 24:32],
                 W[:, 12:13], ALU.is_ge, inc=[("sDVE", 1)])
            P.wait("dve", "sDVE", P.n("sDVE"))
            P.op("dve", "tensor_mul", W[:, 0:8], E[:, 24:32], E[:, 8:16],
                 inc=[("sDVE", 1)])
            dv = P.n("sDVE")
            u = next_unit()
            pt = PP.acquire("pe", u)
            P.wait("pe", "sDVE", dv)
            P.op("pe", "transpose", pt[0:8, 0:128], W[:, 0:8], ident[:, :],
                 inc=[("sPE", 1)])
            pv = P.n("sPE")
            P.wait("act", "sPE", pv)
            P.op("act", "activation", ROWSB[0:8, cols], pt[0:8, 0:128], AF.Copy,
                 inc=[("sACT", 1)])
            PP.release(u, "sACT", P.n("sACT"))
        WRDY = P.n("sACT")
        down([[sh_dn[2 * m], sh_dn[2 * m + 1]] for m in range(KC)], hsh,
             P.n("sDVE"))

        for e in range(8):
            pool = A if e % 2 == 0 else B
            u = next_unit()
            pt = PP.acquire("pe", u)
            P.wait("pe", "sACT", WRDY)
            for h in range(2):
                n0 = h * 512
                P.op("pe", "matmul", pt[:, n0:n0 + 512],
                     ind8[:, e * 128:(e + 1) * 128], ROWSB[0:8, n0:n0 + 512],
                     start=True, stop=True,
                     inc=[("sPE", 1)] if h == 1 else ())
            pv = P.n("sPE")
            P.wait("act", "sPE", pv)
            P.op("act", "activation", pool[15][:, :], pt[:, :], AF.Copy,
                 inc=[("sACT", 1)])
            av = P.n("sACT")
            PP.release(u, "sACT", av)
            P.wait("dve", "sACT", av)
            gateup(ex_gu[e], MC_DE, pool, wb=(pool[15], pool[12]))
            if e == 7:
                ln_begin(B[13], B[14])
            down([[(ex_dn[e][m], MC_DE, 0)] for m in range(KC)], pool,
                 P.n("sDVE"), feed=(e == 7))

    def l1_attn():
        P.wait("dve", "sDVE", P.n("sDVE"))
        P.wait("act", "sACT", P.n("sACT"))
        P.wait("act", "sDVE", P.n("sDVE"))
        xbr = LN_XB["v"]
        rl = {}
        k_sid = WS.register(din["l1k_s"][:, :], KC * 64)
        q_sids = [WS.register(din["l1q_s"][m, :, :], 2048) for m in range(KC)]
        d_sids = [WS.register(din["l1d_s"][m, :, :], 2048) for m in range(KC)]

        VW = B[2]
        P.wait("gp", "sPE", P.n("sPE"))
        vsrc = bass.AP(din["l1v_r"], 0, [[64, 128], [128 * 64, KC], [1, 64]])
        P.op("gp", "dma_start", out=VW[:, 0:KC * 64], in_=vsrc, inc=[("sGP", 16)])
        VW_GP = P.n("sGP")
        uv = next_unit()
        ptv = PP.acquire("pe", uv)
        P.wait("pe", "sGP", VW_GP)
        for t in range(8):
            for k in range(KC):
                if t == 0 and xbr is not None:
                    P.wait("pe", "sACT", xbr[k])
                P.op("pe", "matmul", ptv[:, t * 64:(t + 1) * 64],
                     XB[k][:, t * 128:(t + 1) * 128], VW[:, k * 64:(k + 1) * 64],
                     start=(k == 0), stop=(k == KC - 1),
                     inc=[("sPE", 1)] if (t == 7 and k == KC - 1) else ())
        pv = P.n("sPE")
        P.wait("act", "sPE", pv)
        P.op("act", "activation", B[1][:, 0:512], ptv[:, 0:512], AF.Copy,
             inc=[("sACT", 1)])
        av = P.n("sACT")
        PP.release(uv, "sACT", av)
        P.wait("sp", "sACT", av)
        P.op("sp", "dma_start", out=l1kv_loc[1].rearrange("p f -> (p f)").rearrange(
            "(p f) -> p f", p=128), in_=B[1][:, 0:512], inc=[("sH", 16)])

        uk = next_unit()
        ptk = PP.acquire("pe", uk)
        bufk, ws, wv = WS.consume(k_sid)
        P.wait("pe", ws, wv)
        for k in range(KC):
            lhs = bufk[:, k * 64:(k + 1) * 64]
            for h in range(2):
                n0 = h * 512
                P.op("pe", "matmul", ptk[0:64, n0:n0 + 512], lhs,
                     XB[k][:, n0:n0 + 512], start=(k == 0), stop=(k == KC - 1),
                     inc=[("sPE", 1)] if (k == KC - 1 and h == 1) else ())
        pv = P.n("sPE")
        WS.done(k_sid, "sPE", pv)

        def rope(rows, src_ps, u_src, dst):
            P.wait("act", "sPE", P.n("sPE"))
            P.wait("act", "sDVE", rl.get("ropeS1", 0))
            P.op("act", "activation", S32[0][rows, :], src_ps[rows, :], AF.Copy,
                 inc=[("sACT", 1)])
            av = P.n("sACT")
            PP.release(u_src, "sACT", av)
            up = next_unit()
            ptp = PP.acquire("pe", up)
            P.wait("pe", "sACT", av)
            n = rows.stop - rows.start
            for h in range(2):
                n0 = h * 512
                P.op("pe", "matmul", ptp[rows, n0:n0 + 512],
                     perm[rows, 0:n], S32[0][rows, n0:n0 + 512],
                     start=True, stop=True,
                     inc=[("sPE", 1)] if h == 1 else ())
            pvp = P.n("sPE")
            P.wait("dve", "sACT", av)
            P.wait("dve", "sDVE", rl.get("ropeS1", 0))
            P.op("dve", "tensor_mul", S32[1][rows, :], S32[0][rows, :],
                 cos_t[rows, :], inc=[("sDVE", 1)])
            P.wait("dve", "sDVE", P.n("sDVE"))
            P.wait("dve", "sPE", pvp)
            P.op("dve", "tensor_mul", S32[0][rows, :], ptp[rows, :],
                 sin_t[rows, :], inc=[("sDVE", 1)])
            rl["ropeS0"] = P.n("sDVE")
            PP.release(up, "sDVE", rl["ropeS0"])
            P.wait("dve", "sDVE", rl["ropeS0"])
            P.op("dve", "tensor_add", dst, S32[1][rows, :], S32[0][rows, :],
                 inc=[("sDVE", 1)])
            rl["ropeS1"] = P.n("sDVE")

        rope(slice(0, 64), ptk, uk, B[0][0:64, :])
        P.wait("sp", "sDVE", rl["ropeS1"])
        P.op("sp", "dma_start", out=l1kv_loc[0].rearrange("p f -> (p f)").rearrange(
            "(p f) -> p f", p=64), in_=B[0][0:64, :], inc=[("sH", 16)])
        GP_KV = P.n("sH")

        groups = [[2 * i, 2 * i + 1] for i in range(max(1, ncores // 2))]
        P.wait("gp", "sH", GP_KV)
        P.op("gp", "collective_compute", "AllGather", ALU.bypass,
             replica_groups=groups, ins=[l1kv_loc.ap().opt()],
             outs=[l1kv_all.ap().opt()], inc=[("sCC", 1)])
        CCV = P.n("sCC")

        KH = [B[4], B[5]]
        VT = [B[6], B[7]]
        PE0 = P.n("sPE")
        P.wait("sp", "sCC", CCV)
        P.wait("sp", "sPE", PE0)
        P.wait("gp", "sPE", PE0)
        for g in range(KC):
            rk = g % 2
            lt = g // 2
            src = l1kv_all[rk, 0].rearrange("p f -> (p f)").rearrange(
                "(p f) -> p f", p=64)[:, lt * 128:(lt + 1) * 128]
            dst = KH[g // 8]
            c0 = (g % 8) * 128
            P.op("sp", "dma_start", out=dst[0:64, c0:c0 + 128], in_=src,
                 inc=[("sH", 16)])
            P.op("sp", "dma_start", out=dst[64:128, c0:c0 + 128], in_=src,
                 inc=[("sH", 16)])
            srcv = l1kv_all[rk, 1].rearrange("p f -> (p f)").rearrange(
                "(p f) -> p f", p=128)[:, lt * 64:(lt + 1) * 64]
            dv_ = VT[g // 8]
            cv = (g % 8) * 65
            P.op("sp", "dma_start", out=dv_[:, cv:cv + 64], in_=srcv,
                 inc=[("sH", 16)])
            P.op("gp", "memset", VT[g // 8][:, cv + 64:cv + 65], 1.0,
                 inc=[("sGPC", 1)])
        KV_GP = P.n("sH")
        KV_GPC = P.n("sGPC")

        for m in range(KC):
            buf, ws, wv = WS.consume(q_sids[m])
            u = next_unit()
            pt = PP.acquire("pe", u)
            P.wait("pe", ws, wv)
            for k in range(KC):
                lhs = buf[:, k * 128:(k + 1) * 128]
                for h in range(2):
                    n0 = h * 512
                    P.op("pe", "matmul", pt[:, n0:n0 + 512], lhs,
                         XB[k][:, n0:n0 + 512], start=(k == 0),
                         stop=(k == KC - 1),
                         inc=[("sPE", 1)] if (k == KC - 1 and h == 1) else ())
            pvq = P.n("sPE")
            WS.done(q_sids[m], "sPE", pvq)
            rope(slice(0, 128), pt, u, A[m][:, :])
        Q_DVE = P.n("sDVE")

        P.wait("pe", "sH", KV_GP)
        P.wait("pe", "sGPC", KV_GPC)
        P.wait("pe", "sDVE", Q_DVE)
        P.wait("act", "sACT", P.n("sACT"))
        P.wait("act", "sDVE", P.n("sDVE"))
        UC0, UC1 = 2, 3
        NPBUF = 6
        PSB = [B[8 + i] for i in range(NPBUF)]
        exp_wait = {i: 0 for i in range(NPBUF)}
        prev_tail = [None]
        for hp in range(L1_NH // 2):
            h0 = 2 * hp
            qch = A[hp]
            pair = {"acq": False}

            def emit_ctx_pair(g, segs, hp=hp, qch=qch, pair=pair):
                if not pair["acq"]:
                    PP.acquire("pe", UC0)
                    PP.acquire("pe", UC1)
                    pair["acq"] = True
                vt65 = VT[g // 8][:, (g % 8) * 65:(g % 8) * 65 + 65]
                for hh in range(2):
                    pb = (2 * g + hh) % NPBUF
                    psb = PSB[pb]
                    P.wait("pe", "sDVE", rl[("mask", pb)])
                    for i, (s0, s1) in enumerate(segs):
                        lastg = (g == (7 if s0 < 512 else KC - 1))
                        if hh == 0:
                            P.op("pe", "matmul", psum[UC0][0:65, s0:s1], vt65,
                                 psb[:, s0:s1], start=(g == 0), stop=lastg,
                                 inc=[("sPE", 1)] if i == len(segs) - 1 else ())
                        else:
                            P.op("pe", "matmul", psum[UC1][64:128, s0:s1],
                                 vt65[:, 0:64], psb[:, s0:s1],
                                 start=(g == 0), stop=lastg)
                            P.op("pe", "matmul", psum[UC1][32:33, s0:s1],
                                 vt65[:, 64:65], psb[:, s0:s1],
                                 start=(g == 0), stop=lastg,
                                 inc=[("sPE", 1)] if i == len(segs) - 1 else ())
                    exp_wait[pb] = P.n("sPE")

            pend = []
            for g in range(KC):
                qs = 128 * (g // 2)
                segs = []
                s0 = qs
                while s0 < 1024:
                    s1 = min(1024, (s0 // 512 + 1) * 512)
                    segs.append((s0, s1))
                    s0 = s1
                for hh in range(2):
                    base = hh * 64
                    rows = slice(base, base + 64)
                    klhs = KH[g // 8][rows, (g % 8) * 128:(g % 8) * 128 + 128]
                    us = hh
                    pts = PP.acquire("pe", us)
                    for i, (s0, s1) in enumerate(segs):
                        P.op("pe", "matmul", pts[:, s0:s1], klhs,
                             qch[rows, s0:s1], start=True, stop=True,
                             inc=[("sPE", 1)] if i == len(segs) - 1 else ())
                    pvs = P.n("sPE")
                    pb = (2 * g + hh) % NPBUF
                    psb = PSB[pb]
                    P.wait("act", "sPE", max(pvs, exp_wait[pb]))
                    P.wait("act", "sDVE", rl.get(("mask", pb), 0))
                    P.op("act", "activation", psb[:, qs:1024], pts[:, qs:1024],
                         AF.Exp, scale=0.125, inc=[("sACT", 1)])
                    ave = P.n("sACT")
                    PP.release(us, "sACT", ave)
                    P.wait("dve", "sACT", ave)
                    P.wait("dve", "sPE", exp_wait[pb])
                    P.op("dve", "tensor_mul", psb[:, qs:qs + 128],
                         psb[:, qs:qs + 128], dmask[:, g * 128:(g + 1) * 128],
                         inc=[("sDVE", 1)])
                    rl[("mask", pb)] = P.n("sDVE")
                if g == 2 and prev_tail[0] is not None:
                    prev_tail[0]()
                pend.append((g, segs))
                if len(pend) > 2:
                    emit_ctx_pair(*pend.pop(0))
            for e_ in pend:
                emit_ctx_pair(*e_)
            ctx_done_val = P.n("sPE")

            ptc0, ptc1 = psum[UC0], psum[UC1]
            P.wait("act", "sPE", ctx_done_val)
            P.wait("act", "sDVE", rl.get("rden", 0))
            for dl, ptd in ((64, ptc0), (32, ptc1)):
                P.op("act", "activation", S32[0][dl:dl + 1, :],
                     ptd[dl:dl + 1, :], AF.Ln, inc=[("sACT", 1)])
                P.wait("act", "sACT", P.n("sACT"))
                P.op("act", "activation", ROWSB[dl:dl + 1, :],
                     S32[0][dl:dl + 1, :], AF.Exp, scale=-1.0,
                     inc=[("sACT", 1)])
            avd = P.n("sACT")

            def tail(hp=hp, qch=qch, ctx_done_val=ctx_done_val, avd=avd,
                     ptc0=ptc0, ptc1=ptc1):
                P.wait("pe", "sACT", avd)
                for hh in range(2):
                    n0 = hh * 512
                    P.op("pe", "matmul", ptc1[0:64, n0:n0 + 512],
                         onesb[64:65, 0:64], ROWSB[64:65, n0:n0 + 512],
                         start=True, stop=True)
                    P.op("pe", "matmul", ptc0[64:128, n0:n0 + 512],
                         onesb[32:33, 0:64], ROWSB[32:33, n0:n0 + 512],
                         start=True, stop=True,
                         inc=[("sPE", 1)] if hh == 1 else ())
                pvr = P.n("sPE")
                P.wait("act", "sPE", pvr)
                P.op("act", "activation", S32[1][0:64, :], ptc1[0:64, :],
                     AF.Copy, inc=[("sACT", 1)])
                P.op("act", "activation", S32[1][64:128, :], ptc0[64:128, :],
                     AF.Copy, inc=[("sACT", 1)])
                avr = P.n("sACT")
                P.wait("dve", "sACT", avr)
                P.wait("dve", "sPE", pvr)
                P.op("dve", "tensor_mul", qch[0:64, :], ptc0[0:64, :],
                     S32[1][0:64, :], inc=[("sDVE", 1)])
                P.wait("dve", "sDVE", P.n("sDVE"))
                P.op("dve", "tensor_mul", qch[64:128, :], ptc1[64:128, :],
                     S32[1][64:128, :], inc=[("sDVE", 1)])
                rl["rden"] = P.n("sDVE")
                PP.release(UC0, "sDVE", rl["rden"])
                PP.release(UC1, "sDVE", rl["rden"])

            prev_tail[0] = tail
        prev_tail[0]()

        CTX_DVE = P.n("sDVE")
        P.wait("pe", "sDVE", CTX_DVE)

        def evd(m, pt, pv):
            P.wait("dve", "sPE", pv)
            P.wait("dve", "sDVE", rl.get(("r", m), 0))
            P.op("dve", "tensor_add", R[m][:, 0:512], R[m][:, 0:512], pt[:, 0:512])
            P.op("dve", "tensor_add", R[m][:, 512:1024], R[m][:, 512:1024],
                 pt[:, 512:1024], inc=[("sDVE", 1)])
            rl[("r", m)] = P.n("sDVE")
            ln_feed(m, "sDVE", rl[("r", m)], pv)
            return ("sDVE", rl[("r", m)])

        ln_begin(B[12], B[13])
        proj(A, d_sids, KC, evd)
        ln_finish()

    def l1_ffn():
        P.wait("dve", "sDVE", P.n("sDVE"))
        P.wait("act", "sACT", P.n("sACT"))
        P.wait("act", "sDVE", P.n("sDVE"))
        xbr = LN_XB["v"]
        rl = {}
        P.op("sp", "dma_start", out=l1sc[:, 0:MC_DI], in_=din["l1gsc"][:, :],
             inc=[("sSP", 16)])
        P.op("sp", "dma_start", out=l1sc[:, MC_DI:2 * MC_DI],
             in_=din["l1usc"][:, :], inc=[("sSP", 16)])
        P.op("sp", "dma_start", out=l1sc[:, 2 * MC_DI:2 * MC_DI + KC],
             in_=din["l1dsc"][:, :], inc=[("sSP", 16)])
        SCV = P.n("sSP")
        P.wait("act", "sSP", SCV)
        P.wait("dve", "sSP", SCV)

        f8idx = [0]
        def reg8(src_ap, ncols):
            s = WS.register(src_ap, ncols, fp8=(F8, ("sF80", "sF81"), f8idx[0]))
            f8idx[0] += 1
            return s

        specs = []
        for kg in range(3):
            gu = []
            for mm_ in range(16):
                m = kg * 16 + mm_
                gu.append((m, reg8(din["l1g_s"][m, :, :], 2048),
                           reg8(din["l1u_s"][m, :, :], 2048)))
            dn = [reg8(din["l1dn_s"][m, :, kg * 2048:(kg + 1) * 2048], 2048)
                  for m in range(KC)]
            specs.append((gu, dn))

        for kg, (gu, dn) in enumerate(specs):
            for mm_, (m, sg, su) in enumerate(gu):
                bufg, ws, wv = WS.consume(sg)
                ug = next_unit()
                ptg = PP.acquire("pe", ug)
                P.wait("pe", ws, wv)
                for k in range(KC):
                    if kg == 0 and mm_ == 0 and xbr is not None:
                        P.wait("pe", "sACT", xbr[k])
                    lhs = bufg[:, k * 128:(k + 1) * 128]
                    for h in range(2):
                        n0 = h * 512
                        P.op("pe", "matmul", ptg[:, n0:n0 + 512], lhs,
                             XB[k][:, n0:n0 + 512], start=(k == 0),
                             stop=(k == KC - 1),
                             inc=[("sPE", 1)] if (k == KC - 1 and h == 1) else ())
                pvg = P.n("sPE")
                WS.done(sg, "sPE", pvg)
                bufu, ws, wv = WS.consume(su)
                uu = next_unit()
                ptu = PP.acquire("pe", uu)
                P.wait("pe", ws, wv)
                for k in range(KC):
                    lhs = bufu[:, k * 128:(k + 1) * 128]
                    for h in range(2):
                        n0 = h * 512
                        P.op("pe", "matmul", ptu[:, n0:n0 + 512], lhs,
                             XB[k][:, n0:n0 + 512], start=(k == 0),
                             stop=(k == KC - 1),
                             inc=[("sPE", 1)] if (k == KC - 1 and h == 1) else ())
                pvu = P.n("sPE")
                WS.done(su, "sPE", pvu)
                P.wait("act", "sPE", pvg)
                P.wait("act", "sDVE", rl.get("gsil", 0))
                P.op("act", "activation", S32[0][:, :], ptg[:, :], AF.Silu,
                     scale=l1sc[:, m:m + 1], inc=[("sACT", 1)])
                av = P.n("sACT")
                PP.release(ug, "sACT", av)
                P.wait("dve", "sACT", av)
                P.wait("dve", "sPE", pvu)
                P.op("dve", "scalar_tensor_tensor", B[mm_][:, :], ptu[:, :],
                     l1sc[:, MC_DI + m:MC_DI + m + 1], S32[0][:, :],
                     ALU.mult, ALU.mult, inc=[("sDVE", 1)])
                rl["gsil"] = P.n("sDVE")
                PP.release(uu, "sDVE", rl["gsil"])
            if kg == 2:
                ln_begin(A[0], A[1])
            H_DVE = P.n("sDVE")
            P.wait("pe", "sDVE", H_DVE)
            for m in range(KC):
                buf, ws, wv = WS.consume(dn[m])
                u = next_unit()
                pt = PP.acquire("pe", u)
                P.wait("pe", ws, wv)
                for k in range(16):
                    lhs = buf[:, k * 128:(k + 1) * 128]
                    for h in range(2):
                        n0 = h * 512
                        P.op("pe", "matmul", pt[:, n0:n0 + 512], lhs,
                             B[k][:, n0:n0 + 512], start=(k == 0),
                             stop=(k == 15),
                             inc=[("sPE", 1)] if (k == 15 and h == 1) else ())
                pv = P.n("sPE")
                WS.done(dn[m], "sPE", pv)
                P.wait("dve", "sPE", pv)
                P.wait("dve", "sDVE", rl.get(("r", m), 0))
                dcol = l1sc[:, 2 * MC_DI + m:2 * MC_DI + m + 1]
                P.op("dve", "scalar_tensor_tensor", R[m][:, 0:512],
                     pt[:, 0:512], dcol, R[m][:, 0:512], ALU.mult, ALU.add)
                P.op("dve", "scalar_tensor_tensor", R[m][:, 512:1024],
                     pt[:, 512:1024], dcol, R[m][:, 512:1024], ALU.mult,
                     ALU.add, inc=[("sDVE", 1)])
                rl[("r", m)] = P.n("sDVE")
                if kg == 2:
                    ln_feed(m, "sDVE", rl[("r", m)], pv)
                PP.release(u, "sDVE", rl[("r", m)])

    l0_attn()
    if stop_after != "l0attn":
        l0_ffn()
        ln_finish()
    if stop_after not in ("l0attn", "l0ffn"):
        l1_attn()
    if stop_after is None:
        l1_ffn()
        ln_finish(final=True)

    if P.n("sOUT") == 0:
        P.wait("sp", "sACT", P.n("sACT"))
        P.wait("sp", "sDVE", P.n("sDVE"))
        ydst = bass.AP(dy, 0, [[T, 128], [128 * T, KC], [1, T]])
        P.op("sp", "dma_start", out=ydst, in_=RALL[:, :], inc=[("sOUT", 16)])
    P.wait("sp", "sOUT", P.n("sOUT"))

    with nc.Block() as block:
        @block.gpsimd
        def _(g):
            P.emit("gp", g)

        @block.tensor
        def _(t):
            P.emit("pe", t)

        @block.vector
        def _(v):
            P.emit("dve", v)

        @block.scalar
        def _(s):
            P.emit("act", s)

        @block.sync
        def _(s):
            P.emit("sp", s)

    return nc


def kernel(**inputs):
    in_maps = prepare_inputs(inputs, NCORES)
    nc = build_nc(NCORES)
    res = run_bass_kernel_spmd(nc, in_maps, core_ids=list(range(NCORES)))
    return gather_output(res.results, NCORES)


# version 10
# speedup vs baseline: 1.2091x; 1.0009x over previous
import math

import numpy as np

import concourse.bass as bass
import concourse.mybir as mybir
from concourse.bass_utils import run_bass_kernel_spmd

F32 = mybir.dt.float32
BF16 = mybir.dt.bfloat16
FP8 = mybir.dt.float8e4
AF = mybir.ActivationFunctionType
ALU = mybir.AluOpType
AX = mybir.AxisListType

D = 2048
KC = 16
T = 1024
NCORES = 8
L0_HEADS = 16
L0_SCALE = float(np.sqrt(128.0))
L1_NH, L1_HD = 32, 64
DS = 2816
DE = 1408
DI = 6144
MC_DS = DS // 128
MC_DE = DE // 128
MC_DI = DI // 128

STOP_AFTER = None



def _mstripes(w_io):
    K, M = w_io.shape
    kc, mc = K // 128, M // 128
    t = w_io.reshape(kc, 128, mc, 128).transpose(2, 1, 0, 3).reshape(mc, 128, kc * 128)
    return np.ascontiguousarray(t)


def _rowstripes(w_io):
    K, M = w_io.shape
    return np.ascontiguousarray(w_io.reshape(K // 128, 128, M))


def _rope_tables(positions):
    import jax
    import jax.numpy as jnp
    with jax.default_device(jax.devices("cpu")[0]):
        inv_freq = 1.0 / 10000 ** (jnp.arange(0, L1_HD, 2, dtype=jnp.float32) / L1_HD)
        pos = jnp.asarray(positions, dtype=jnp.float32)
        freqs = pos[:, None] * inv_freq[None, :]
        cos = np.asarray(jnp.cos(freqs)).T
        sin = np.asarray(jnp.sin(freqs)).T
    cos64 = np.concatenate([cos, cos], axis=0)
    sin64 = np.concatenate([-sin, sin], axis=0)
    cos128 = np.concatenate([cos64, cos64], axis=0)
    sin128 = np.concatenate([sin64, sin64], axis=0)
    return np.ascontiguousarray(cos128), np.ascontiguousarray(sin128)


def _consts():
    ident = np.eye(128, dtype=np.float32)
    perm = np.zeros((128, 128), dtype=np.float32)
    for m in range(128):
        p = (m // 64) * 64 + ((m % 64) + 32) % 64
        perm[p, m] = 1.0
    tri = np.triu(np.ones((128, 128), dtype=np.float32)).astype(np.float32)
    ones = np.ones((128, 128), dtype=np.float32)
    ind8 = np.zeros((8, 8 * 128), dtype=np.float32)
    for e in range(8):
        ind8[e, e * 128:(e + 1) * 128] = 1.0
    return ident, perm, tri, ones, ind8


def prepare_inputs(inputs, ncores=NCORES):
    f = {k: np.asarray(v) for k, v in inputs.items()}
    shared = {}
    shared["wq_s"] = _mstripes(np.ascontiguousarray(f["L0_attn_wq"].T))
    shared["wu_s"] = _mstripes(np.ascontiguousarray(f["L0_attn_wu"].T))
    shared["wo_s"] = _mstripes(np.ascontiguousarray(f["L0_attn_wo"].T))
    shared["wk_r"] = _rowstripes(np.ascontiguousarray(f["L0_attn_wk"].T))
    shared["wv_r"] = _rowstripes(np.ascontiguousarray(f["L0_attn_wv"].T))
    shared["wrtr"] = np.ascontiguousarray(
        f["L0_ffn_W_router"].reshape(KC, 128, 8).transpose(1, 0, 2).reshape(128, KC * 8))
    shared["wsg_s"] = _mstripes(f["L0_ffn_Ws_gate"])
    shared["wsu_s"] = _mstripes(f["L0_ffn_Ws_up"])
    shared["wsd_s"] = _mstripes(f["L0_ffn_Ws_down"])
    shared["weg_s"] = np.ascontiguousarray(
        np.stack([_mstripes(f["L0_ffn_We_gate"][e]) for e in range(8)]))
    shared["weu_s"] = np.ascontiguousarray(
        np.stack([_mstripes(f["L0_ffn_We_up"][e]) for e in range(8)]))
    shared["wed_s"] = np.ascontiguousarray(
        np.stack([_mstripes(f["L0_ffn_We_down"][e]) for e in range(8)]))
    wqkv = f["L1_attn_w_qkv"]
    shared["l1q_s"] = _mstripes(np.ascontiguousarray(wqkv[:, 0:2048]))
    shared["l1k_s"] = np.ascontiguousarray(
        wqkv[:, 2048:2112].reshape(KC, 128, 64).transpose(1, 0, 2).reshape(128, KC * 64))
    shared["l1v_r"] = _rowstripes(np.ascontiguousarray(wqkv[:, 2112:2176]))
    shared["l1d_s"] = _mstripes(f["L1_attn_w_dense"])
    shared["l1g_s"] = _mstripes(f["L1_ffn_gate_w"])
    shared["l1u_s"] = _mstripes(f["L1_ffn_up_w"])
    shared["l1dn_s"] = _mstripes(f["L1_ffn_down_w"])
    shared["l1gsc"] = np.ascontiguousarray(f["L1_ffn_gate_scale"].reshape(MC_DI, 128).T)
    shared["l1usc"] = np.ascontiguousarray(f["L1_ffn_up_scale"].reshape(MC_DI, 128).T)
    shared["l1dsc"] = np.ascontiguousarray(f["L1_ffn_down_scale"].reshape(KC, 128).T)
    ident, perm, tri, ones, ind8 = _consts()
    cb = np.zeros((128, 4), dtype=np.float32)
    cb[:, 0] = 0.5 * math.log(128.0)
    cb[:, 1] = 1e-5
    shared["c_bias"] = cb
    shared["c_ident"] = ident
    shared["c_perm"] = perm
    shared["c_tri"] = tri
    shared["c_ones"] = ones
    shared["c_ind8"] = ind8

    x = f["L0_attn_x"]
    in_maps = []
    for c in range(ncores):
        b, par = c // 2, c % 2
        blocks = [par + 2 * j for j in range(8)]
        tok = np.concatenate([np.arange(g * 128, (g + 1) * 128) for g in blocks])
        m = dict(shared)
        m["x"] = np.ascontiguousarray(x[b, tok, :].T)
        cosq, sinq = _rope_tables(tok)
        m["rope_cos"] = cosq
        m["rope_sin"] = sinq
        dm = np.zeros((128, KC * 128), dtype=np.float32)
        tri = np.triu(np.ones((128, 128), dtype=np.float32))
        for g in range(KC):
            gb = par + 2 * (g // 2)
            if gb == g:
                dm[:, g * 128:(g + 1) * 128] = tri
            elif gb > g:
                dm[:, g * 128:(g + 1) * 128] = 1.0
        m["c_dmask"] = dm
        in_maps.append(m)
    return in_maps


def gather_output(results, ncores=NCORES):
    out = np.empty((4, 2048, 2048), dtype=np.float32)
    for c in range(ncores):
        b, par = c // 2, c % 2
        blocks = [par + 2 * j for j in range(8)]
        tok = np.concatenate([np.arange(g * 128, (g + 1) * 128) for g in blocks])
        out[b, tok, :] = results[c]["y"].T
    return out



ENGS = ("gp", "pe", "dve", "act", "sp")


class Plan:
    def __init__(self, nc):
        self.nc = nc
        self.q = {e: [] for e in ENGS}
        self.cnt = {}
        self.sems = {}

    def sem(self, name):
        if name not in self.sems:
            self.sems[name] = self.nc.alloc_semaphore(name=name)
            self.cnt[name] = 0
        return name

    def wait(self, eng, s, v):
        if v > 0:
            self.q[eng].append(("w", s, int(v)))

    def op(self, eng, fn, *args, inc=(), **kw):
        for s, k in inc:
            self.cnt[s] += k
        self.q[eng].append(("o", fn, args, kw, tuple(inc)))

    def n(self, s):
        return self.cnt.get(s, 0)

    def emit(self, eng, engine):
        for item in self.q[eng]:
            if item[0] == "w":
                engine.wait_ge(self.sems[item[1]], item[2])
            else:
                _, fn, args, kw, inc = item
                inst = getattr(engine, fn)(*args, **kw)
                for s, k in inc:
                    inst = inst.then_inc(self.sems[s], k)


class PsumPool:

    def __init__(self, plan, tensors):
        self.p = plan
        self.t = tensors
        self.free_at = [None] * len(tensors)

    def acquire(self, eng, u):
        fa = self.free_at[u]
        if fa is not None:
            self.p.wait(eng, fa[0], fa[1])
        return self.t[u]

    def release(self, u, sem, val):
        self.free_at[u] = (sem, val)


class WStream:

    NB = 3
    LA = 2

    def __init__(self, plan, bufs):
        self.p = plan
        self.bufs = bufs
        self.sems = [plan.sem(f"sW{j}") for j in range(self.NB)]
        self.stripes = []
        self.loaded = 0
        self.consumed_at = {}
        self.f8_done = {}
        self.nexti = 0

    def register(self, src_ap, ncols, fp8=None):
        self.stripes.append((src_ap, ncols, fp8))
        return len(self.stripes) - 1

    def _plan_load(self):
        k = self.loaded
        if k >= len(self.stripes):
            return
        src, ncols, fp8 = self.stripes[k]
        b = k % self.NB
        if fp8 is not None:
            f8bufs, f8sems, f8idx = fp8
            j = f8idx % 2
            prev = self.f8_done.get(f8idx - 2)
            if prev is not None:
                self.p.wait("gp", prev[0], prev[1])
            self.p.op("gp", "dma_start", out=f8bufs[j][:, 0:ncols], in_=src,
                      inc=[(f8sems[j], 16)])
            self.p.wait("gp", f8sems[j], self.p.n(f8sems[j]))
            if k >= self.NB:
                dep = self.consumed_at.get(k - self.NB)
                assert dep is not None
                self.p.wait("gp", dep[0], dep[1])
            self.p.op("gp", "dma_start", out=self.bufs[b][:, 0:ncols],
                      in_=f8bufs[j][:, 0:ncols], inc=[(self.sems[b], 16)])
            self.f8_done[f8idx] = (self.sems[b], 16 * (k // self.NB + 1))
        else:
            if k >= self.NB:
                dep = self.consumed_at.get(k - self.NB)
                assert dep is not None, f"stripe {k} loads before {k - self.NB} consumed"
                self.p.wait("gp", dep[0], dep[1])
            self.p.op("gp", "dma_start", out=self.bufs[b][:, 0:ncols], in_=src,
                      inc=[(self.sems[b], 16)])
        self.loaded += 1

    def consume(self, k):
        while self.loaded <= min(k + self.LA, len(self.stripes) - 1):
            self._plan_load()
        b = k % self.NB
        return self.bufs[b], self.sems[b], 16 * (k // self.NB + 1)

    def done(self, k, sem, val):
        self.consumed_at[k] = (sem, val)



def build_nc(ncores=NCORES, stop_after=STOP_AFTER):
    nc = bass.Bass("TRN2", target_bir_lowering=False)

    dx = nc.dram_tensor("x", [D, T], F32, kind="ExternalInput")
    dy = nc.dram_tensor("y", [D, T], F32, kind="ExternalOutput")
    din = {}
    def di(name, shape):
        din[name] = nc.dram_tensor(name, shape, F32, kind="ExternalInput")
        return din[name]
    di("wq_s", [KC, 128, KC * 128]); di("wu_s", [KC, 128, KC * 128])
    di("wo_s", [KC, 128, KC * 128])
    di("wk_r", [KC, 128, D]); di("wv_r", [KC, 128, D])
    di("wrtr", [128, KC * 8])
    di("wsg_s", [MC_DS, 128, KC * 128]); di("wsu_s", [MC_DS, 128, KC * 128])
    di("wsd_s", [KC, 128, MC_DS * 128])
    di("weg_s", [8, MC_DE, 128, KC * 128]); di("weu_s", [8, MC_DE, 128, KC * 128])
    di("wed_s", [8, KC, 128, MC_DE * 128])
    di("l1q_s", [KC, 128, KC * 128]); di("l1k_s", [128, KC * 64])
    di("l1v_r", [KC, 128, 64]); di("l1d_s", [KC, 128, KC * 128])
    di("l1g_s", [MC_DI, 128, KC * 128]); di("l1u_s", [MC_DI, 128, KC * 128])
    di("l1dn_s", [KC, 128, MC_DI * 128])
    di("l1gsc", [128, MC_DI]); di("l1usc", [128, MC_DI]); di("l1dsc", [128, KC])
    di("c_bias", [128, 4])
    di("c_ident", [128, 128]); di("c_perm", [128, 128]); di("c_tri", [128, 128])
    di("c_ones", [128, 128]); di("c_ind8", [8, 8 * 128])
    di("rope_cos", [128, T]); di("rope_sin", [128, T])
    di("c_dmask", [128, KC * 128])

    kv_part = nc.dram_tensor("kv_part", [4, 128, 512], F32)
    kv_sum = nc.dram_tensor("kv_sum", [4, 128, 512], F32)
    l1kv_loc = nc.dram_tensor("l1kv_loc", [2, 64, T], BF16)
    l1kv_all = nc.dram_tensor("l1kv_all", [2, 2, 64, T], BF16)

    a = nc.alloc_sbuf_tensor
    RALL = a("RALL", [128, KC * T], F32)
    XBALL = a("XBALL", [128, KC * T], BF16)
    AALL = a("AALL", [128, KC * T], BF16)
    BALL = a("BALL", [128, KC * T], BF16)
    R = [RALL[:, i * T:(i + 1) * T] for i in range(KC)]
    XB = [XBALL[:, i * T:(i + 1) * T] for i in range(KC)]
    A = [AALL[:, i * T:(i + 1) * T] for i in range(KC)]
    B = [BALL[:, i * T:(i + 1) * T] for i in range(KC)]
    WB = [a(f"WB{j}", [128, 2048], BF16) for j in range(WStream.NB)]
    F8 = [a(f"F8{j}", [128, 2048], FP8) for j in range(2)]
    S32 = [a(f"S32{j}", [128, T], F32) for j in range(2)]
    ROWS = a("ROWS", [128, T], F32)
    ROWSB = a("ROWSB", [128, T], BF16)
    cos_t = a("cos_t", [128, T], F32)
    sin_t = a("sin_t", [128, T], F32)
    ident = a("ident", [128, 128], F32)
    perm = a("perm", [128, 128], F32)
    tri = a("tri", [128, 128], BF16)
    onesf = a("onesf", [128, 128], F32)
    onesb = a("onesb", [128, 128], BF16)
    CB = a("CB", [128, 4], F32)
    ind8 = a("ind8", [8, 8 * 128], BF16)
    rtr = a("rtr", [128, KC * 8], BF16)
    dmask = a("dmask", [128, KC * 128], BF16)
    SMX = a("SMX", [128, 48], F32)
    l1sc = a("l1sc", [128, 2 * MC_DI + KC], F32)
    psum = [nc.alloc_psum_tensor(f"PS{u}", [128, 1024], F32) for u in range(4)]

    P = Plan(nc)
    PP = PsumPool(P, psum)
    LN_XB = {"v": None}
    for s in ("sX", "sXG", "sXG2", "sXC", "sPE", "sDVE", "sACT", "sGP",
              "sGPC", "sCC", "sOUT", "sSP", "sH", "sF80", "sF81"):
        P.sem(s)
    WS = WStream(P, WB)

    xsrc0 = bass.AP(dx, 0, [[T, 128], [128 * T, 2], [1, T]])
    xsrc1 = bass.AP(dx, 2 * 128 * T, [[T, 128], [128 * T, KC - 2], [1, T]])
    P.op("sp", "dma_start", out=RALL[:, :],
         in_=bass.AP(dx, 0, [[T, 128], [128 * T, KC], [1, T]]), inc=[("sX", 16)])
    P.op("gp", "dma_start", out=XBALL[:, 0:2 * T], in_=xsrc0, inc=[("sXG2", 16)])
    P.op("gp", "dma_start", out=XBALL[:, 2 * T:], in_=xsrc1, inc=[("sXG", 16)])
    for dst, src in ((cos_t, din["rope_cos"]), (sin_t, din["rope_sin"]),
                     (ident, din["c_ident"]), (perm, din["c_perm"]),
                     (onesf, din["c_ones"]), (CB, din["c_bias"])):
        P.op("sp", "dma_start", out=dst[:, :], in_=src[:, :], inc=[("sX", 16)])
    X_READY = P.n("sX")
    XG_READY = P.n("sXG")

    for e in ("dve", "act"):
        P.wait(e, "sX", X_READY)
        P.wait(e, "sXG", XG_READY)
    P.wait("pe", "sXG2", 16)

    state = {"u": 0, "avoid": set()}

    def next_unit():
        while True:
            u = state["u"]
            state["u"] = (u + 1) % 4
            if u not in state["avoid"]:
                return u

    def proj(src_chunks, stripe_ids, kcw, evac, kr=0):
        for m, sid in enumerate(stripe_ids):
            buf, wsem, wval = WS.consume(sid)
            u = next_unit()
            pt = PP.acquire("pe", u)
            P.wait("pe", wsem, wval)
            for k in range(kcw):
                lhs = buf[:, k * 128:(k + 1) * 128]
                for h in range(2):
                    n0 = h * 512
                    P.op("pe", "matmul", pt[:, n0:n0 + 512], lhs,
                         src_chunks[kr + k][:, n0:n0 + 512],
                         start=(k == 0), stop=(k == kcw - 1),
                         inc=[("sPE", 1)] if (k == kcw - 1 and h == 1) else ())
            pv = P.n("sPE")
            WS.done(sid, "sPE", pv)
            sem, val = evac(m, pt, pv)
            PP.release(u, sem, val)

    def l0_attn():
        kv_sids = [[[WS.register(
                        bass.AP(w, 4 * k4 * 128 * 2048 + hg * 512,
                                [[2048, 128], [128 * 2048, 4], [1, 512]]), 2048)
                     for k4 in range(KC // 4)]
                    for w in (din["wk_r"], din["wv_r"])] for hg in range(4)]
        q_sids = [WS.register(din["wq_s"][m, :, :], 2048) for m in range(KC)]
        u_sids = [WS.register(din["wu_s"][m, :, :], 2048) for m in range(KC)]
        o_sids = [WS.register(din["wo_s"][m, :, :], 2048) for m in range(KC)]
        while WS.loaded < 3:
            WS._plan_load()
        for dst, csrc in ((tri, din["c_tri"]), (onesb, din["c_ones"][:, :]),
                          (ind8, din["c_ind8"]), (rtr, din["wrtr"]),
                          (dmask, din["c_dmask"])):
            P.op("gp", "dma_start", out=dst[:, :], in_=csrc[:, :],
                 inc=[("sXC", 16)])
        XC = P.n("sXC")
        P.wait("dve", "sXC", XC)
        P.wait("act", "sXC", XC)

        gp_kv_dma = 0
        for hg in range(4):
            CK = [B[t // 2][:, (t % 2) * 512:(t % 2 + 1) * 512] for t in range(8)]
            CV = [B[4 + t // 2][:, (t % 2) * 512:(t % 2 + 1) * 512] for t in range(8)]
            for which, CC, actf in ((0, CK, AF.Relu), (1, CV, AF.Copy)):
                units = [next_unit() for _ in range(4)]
                for uu in units:
                    PP.acquire("pe", uu)
                for k4 in range(KC // 4):
                    sid = kv_sids[hg][which][k4]
                    buf, wsem, wval = WS.consume(sid)
                    P.wait("pe", wsem, wval)
                    if hg == 0 and which == 0 and k4 == 0:
                        P.wait("pe", "sXG", XG_READY)
                    for kk in range(4):
                        k = 4 * k4 + kk
                        for t in range(8):
                            pt = psum[units[t % 4]]
                            half = (t // 4) * 512
                            P.op("pe", "matmul", pt[:, half:half + 512],
                                 XB[k][:, t * 128:(t + 1) * 128],
                                 buf[:, kk * 512:kk * 512 + 512],
                                 start=(k == 0), stop=(k == KC - 1),
                                 inc=[("sPE", 1)] if t == 7 else ())
                    WS.done(sid, "sPE", P.n("sPE"))
                pv = P.n("sPE")
                P.wait("act", "sPE", pv)
                for t in range(8):
                    pt = psum[units[t % 4]]
                    half = (t // 4) * 512
                    P.op("act", "activation", CC[t], pt[:, half:half + 512], actf,
                         scale=(1.0 / L0_SCALE) if which == 0 else 1.0,
                         inc=[("sACT", 1)])
                av = P.n("sACT")
                for uu in units:
                    PP.release(uu, "sACT", av)
            u = next_unit()
            pt = PP.acquire("pe", u)
            P.wait("pe", "sACT", P.n("sACT"))
            for hh in range(4):
                hc = slice(hh * 128, (hh + 1) * 128)
                for t in range(8):
                    P.op("pe", "matmul", pt[:, hc], CK[t][:, hc], CV[t][:, hc],
                         start=(t == 0), stop=(t == 7),
                         inc=[("sPE", 1)] if (hh == 3 and t == 7) else ())
            pv = P.n("sPE")
            slot = S32[hg % 2]
            P.wait("act", "sPE", pv)
            if hg >= 2:
                P.wait("act", "sH", 16 * hg)
            P.op("act", "activation", slot[:, 0:512], pt[:, 0:512], AF.Copy,
                 inc=[("sACT", 1)])
            av = P.n("sACT")
            PP.release(u, "sACT", av)
            P.wait("sp", "sACT", av)
            P.op("sp", "dma_start", out=kv_part[hg, :, :], in_=slot[:, 0:512],
                 inc=[("sH", 16)])
            gp_kv_dma = P.n("sH")
        groups = [[2 * i, 2 * i + 1] for i in range(max(1, ncores // 2))]
        P.wait("gp", "sH", gp_kv_dma)
        P.op("gp", "collective_compute", "AllReduce", ALU.add,
             replica_groups=groups, ins=[kv_part.ap().opt()],
             outs=[kv_sum.ap().opt()], inc=[("sCC", 1)])
        CC_DONE = P.n("sCC")

        def evq(m, pt, pv):
            P.wait("act", "sPE", pv)
            P.op("act", "activation", A[m][:, 0:512], pt[:, 0:512], AF.Relu,
                 scale=1.0 / L0_SCALE)
            P.op("act", "activation", A[m][:, 512:1024], pt[:, 512:1024], AF.Relu,
                 scale=1.0 / L0_SCALE, inc=[("sACT", 1)])
            return ("sACT", P.n("sACT"))

        def evu(m, pt, pv):
            P.wait("act", "sPE", pv)
            P.op("act", "activation", B[m][:, 0:512], pt[:, 0:512], AF.Silu)
            P.op("act", "activation", B[m][:, 512:1024], pt[:, 512:1024], AF.Silu,
                 inc=[("sACT", 1)])
            return ("sACT", P.n("sACT"))

        proj(XB, q_sids, KC, evq)
        proj(XB, u_sids, KC, evu)
        ACT_QU = P.n("sACT")
        PE_PROJ = P.n("sPE")

        P.wait("gp", "sCC", CC_DONE)
        P.wait("gp", "sPE", PE_PROJ)
        for hg in range(4):
            P.op("gp", "dma_start",
                 out=XB[hg // 2][:, (hg % 2) * 512:(hg % 2 + 1) * 512],
                 in_=kv_sum[hg, :, :], inc=[("sGP", 16)])
        KVRDY = P.n("sGP")

        P.wait("pe", "sGP", KVRDY)
        P.wait("pe", "sXC", P.n("sXC"))
        P.wait("pe", "sACT", ACT_QU)
        P.wait("dve", "sACT", ACT_QU)
        P.wait("act", "sDVE", P.n("sDVE"))
        UO = [0, 1]
        UCS, UBC = 2, 3
        st_ = {}

        def srms_o(h):
            kvt = XB[h // 8][:, (h % 8) * 128:(h % 8 + 1) * 128]
            u = UO[h % 2]
            pt = PP.acquire("pe", u)
            P.op("pe", "matmul", pt[:, 0:512], kvt, A[h][:, 0:512],
                 start=True, stop=True)
            P.op("pe", "matmul", pt[:, 512:1024], kvt, A[h][:, 512:1024],
                 start=True, stop=True, inc=[("sPE", 1)])
            st_[("opv", h)] = P.n("sPE")

        def srms_stats(h):
            u = UO[h % 2]
            pt = psum[u]
            P.wait("act", "sPE", st_[("opv", h)])
            P.wait("act", "sDVE", st_.get(("tt1", h - 2), 0))
            P.wait("act", "sPE", st_.get(("cs", h - 2), 0))
            P.op("act", "activation", S32[h % 2][:, :], pt[:, :], AF.Copy,
                 inc=[("sACT", 1)])
            P.op("act", "activation", XB[3 + h % 2][:, :], pt[:, :], AF.Square,
                 inc=[("sACT", 1)])
            av = P.n("sACT")
            st_[("ev", h)] = av
            PP.release(u, "sACT", av)
            ptc = PP.acquire("pe", UCS)
            P.wait("pe", "sACT", av)
            lane = 32 + (h % 2) * 32
            P.op("pe", "matmul", ptc[lane:lane + 1, 0:512], onesb[:, 0:1],
                 XB[3 + h % 2][:, 0:512], start=True, stop=True)
            P.op("pe", "matmul", ptc[lane:lane + 1, 512:1024], onesb[:, 0:1],
                 XB[3 + h % 2][:, 512:1024], start=True, stop=True,
                 inc=[("sPE", 1)])
            st_[("cs", h)] = P.n("sPE")
            P.wait("dve", "sPE", st_[("cs", h)])
            P.wait("dve", "sACT", st_.get(("rstd", h - 1), 0))
            P.op("dve", "tensor_scalar_max", ROWS[lane:lane + 1, :],
                 ptc[lane:lane + 1, :], 1.2829695e-22, inc=[("sDVE", 1)])
            dv = P.n("sDVE")
            PP.release(UCS, "sDVE", dv)
            P.wait("act", "sDVE", dv)
            P.op("act", "activation", ROWS[lane:lane + 1, :],
                 ROWS[lane:lane + 1, :], AF.Ln, inc=[("sACT", 1)])
            P.wait("act", "sACT", P.n("sACT"))
            P.op("act", "activation", ROWSB[lane:lane + 1, :],
                 ROWS[lane:lane + 1, :], AF.Exp, scale=-0.5,
                 bias=CB[lane:lane + 1, 0:1], inc=[("sACT", 1)])
            st_[("rstd", h)] = P.n("sACT")

        def srms_mul(h):
            lane = 32 + (h % 2) * 32
            ptb = PP.acquire("pe", UBC)
            P.wait("pe", "sACT", st_[("rstd", h)])
            P.op("pe", "matmul", ptb[:, 0:512], onesb[lane:lane + 1, 0:128],
                 ROWSB[lane:lane + 1, 0:512], start=True, stop=True)
            P.op("pe", "matmul", ptb[:, 512:1024], onesb[lane:lane + 1, 0:128],
                 ROWSB[lane:lane + 1, 512:1024], start=True, stop=True,
                 inc=[("sPE", 1)])
            pv3 = P.n("sPE")
            P.wait("dve", "sPE", pv3)
            P.wait("dve", "sDVE", st_.get(("tt2", h - 1), 0))
            P.op("dve", "tensor_mul", XB[2][:, :], S32[h % 2][:, :], ptb[:, :],
                 inc=[("sDVE", 1)])
            st_[("tt1", h)] = P.n("sDVE")
            PP.release(UBC, "sDVE", st_[("tt1", h)])
            P.wait("dve", "sDVE", st_[("tt1", h)])
            P.op("dve", "tensor_mul", B[h][:, :], XB[2][:, :], B[h][:, :],
                 inc=[("sDVE", 1)])
            st_[("tt2", h)] = P.n("sDVE")

        for h in range(L0_HEADS + 2):
            if h < L0_HEADS:
                srms_o(h)
            if 1 <= h <= L0_HEADS:
                srms_stats(h - 1)
            if h >= 2:
                srms_mul(h - 2)

        DVE_O = P.n("sDVE")
        P.wait("pe", "sDVE", DVE_O)
        P.wait("dve", "sX", X_READY)

        def evo(m, pt, pv):
            P.wait("dve", "sPE", pv)
            P.op("dve", "tensor_add", R[m][:, 0:512], R[m][:, 0:512], pt[:, 0:512])
            P.op("dve", "tensor_add", R[m][:, 512:1024], R[m][:, 512:1024],
                 pt[:, 512:1024], inc=[("sDVE", 1)])
            val = P.n("sDVE")
            ln_feed(m, "sDVE", val, pv)
            return ("sDVE", val)

        ln_begin(A[0], A[1])
        proj(B, o_sids, KC, evo)
        ln_finish()

    LNC = {"pend": []}

    def ln_begin(sq0, sq1):
        u = next_unit()
        pt = PP.acquire("pe", u)
        u2 = next_unit()
        pt2 = PP.acquire("pe", u2)
        state["avoid"] = {u, u2}
        LNC.update(u=u, u2=u2, pt=pt, pt2=pt2, sq=(sq0, sq1), cs={}, pend=[])

    def ln_chunk(m, rsem, rval, xbfree):
        pt, pt2 = LNC["pt"], LNC["pt2"]
        sq = LNC["sq"][m % 2]
        P.wait("act", rsem, rval)
        P.wait("act", "sPE", xbfree)
        P.op("act", "activation", XB[m][:, :], R[m][:, :], AF.Copy,
             inc=[("sACT", 1)])
        av = P.n("sACT")
        P.wait("dve", "sACT", av)
        P.wait("dve", "sPE", LNC["cs"].get(m - 2, 0))
        P.op("dve", "tensor_mul", sq[:, :], XB[m][:, :], XB[m][:, :],
             inc=[("sDVE", 1)])
        dv = P.n("sDVE")
        P.wait("pe", "sACT", av)
        for hh in range(2):
            n0 = hh * 512
            P.op("pe", "matmul", pt[32:33, n0:n0 + 512], onesb[:, 0:1],
                 XB[m][:, n0:n0 + 512], start=(m == 0), stop=(m == KC - 1))
        P.wait("pe", "sDVE", dv)
        for hh in range(2):
            n0 = hh * 512
            P.op("pe", "matmul", pt2[32:33, n0:n0 + 512], onesb[:, 0:1],
                 sq[:, n0:n0 + 512], start=(m == 0), stop=(m == KC - 1),
                 inc=[("sPE", 1)] if (m == KC - 1 and hh == 1) else ())
        LNC["cs"][m] = P.n("sPE")

    def ln_feed(m, rsem, rval, xbfree):
        LNC["pend"].append((m, rsem, rval, xbfree))
        if len(LNC["pend"]) > 1:
            ln_chunk(*LNC["pend"].pop(0))

    def ln_finish(final=False):
        for e_ in LNC["pend"]:
            ln_chunk(*e_)
        LNC["pend"] = []
        state["avoid"] = set()
        u, u2, pt, pt2 = LNC["u"], LNC["u2"], LNC["pt"], LNC["pt2"]
        pv = P.n("sPE")
        P.wait("act", "sPE", pv)
        P.op("act", "activation", ROWS[32:33, :], pt[32:33, :], AF.Copy,
             scale=1.0 / 2048.0, inc=[("sACT", 1)])
        P.op("act", "activation", S32[0][32:33, :], pt2[32:33, :], AF.Copy,
             scale=1.0 / 2048.0, inc=[("sACT", 1)])
        av = P.n("sACT")
        PP.release(u, "sACT", av)
        PP.release(u2, "sACT", av)
        P.wait("dve", "sACT", av)
        P.wait("dve", "sDVE", P.n("sDVE"))
        P.op("dve", "tensor_mul", S32[1][32:33, :], ROWS[32:33, :],
             ROWS[32:33, :], inc=[("sDVE", 1)])
        P.wait("dve", "sDVE", P.n("sDVE"))
        P.op("dve", "tensor_sub", S32[1][32:33, :], S32[0][32:33, :],
             S32[1][32:33, :], inc=[("sDVE", 1)])
        dv = P.n("sDVE")
        P.wait("act", "sDVE", dv)
        P.op("act", "activation", S32[1][32:33, :], S32[1][32:33, :], AF.Ln,
             bias=CB[32:33, 1:2], inc=[("sACT", 1)])
        P.wait("act", "sACT", P.n("sACT"))
        P.op("act", "activation", S32[1][32:33, :], S32[1][32:33, :], AF.Exp,
             scale=-0.5, inc=[("sACT", 1)])
        av2 = P.n("sACT")
        u3 = next_unit()
        ptm = PP.acquire("pe", u3)
        u4 = next_unit()
        ptr = PP.acquire("pe", u4)
        P.wait("pe", "sACT", av2)
        for hh in range(2):
            n0 = hh * 512
            P.op("pe", "matmul", ptm[:, n0:n0 + 512], onesf[32:33, :],
                 ROWS[32:33, n0:n0 + 512], start=True, stop=True)
            P.op("pe", "matmul", ptr[:, n0:n0 + 512], onesf[32:33, :],
                 S32[1][32:33, n0:n0 + 512], start=True, stop=True,
                 inc=[("sPE", 1)] if hh == 1 else ())
        pv2 = P.n("sPE")
        xbv = []
        P.wait("dve", "sPE", pv2)
        for m in range(KC):
            if m > 0:
                P.wait("dve", "sDVE", P.n("sDVE"))
            P.op("dve", "tensor_sub", S32[0][:, :], R[m][:, :], ptm[:, :],
                 inc=[("sDVE", 1)])
            P.wait("dve", "sDVE", P.n("sDVE"))
            P.op("dve", "tensor_mul", R[m][:, :], S32[0][:, :], ptr[:, :],
                 inc=[("sDVE", 1)])
            dv3 = P.n("sDVE")
            if final:
                P.wait("sp", "sDVE", dv3)
                P.op("sp", "dma_start", out=bass.AP(dy, m * 128 * T,
                     [[T, 128], [1, T]]), in_=R[m][:, :], inc=[("sOUT", 16)])
            else:
                P.wait("act", "sDVE", dv3)
                P.op("act", "activation", XB[m][:, :], R[m][:, :], AF.Copy,
                     inc=[("sACT", 1)])
                xbv.append(P.n("sACT"))
        if not final:
            LN_XB["v"] = xbv
        PP.release(u3, "sDVE", P.n("sDVE"))
        PP.release(u4, "sDVE", P.n("sDVE"))

    def l0_ffn():
        P.wait("dve", "sDVE", P.n("sDVE"))
        P.wait("act", "sACT", P.n("sACT"))
        xbr = LN_XB["v"]
        sh_gu = []
        for m in range(MC_DS):
            sh_gu.append(WS.register(din["wsg_s"][m, :, :], 2048))
            sh_gu.append(WS.register(din["wsu_s"][m, :, :], 2048))
        sh_dn = []
        for m in range(KC):
            sh_dn.append((WS.register(din["wsd_s"][m, :, 0:2048], 2048), 16, 0))
            sh_dn.append((WS.register(din["wsd_s"][m, :, 2048:2816], 768), 6, 16))
        ex_gu, ex_dn = [], []
        for e in range(8):
            gu = []
            for m in range(MC_DE):
                gu.append(WS.register(din["weg_s"][e, m, :, :], 2048))
                gu.append(WS.register(din["weu_s"][e, m, :, :], 2048))
            ex_gu.append(gu)
            ex_dn.append([WS.register(din["wed_s"][e, m, :, 0:MC_DE * 128],
                                      MC_DE * 128) for m in range(KC)])

        u = next_unit()
        pt = PP.acquire("pe", u)
        for k in range(KC):
            if xbr is not None:
                P.wait("pe", "sACT", xbr[k])
            lhs = rtr[:, k * 8:(k + 1) * 8]
            for h in range(2):
                n0 = h * 512
                P.op("pe", "matmul", pt[0:8, n0:n0 + 512], lhs,
                     XB[k][:, n0:n0 + 512], start=(k == 0), stop=(k == KC - 1),
                     inc=[("sPE", 1)] if (k == KC - 1 and h == 1) else ())
        pv = P.n("sPE")
        P.wait("act", "sPE", pv)
        P.op("act", "activation", ROWS[0:8, :], pt[0:8, :], AF.Copy,
             inc=[("sACT", 1)])
        LOGV = P.n("sACT")
        PP.release(u, "sACT", LOGV)


        r_last = {}

        def gateup(gu_sids, mc, hpool, wb=None):
            for m in range(mc):
                bufg, sg, vg = WS.consume(gu_sids[2 * m])
                ug = next_unit()
                ptg = PP.acquire("pe", ug)
                P.wait("pe", sg, vg)
                for k in range(KC):
                    lhs = bufg[:, k * 128:(k + 1) * 128]
                    for h in range(2):
                        n0 = h * 512
                        P.op("pe", "matmul", ptg[:, n0:n0 + 512], lhs,
                             XB[k][:, n0:n0 + 512], start=(k == 0),
                             stop=(k == KC - 1),
                             inc=[("sPE", 1)] if (k == KC - 1 and h == 1) else ())
                pvg = P.n("sPE")
                WS.done(gu_sids[2 * m], "sPE", pvg)
                bufu, su, vu = WS.consume(gu_sids[2 * m + 1])
                uu = next_unit()
                ptu = PP.acquire("pe", uu)
                P.wait("pe", su, vu)
                for k in range(KC):
                    lhs = bufu[:, k * 128:(k + 1) * 128]
                    for h in range(2):
                        n0 = h * 512
                        P.op("pe", "matmul", ptu[:, n0:n0 + 512], lhs,
                             XB[k][:, n0:n0 + 512], start=(k == 0),
                             stop=(k == KC - 1),
                             inc=[("sPE", 1)] if (k == KC - 1 and h == 1) else ())
                pvu = P.n("sPE")
                WS.done(gu_sids[2 * m + 1], "sPE", pvu)
                P.wait("act", "sPE", pvg)
                P.wait("act", "sDVE", r_last.get("gsil", 0))
                P.op("act", "activation", S32[0][:, :], ptg[:, :], AF.Silu,
                     inc=[("sACT", 1)])
                av = P.n("sACT")
                PP.release(ug, "sACT", av)
                P.wait("dve", "sACT", av)
                P.wait("dve", "sPE", pvu)
                if wb is None:
                    P.op("dve", "tensor_mul", hpool[m][:, :], ptu[:, :],
                         S32[0][:, :], inc=[("sDVE", 1)])
                else:
                    P.wait("dve", "sDVE", r_last.get("tt2", 0))
                    P.op("dve", "tensor_mul", wb[1][:, :], ptu[:, :],
                         S32[0][:, :], inc=[("sDVE", 1)])
                dv = P.n("sDVE")
                r_last["gsil"] = dv
                PP.release(uu, "sDVE", dv)
                if wb is not None:
                    P.wait("dve", "sDVE", dv)
                    P.op("dve", "tensor_mul", hpool[m][:, :], wb[1][:, :],
                         wb[0][:, :], inc=[("sDVE", 1)])
                    r_last["tt2"] = P.n("sDVE")

        def down(dn_specs, hsrc, h_ready, feed=False):
            P.wait("pe", "sDVE", h_ready)
            for m, specs in enumerate(dn_specs):
                u = next_unit()
                pt = PP.acquire("pe", u)
                nsp = len(specs)
                for si, (sid, kcw, kr) in enumerate(specs):
                    buf, ws, wv = WS.consume(sid)
                    P.wait("pe", ws, wv)
                    for k in range(kcw):
                        lhs = buf[:, k * 128:(k + 1) * 128]
                        for h in range(2):
                            n0 = h * 512
                            P.op("pe", "matmul", pt[:, n0:n0 + 512], lhs,
                                 hsrc[kr + k][:, n0:n0 + 512],
                                 start=(si == 0 and k == 0),
                                 stop=(si == nsp - 1 and k == kcw - 1),
                                 inc=[("sPE", 1)] if (k == kcw - 1 and h == 1)
                                 else ())
                    WS.done(sid, "sPE", P.n("sPE"))
                pv = P.n("sPE")
                P.wait("dve", "sPE", pv)
                P.wait("dve", "sDVE", r_last.get(m, 0))
                P.op("dve", "tensor_add", R[m][:, 0:512], R[m][:, 0:512],
                     pt[:, 0:512])
                P.op("dve", "tensor_add", R[m][:, 512:1024], R[m][:, 512:1024],
                     pt[:, 512:1024], inc=[("sDVE", 1)])
                r_last[m] = P.n("sDVE")
                if feed:
                    ln_feed(m, "sDVE", r_last[m], pv)
                PP.release(u, "sDVE", r_last[m])

        hsh = B[0:16] + A[0:6]
        gateup(sh_gu, MC_DS, hsh)
        for t in range(8):
            cols = slice(t * 128, (t + 1) * 128)
            u = next_unit()
            pt = PP.acquire("pe", u)
            P.wait("pe", "sACT", LOGV)
            if t > 0:
                P.wait("pe", "sDVE", P.n("sDVE"))
            P.op("pe", "transpose", pt[:, 0:8], ROWS[0:8, cols],
                 ident[0:8, 0:8], inc=[("sPE", 1)])
            pv = P.n("sPE")
            E = SMX
            W = SMX[:, 32:48]
            P.wait("dve", "sPE", pv)
            P.wait("dve", "sACT", P.n("sACT"))
            P.op("dve", "reduce_max", W[:, 8:9], pt[:, 0:8], axis=AX.X,
                 negate=True, inc=[("sDVE", 1)])
            dv = P.n("sDVE")
            P.wait("act", "sDVE", dv)
            P.op("act", "activation", E[:, 0:8], pt[:, 0:8], AF.Exp,
                 bias=W[:, 8:9], inc=[("sACT", 1)])
            av = P.n("sACT")
            PP.release(u, "sACT", av)
            P.wait("dve", "sACT", av)
            P.op("dve", "reduce_sum", W[:, 9:10], E[:, 0:8], axis=AX.X,
                 inc=[("sDVE", 1)])
            dv = P.n("sDVE")
            P.wait("act", "sDVE", dv)
            P.op("act", "activation", W[:, 10:11], W[:, 9:10], AF.Ln,
                 inc=[("sACT", 1)])
            P.wait("act", "sACT", P.n("sACT"))
            P.op("act", "activation", W[:, 10:11], W[:, 10:11], AF.Exp,
                 scale=-1.0, inc=[("sACT", 1)])
            av = P.n("sACT")
            P.wait("dve", "sACT", av)
            P.op("dve", "tensor_single_scalar", E[:, 24:32], E[:, 0:8],
                 W[:, 10:11], ALU.mult, inc=[("sDVE", 1)])
            P.wait("dve", "sDVE", P.n("sDVE"))
            P.op("dve", "reduce_max", W[:, 11:12], E[:, 24:32], axis=AX.X,
                 inc=[("sDVE", 1)])
            P.wait("dve", "sDVE", P.n("sDVE"))
            P.op("dve", "tensor_single_scalar", E[:, 8:16], E[:, 24:32],
                 W[:, 11:12], ALU.is_ge, inc=[("sDVE", 1)])
            P.wait("dve", "sDVE", P.n("sDVE"))
            P.op("dve", "tensor_scalar", E[:, 8:16], E[:, 8:16], -1.0, 1.0,
                 ALU.mult, ALU.add, inc=[("sDVE", 1)])
            P.wait("dve", "sDVE", P.n("sDVE"))
            P.op("dve", "tensor_mul", E[:, 16:24], E[:, 24:32], E[:, 8:16],
                 inc=[("sDVE", 1)])
            P.wait("dve", "sDVE", P.n("sDVE"))
            P.op("dve", "reduce_max", W[:, 12:13], E[:, 16:24], axis=AX.X,
                 inc=[("sDVE", 1)])
            P.wait("dve", "sDVE", P.n("sDVE"))
            P.op("dve", "tensor_single_scalar", E[:, 8:16], E[:, 24:32],
                 W[:, 12:13], ALU.is_ge, inc=[("sDVE", 1)])
            P.wait("dve", "sDVE", P.n("sDVE"))
            P.op("dve", "tensor_mul", W[:, 0:8], E[:, 24:32], E[:, 8:16],
                 inc=[("sDVE", 1)])
            dv = P.n("sDVE")
            u = next_unit()
            pt = PP.acquire("pe", u)
            P.wait("pe", "sDVE", dv)
            P.op("pe", "transpose", pt[0:8, 0:128], W[:, 0:8], ident[:, :],
                 inc=[("sPE", 1)])
            pv = P.n("sPE")
            P.wait("act", "sPE", pv)
            P.op("act", "activation", ROWSB[0:8, cols], pt[0:8, 0:128], AF.Copy,
                 inc=[("sACT", 1)])
            PP.release(u, "sACT", P.n("sACT"))
        WRDY = P.n("sACT")
        down([[sh_dn[2 * m], sh_dn[2 * m + 1]] for m in range(KC)], hsh,
             P.n("sDVE"))

        for e in range(8):
            pool = A if e % 2 == 0 else B
            u = next_unit()
            pt = PP.acquire("pe", u)
            P.wait("pe", "sACT", WRDY)
            for h in range(2):
                n0 = h * 512
                P.op("pe", "matmul", pt[:, n0:n0 + 512],
                     ind8[:, e * 128:(e + 1) * 128], ROWSB[0:8, n0:n0 + 512],
                     start=True, stop=True,
                     inc=[("sPE", 1)] if h == 1 else ())
            pv = P.n("sPE")
            P.wait("act", "sPE", pv)
            P.op("act", "activation", pool[15][:, :], pt[:, :], AF.Copy,
                 inc=[("sACT", 1)])
            av = P.n("sACT")
            PP.release(u, "sACT", av)
            P.wait("dve", "sACT", av)
            gateup(ex_gu[e], MC_DE, pool, wb=(pool[15], pool[12]))
            if e == 7:
                ln_begin(B[13], B[14])
            down([[(ex_dn[e][m], MC_DE, 0)] for m in range(KC)], pool,
                 P.n("sDVE"), feed=(e == 7))

    def l1_attn():
        P.wait("dve", "sDVE", P.n("sDVE"))
        P.wait("act", "sACT", P.n("sACT"))
        P.wait("act", "sDVE", P.n("sDVE"))
        xbr = LN_XB["v"]
        rl = {}
        k_sid = WS.register(din["l1k_s"][:, :], KC * 64)
        q_sids = [WS.register(din["l1q_s"][m, :, :], 2048) for m in range(KC)]
        d_sids = [WS.register(din["l1d_s"][m, :, :], 2048) for m in range(KC)]

        VW = B[2]
        P.wait("gp", "sPE", P.n("sPE"))
        vsrc = bass.AP(din["l1v_r"], 0, [[64, 128], [128 * 64, KC], [1, 64]])
        P.op("gp", "dma_start", out=VW[:, 0:KC * 64], in_=vsrc, inc=[("sGP", 16)])
        VW_GP = P.n("sGP")
        uv = next_unit()
        ptv = PP.acquire("pe", uv)
        P.wait("pe", "sGP", VW_GP)
        for t in range(8):
            for k in range(KC):
                if t == 0 and xbr is not None:
                    P.wait("pe", "sACT", xbr[k])
                P.op("pe", "matmul", ptv[:, t * 64:(t + 1) * 64],
                     XB[k][:, t * 128:(t + 1) * 128], VW[:, k * 64:(k + 1) * 64],
                     start=(k == 0), stop=(k == KC - 1),
                     inc=[("sPE", 1)] if (t == 7 and k == KC - 1) else ())
        pv = P.n("sPE")
        P.wait("act", "sPE", pv)
        P.op("act", "activation", B[1][:, 0:512], ptv[:, 0:512], AF.Copy,
             inc=[("sACT", 1)])
        av = P.n("sACT")
        PP.release(uv, "sACT", av)
        P.wait("sp", "sACT", av)
        P.op("sp", "dma_start", out=l1kv_loc[1].rearrange("p f -> (p f)").rearrange(
            "(p f) -> p f", p=128), in_=B[1][:, 0:512], inc=[("sH", 16)])

        uk = next_unit()
        ptk = PP.acquire("pe", uk)
        bufk, ws, wv = WS.consume(k_sid)
        P.wait("pe", ws, wv)
        for k in range(KC):
            lhs = bufk[:, k * 64:(k + 1) * 64]
            for h in range(2):
                n0 = h * 512
                P.op("pe", "matmul", ptk[0:64, n0:n0 + 512], lhs,
                     XB[k][:, n0:n0 + 512], start=(k == 0), stop=(k == KC - 1),
                     inc=[("sPE", 1)] if (k == KC - 1 and h == 1) else ())
        pv = P.n("sPE")
        WS.done(k_sid, "sPE", pv)

        def rope(rows, src_ps, u_src, dst):
            P.wait("act", "sPE", P.n("sPE"))
            P.wait("act", "sDVE", rl.get("ropeS1", 0))
            P.op("act", "activation", S32[0][rows, :], src_ps[rows, :], AF.Copy,
                 inc=[("sACT", 1)])
            av = P.n("sACT")
            PP.release(u_src, "sACT", av)
            up = next_unit()
            ptp = PP.acquire("pe", up)
            P.wait("pe", "sACT", av)
            n = rows.stop - rows.start
            for h in range(2):
                n0 = h * 512
                P.op("pe", "matmul", ptp[rows, n0:n0 + 512],
                     perm[rows, 0:n], S32[0][rows, n0:n0 + 512],
                     start=True, stop=True,
                     inc=[("sPE", 1)] if h == 1 else ())
            pvp = P.n("sPE")
            P.wait("dve", "sACT", av)
            P.wait("dve", "sDVE", rl.get("ropeS1", 0))
            P.op("dve", "tensor_mul", S32[1][rows, :], S32[0][rows, :],
                 cos_t[rows, :], inc=[("sDVE", 1)])
            P.wait("dve", "sDVE", P.n("sDVE"))
            P.wait("dve", "sPE", pvp)
            P.op("dve", "tensor_mul", S32[0][rows, :], ptp[rows, :],
                 sin_t[rows, :], inc=[("sDVE", 1)])
            rl["ropeS0"] = P.n("sDVE")
            PP.release(up, "sDVE", rl["ropeS0"])
            P.wait("dve", "sDVE", rl["ropeS0"])
            P.op("dve", "tensor_add", dst, S32[1][rows, :], S32[0][rows, :],
                 inc=[("sDVE", 1)])
            rl["ropeS1"] = P.n("sDVE")

        rope(slice(0, 64), ptk, uk, B[0][0:64, :])
        P.wait("sp", "sDVE", rl["ropeS1"])
        P.op("sp", "dma_start", out=l1kv_loc[0].rearrange("p f -> (p f)").rearrange(
            "(p f) -> p f", p=64), in_=B[0][0:64, :], inc=[("sH", 16)])
        GP_KV = P.n("sH")

        groups = [[2 * i, 2 * i + 1] for i in range(max(1, ncores // 2))]
        P.wait("gp", "sH", GP_KV)
        P.op("gp", "collective_compute", "AllGather", ALU.bypass,
             replica_groups=groups, ins=[l1kv_loc.ap().opt()],
             outs=[l1kv_all.ap().opt()], inc=[("sCC", 1)])
        CCV = P.n("sCC")

        KH = [B[4], B[5]]
        VT = [B[6], B[7]]
        PE0 = P.n("sPE")
        P.wait("sp", "sCC", CCV)
        P.wait("sp", "sPE", PE0)
        P.wait("gp", "sPE", PE0)
        for g in range(KC):
            rk = g % 2
            lt = g // 2
            src = l1kv_all[rk, 0].rearrange("p f -> (p f)").rearrange(
                "(p f) -> p f", p=64)[:, lt * 128:(lt + 1) * 128]
            dst = KH[g // 8]
            c0 = (g % 8) * 128
            P.op("sp", "dma_start", out=dst[0:64, c0:c0 + 128], in_=src,
                 inc=[("sH", 16)])
            P.op("sp", "dma_start", out=dst[64:128, c0:c0 + 128], in_=src,
                 inc=[("sH", 16)])
            srcv = l1kv_all[rk, 1].rearrange("p f -> (p f)").rearrange(
                "(p f) -> p f", p=128)[:, lt * 64:(lt + 1) * 64]
            dv_ = VT[g // 8]
            cv = (g % 8) * 65
            P.op("sp", "dma_start", out=dv_[:, cv:cv + 64], in_=srcv,
                 inc=[("sH", 16)])
            P.op("gp", "memset", VT[g // 8][:, cv + 64:cv + 65], 1.0,
                 inc=[("sGPC", 1)])
        KV_GP = P.n("sH")
        KV_GPC = P.n("sGPC")

        for m in range(KC):
            buf, ws, wv = WS.consume(q_sids[m])
            u = next_unit()
            pt = PP.acquire("pe", u)
            P.wait("pe", ws, wv)
            for k in range(KC):
                lhs = buf[:, k * 128:(k + 1) * 128]
                for h in range(2):
                    n0 = h * 512
                    P.op("pe", "matmul", pt[:, n0:n0 + 512], lhs,
                         XB[k][:, n0:n0 + 512], start=(k == 0),
                         stop=(k == KC - 1),
                         inc=[("sPE", 1)] if (k == KC - 1 and h == 1) else ())
            pvq = P.n("sPE")
            WS.done(q_sids[m], "sPE", pvq)
            rope(slice(0, 128), pt, u, A[m][:, :])
        Q_DVE = P.n("sDVE")

        P.wait("pe", "sH", KV_GP)
        P.wait("pe", "sGPC", KV_GPC)
        P.wait("pe", "sDVE", Q_DVE)
        P.wait("act", "sACT", P.n("sACT"))
        P.wait("act", "sDVE", P.n("sDVE"))
        UC0, UC1 = 2, 3
        NPBUF = 6
        PSB = [B[8 + i] for i in range(NPBUF)]
        exp_wait = {i: 0 for i in range(NPBUF)}
        prev_tail = [None]
        for hp in range(L1_NH // 2):
            h0 = 2 * hp
            qch = A[hp]
            pair = {"acq": False}

            def emit_ctx_pair(g, segs, hp=hp, qch=qch, pair=pair):
                if not pair["acq"]:
                    PP.acquire("pe", UC0)
                    PP.acquire("pe", UC1)
                    pair["acq"] = True
                vt65 = VT[g // 8][:, (g % 8) * 65:(g % 8) * 65 + 65]
                for hh in range(2):
                    pb = (2 * g + hh) % NPBUF
                    psb = PSB[pb]
                    P.wait("pe", "sDVE", rl[("mask", pb)])
                    for i, (s0, s1) in enumerate(segs):
                        lastg = (g == (7 if s0 < 512 else KC - 1))
                        if hh == 0:
                            P.op("pe", "matmul", psum[UC0][0:65, s0:s1], vt65,
                                 psb[:, s0:s1], start=(g == 0), stop=lastg,
                                 inc=[("sPE", 1)] if i == len(segs) - 1 else ())
                        else:
                            P.op("pe", "matmul", psum[UC1][64:128, s0:s1],
                                 vt65[:, 0:64], psb[:, s0:s1],
                                 start=(g == 0), stop=lastg)
                            P.op("pe", "matmul", psum[UC1][32:33, s0:s1],
                                 vt65[:, 64:65], psb[:, s0:s1],
                                 start=(g == 0), stop=lastg,
                                 inc=[("sPE", 1)] if i == len(segs) - 1 else ())
                    exp_wait[pb] = P.n("sPE")

            pend = []
            for g in range(KC):
                qs = 128 * (g // 2)
                segs = []
                s0 = qs
                while s0 < 1024:
                    s1 = min(1024, (s0 // 512 + 1) * 512)
                    segs.append((s0, s1))
                    s0 = s1
                for hh in range(2):
                    base = hh * 64
                    rows = slice(base, base + 64)
                    klhs = KH[g // 8][rows, (g % 8) * 128:(g % 8) * 128 + 128]
                    us = hh
                    pts = PP.acquire("pe", us)
                    for i, (s0, s1) in enumerate(segs):
                        P.op("pe", "matmul", pts[:, s0:s1], klhs,
                             qch[rows, s0:s1], start=True, stop=True,
                             inc=[("sPE", 1)] if i == len(segs) - 1 else ())
                    pvs = P.n("sPE")
                    pb = (2 * g + hh) % NPBUF
                    psb = PSB[pb]
                    P.wait("act", "sPE", max(pvs, exp_wait[pb]))
                    P.wait("act", "sDVE", rl.get(("mask", pb), 0))
                    P.op("act", "activation", psb[:, qs:1024], pts[:, qs:1024],
                         AF.Exp, scale=0.125, inc=[("sACT", 1)])
                    ave = P.n("sACT")
                    PP.release(us, "sACT", ave)
                    P.wait("dve", "sACT", ave)
                    P.wait("dve", "sPE", exp_wait[pb])
                    P.op("dve", "tensor_mul", psb[:, qs:qs + 128],
                         psb[:, qs:qs + 128], dmask[:, g * 128:(g + 1) * 128],
                         inc=[("sDVE", 1)])
                    rl[("mask", pb)] = P.n("sDVE")
                if g == 2 and prev_tail[0] is not None:
                    prev_tail[0]()
                pend.append((g, segs))
                if len(pend) > 2:
                    emit_ctx_pair(*pend.pop(0))
            for e_ in pend:
                emit_ctx_pair(*e_)
            ctx_done_val = P.n("sPE")

            ptc0, ptc1 = psum[UC0], psum[UC1]
            P.wait("act", "sPE", ctx_done_val)
            P.wait("act", "sDVE", rl.get("rden", 0))
            for dl, ptd in ((64, ptc0), (32, ptc1)):
                P.op("act", "activation", S32[0][dl:dl + 1, :],
                     ptd[dl:dl + 1, :], AF.Ln, inc=[("sACT", 1)])
                P.wait("act", "sACT", P.n("sACT"))
                P.op("act", "activation", ROWSB[dl:dl + 1, :],
                     S32[0][dl:dl + 1, :], AF.Exp, scale=-1.0,
                     inc=[("sACT", 1)])
            avd = P.n("sACT")

            def tail(hp=hp, qch=qch, ctx_done_val=ctx_done_val, avd=avd,
                     ptc0=ptc0, ptc1=ptc1):
                P.wait("pe", "sACT", avd)
                for hh in range(2):
                    n0 = hh * 512
                    P.op("pe", "matmul", ptc1[0:64, n0:n0 + 512],
                         onesb[64:65, 0:64], ROWSB[64:65, n0:n0 + 512],
                         start=True, stop=True)
                    P.op("pe", "matmul", ptc0[64:128, n0:n0 + 512],
                         onesb[32:33, 0:64], ROWSB[32:33, n0:n0 + 512],
                         start=True, stop=True,
                         inc=[("sPE", 1)] if hh == 1 else ())
                pvr = P.n("sPE")
                P.wait("dve", "sPE", pvr)
                P.wait("dve", "sDVE", rl.get("rdevac", 0))
                P.op("dve", "tensor_copy", S32[1][0:64, :], ptc1[0:64, :],
                     inc=[("sDVE", 1)])
                P.op("dve", "tensor_copy", S32[1][64:128, :], ptc0[64:128, :],
                     inc=[("sDVE", 1)])
                rl["rdevac"] = P.n("sDVE")
                P.wait("dve", "sDVE", rl["rdevac"])
                P.op("dve", "tensor_mul", qch[0:64, :], ptc0[0:64, :],
                     S32[1][0:64, :], inc=[("sDVE", 1)])
                P.wait("dve", "sDVE", P.n("sDVE"))
                P.op("dve", "tensor_mul", qch[64:128, :], ptc1[64:128, :],
                     S32[1][64:128, :], inc=[("sDVE", 1)])
                rl["rden"] = P.n("sDVE")
                PP.release(UC0, "sDVE", rl["rden"])
                PP.release(UC1, "sDVE", rl["rden"])

            prev_tail[0] = tail
        prev_tail[0]()

        CTX_DVE = P.n("sDVE")
        P.wait("pe", "sDVE", CTX_DVE)

        def evd(m, pt, pv):
            P.wait("dve", "sPE", pv)
            P.wait("dve", "sDVE", rl.get(("r", m), 0))
            P.op("dve", "tensor_add", R[m][:, 0:512], R[m][:, 0:512], pt[:, 0:512])
            P.op("dve", "tensor_add", R[m][:, 512:1024], R[m][:, 512:1024],
                 pt[:, 512:1024], inc=[("sDVE", 1)])
            rl[("r", m)] = P.n("sDVE")
            ln_feed(m, "sDVE", rl[("r", m)], pv)
            return ("sDVE", rl[("r", m)])

        ln_begin(B[12], B[13])
        proj(A, d_sids, KC, evd)
        ln_finish()

    def l1_ffn():
        P.wait("dve", "sDVE", P.n("sDVE"))
        P.wait("act", "sACT", P.n("sACT"))
        P.wait("act", "sDVE", P.n("sDVE"))
        xbr = LN_XB["v"]
        rl = {}
        P.op("sp", "dma_start", out=l1sc[:, 0:MC_DI], in_=din["l1gsc"][:, :],
             inc=[("sSP", 16)])
        P.op("sp", "dma_start", out=l1sc[:, MC_DI:2 * MC_DI],
             in_=din["l1usc"][:, :], inc=[("sSP", 16)])
        P.op("sp", "dma_start", out=l1sc[:, 2 * MC_DI:2 * MC_DI + KC],
             in_=din["l1dsc"][:, :], inc=[("sSP", 16)])
        SCV = P.n("sSP")
        P.wait("act", "sSP", SCV)
        P.wait("dve", "sSP", SCV)

        f8idx = [0]
        def reg8(src_ap, ncols):
            s = WS.register(src_ap, ncols, fp8=(F8, ("sF80", "sF81"), f8idx[0]))
            f8idx[0] += 1
            return s

        specs = []
        for kg in range(3):
            gu = []
            for mm_ in range(16):
                m = kg * 16 + mm_
                gu.append((m, reg8(din["l1g_s"][m, :, :], 2048),
                           reg8(din["l1u_s"][m, :, :], 2048)))
            dn = [reg8(din["l1dn_s"][m, :, kg * 2048:(kg + 1) * 2048], 2048)
                  for m in range(KC)]
            specs.append((gu, dn))

        for kg, (gu, dn) in enumerate(specs):
            for mm_, (m, sg, su) in enumerate(gu):
                bufg, ws, wv = WS.consume(sg)
                ug = next_unit()
                ptg = PP.acquire("pe", ug)
                P.wait("pe", ws, wv)
                for k in range(KC):
                    if kg == 0 and mm_ == 0 and xbr is not None:
                        P.wait("pe", "sACT", xbr[k])
                    lhs = bufg[:, k * 128:(k + 1) * 128]
                    for h in range(2):
                        n0 = h * 512
                        P.op("pe", "matmul", ptg[:, n0:n0 + 512], lhs,
                             XB[k][:, n0:n0 + 512], start=(k == 0),
                             stop=(k == KC - 1),
                             inc=[("sPE", 1)] if (k == KC - 1 and h == 1) else ())
                pvg = P.n("sPE")
                WS.done(sg, "sPE", pvg)
                bufu, ws, wv = WS.consume(su)
                uu = next_unit()
                ptu = PP.acquire("pe", uu)
                P.wait("pe", ws, wv)
                for k in range(KC):
                    lhs = bufu[:, k * 128:(k + 1) * 128]
                    for h in range(2):
                        n0 = h * 512
                        P.op("pe", "matmul", ptu[:, n0:n0 + 512], lhs,
                             XB[k][:, n0:n0 + 512], start=(k == 0),
                             stop=(k == KC - 1),
                             inc=[("sPE", 1)] if (k == KC - 1 and h == 1) else ())
                pvu = P.n("sPE")
                WS.done(su, "sPE", pvu)
                P.wait("act", "sPE", pvg)
                P.wait("act", "sDVE", rl.get("gsil", 0))
                P.op("act", "activation", S32[0][:, :], ptg[:, :], AF.Silu,
                     scale=l1sc[:, m:m + 1], inc=[("sACT", 1)])
                av = P.n("sACT")
                PP.release(ug, "sACT", av)
                P.wait("dve", "sACT", av)
                P.wait("dve", "sPE", pvu)
                P.op("dve", "scalar_tensor_tensor", B[mm_][:, :], ptu[:, :],
                     l1sc[:, MC_DI + m:MC_DI + m + 1], S32[0][:, :],
                     ALU.mult, ALU.mult, inc=[("sDVE", 1)])
                rl["gsil"] = P.n("sDVE")
                PP.release(uu, "sDVE", rl["gsil"])
            if kg == 2:
                ln_begin(A[0], A[1])
            H_DVE = P.n("sDVE")
            P.wait("pe", "sDVE", H_DVE)
            for m in range(KC):
                buf, ws, wv = WS.consume(dn[m])
                u = next_unit()
                pt = PP.acquire("pe", u)
                P.wait("pe", ws, wv)
                for k in range(16):
                    lhs = buf[:, k * 128:(k + 1) * 128]
                    for h in range(2):
                        n0 = h * 512
                        P.op("pe", "matmul", pt[:, n0:n0 + 512], lhs,
                             B[k][:, n0:n0 + 512], start=(k == 0),
                             stop=(k == 15),
                             inc=[("sPE", 1)] if (k == 15 and h == 1) else ())
                pv = P.n("sPE")
                WS.done(dn[m], "sPE", pv)
                P.wait("dve", "sPE", pv)
                P.wait("dve", "sDVE", rl.get(("r", m), 0))
                dcol = l1sc[:, 2 * MC_DI + m:2 * MC_DI + m + 1]
                P.op("dve", "scalar_tensor_tensor", R[m][:, 0:512],
                     pt[:, 0:512], dcol, R[m][:, 0:512], ALU.mult, ALU.add)
                P.op("dve", "scalar_tensor_tensor", R[m][:, 512:1024],
                     pt[:, 512:1024], dcol, R[m][:, 512:1024], ALU.mult,
                     ALU.add, inc=[("sDVE", 1)])
                rl[("r", m)] = P.n("sDVE")
                if kg == 2:
                    ln_feed(m, "sDVE", rl[("r", m)], pv)
                PP.release(u, "sDVE", rl[("r", m)])

    l0_attn()
    if stop_after != "l0attn":
        l0_ffn()
        ln_finish()
    if stop_after not in ("l0attn", "l0ffn"):
        l1_attn()
    if stop_after is None:
        l1_ffn()
        ln_finish(final=True)

    if P.n("sOUT") == 0:
        P.wait("sp", "sACT", P.n("sACT"))
        P.wait("sp", "sDVE", P.n("sDVE"))
        ydst = bass.AP(dy, 0, [[T, 128], [128 * T, KC], [1, T]])
        P.op("sp", "dma_start", out=ydst, in_=RALL[:, :], inc=[("sOUT", 16)])
    P.wait("sp", "sOUT", P.n("sOUT"))

    with nc.Block() as block:
        @block.gpsimd
        def _(g):
            P.emit("gp", g)

        @block.tensor
        def _(t):
            P.emit("pe", t)

        @block.vector
        def _(v):
            P.emit("dve", v)

        @block.scalar
        def _(s):
            P.emit("act", s)

        @block.sync
        def _(s):
            P.emit("sp", s)

    return nc


def kernel(**inputs):
    in_maps = prepare_inputs(inputs, NCORES)
    nc = build_nc(NCORES)
    res = run_bass_kernel_spmd(nc, in_maps, core_ids=list(range(NCORES)))
    return gather_output(res.results, NCORES)
